# version 09
# baseline (speedup 1.0000x reference)
import numpy as np
from contextlib import ExitStack

import concourse.bass as bass
import concourse.mybir as mybir
from concourse.bass_utils import run_bass_kernel_spmd
from concourse.alu_op_type import AluOpType as OP

F32 = mybir.dt.float32
BF16 = mybir.dt.bfloat16
AF = mybir.ActivationFunctionType
AX = mybir.AxisListType.X

D = 2048
NF = 16
TT = 512
NTC = 4
NE = 16
NIC = 11
NSI = 22
EPS = 1e-5
NEGBIG = -1.0e30



def _cm(W):
    O, I = W.shape
    M, F = O // 128, I // 128
    W4 = W.reshape(M, 128, F, 128)
    return np.ascontiguousarray(W4.transpose(0, 3, 2, 1).reshape(M, 128, F * 128))


def _pairs(cm_grid):
    G = cm_grid.shape[0] // 2
    order = []
    for j in range(G):
        order += [j, G + j]
    return np.ascontiguousarray(cm_grid[order].reshape(G, 2, 128, cm_grid.shape[2]))


def _colT(v):
    return np.ascontiguousarray(v.reshape(16, 128).T.astype(np.float32))


def _stage_shared_inputs(inputs):
    f32 = np.float32
    g = {}
    g["wk_cm"] = _cm(np.asarray(inputs["L0_attn_Wk"], f32))
    g["wv_cm"] = _cm(np.asarray(inputs["L0_attn_Wv"], f32))
    g["wr_cm"] = _cm(np.asarray(inputs["L0_attn_Wr"], f32))
    g["wo_cm"] = _cm(np.asarray(inputs["L0_attn_Wo"], f32))
    eup = np.asarray(inputs["L0_ffn_expert_up_w"], f32)
    edn = np.asarray(inputs["L0_ffn_expert_down_w"], f32)
    g["eup_cm"] = np.ascontiguousarray(
        np.stack([_pairs(_cm(eup[e])) for e in range(NE)]))
    g["edn_cm"] = np.ascontiguousarray(
        np.stack([_cm(edn[e]) for e in range(NE)]))
    g["sup_cm"] = _pairs(_cm(np.asarray(inputs["L0_ffn_shared_up_w"], f32)))
    g["sdn_cm"] = _cm(np.asarray(inputs["L0_ffn_shared_down_w"], f32))
    g["gwt"] = np.ascontiguousarray(np.asarray(inputs["L0_ffn_gate_w"], f32).T)
    g["tdT"] = _colT(np.asarray(inputs["L0_attn_time_decay"], f32).reshape(-1))
    g["tfT"] = _colT(np.asarray(inputs["L0_attn_time_first"], f32).reshape(-1))
    g["tmk"] = _colT(np.asarray(inputs["L0_attn_time_mix_k"], f32).reshape(-1))
    g["tmv"] = _colT(np.asarray(inputs["L0_attn_time_mix_v"], f32).reshape(-1))
    g["tmr"] = _colT(np.asarray(inputs["L0_attn_time_mix_r"], f32).reshape(-1))
    g["ident"] = np.eye(128, dtype=f32)
    return g



class Sem:
    __slots__ = ("h", "cnt", "name")

    def __init__(self, h, name):
        self.h = h
        self.cnt = 0
        self.name = name


class Prog:
    ENGS = ("sync", "scalar", "vector", "tensor", "gpsimd")

    def __init__(self):
        self.st = {e: [] for e in self.ENGS}
        self.wr = {e: set() for e in self.ENGS}
        self.rd = {e: set() for e in self.ENGS}

    def wait(self, eng, sem, val):
        if val is None or val <= 0:
            return
        self.st[eng].append(lambda E, h=sem.h, v=val: E.wait_ge(h, v))

    def wait2(self, eng, sv):
        if sv is None:
            return
        self.wait(eng, sv[0], sv[1])

    def raw(self, eng, fn):
        self.st[eng].append(fn)

    def op(self, eng, fn, reads=(), writes=(), inc=None, inc_n=1):
        w = self.wr[eng]
        r = self.rd[eng]
        if eng in ("vector", "scalar", "gpsimd"):
            if (any(x in w for x in reads) or any(x in w for x in writes)
                    or any(x in r for x in writes)):
                self.st[eng].append(lambda E: E.drain())
                w.clear()
                r.clear()
        if inc is not None:
            inc.cnt += inc_n
            self.st[eng].append(
                lambda E, f=fn, h=inc.h, n=inc_n: f(E).then_inc(h, n))
            tgt = inc.cnt
        else:
            self.st[eng].append(fn)
            tgt = None
        w.update(writes)
        r.update(reads)
        return tgt

    def mile(self, eng, sem, n=1):
        assert eng != "tensor"
        self.wr[eng].clear()
        self.rd[eng].clear()
        sem.cnt += n
        self.st[eng].append(lambda E, h=sem.h, n2=n: E.drain().then_inc(h, n2))
        return sem.cnt


class DmaStream:

    def __init__(self, sems):
        self.sems = sems
        self.i = 0

    def issue(self, p, eng, fn, serialize=False):
        s = self.sems[self.i % len(self.sems)]
        self.i += 1
        if serialize and s.cnt > 0:
            p.wait(eng, s, s.cnt)
        tgt = p.op(eng, fn, inc=s, inc_n=16)
        return (s, tgt)


def _mk_runner(ops):
    def _run(E):
        for f in ops:
            f(E)
    return _run



def build_program(t_loc, replica_groups, cap):
    ntile = t_loc // TT
    ngrp = cap // 128
    nts_list = []
    _r = cap
    while _r > 0:
        nts_list.append(min(512, _r))
        _r -= min(512, _r)
    nc = bass.Bass("TRN2", target_bir_lowering=False)

    x_d = nc.dram_tensor("x", [t_loc, D], F32, kind="ExternalInput")
    xprev_d = nc.dram_tensor("xprev", [1, D], F32, kind="ExternalInput")
    isodd_d = nc.dram_tensor("isodd", [128, 1], F32, kind="ExternalInput")
    tdT_d = nc.dram_tensor("tdT", [128, 16], F32, kind="ExternalInput")
    tfT_d = nc.dram_tensor("tfT", [128, 16], F32, kind="ExternalInput")
    tmk_d = nc.dram_tensor("tmk", [128, 16], F32, kind="ExternalInput")
    tmv_d = nc.dram_tensor("tmv", [128, 16], F32, kind="ExternalInput")
    tmr_d = nc.dram_tensor("tmr", [128, 16], F32, kind="ExternalInput")
    ident_d = nc.dram_tensor("ident", [128, 128], F32, kind="ExternalInput")
    gwt_d = nc.dram_tensor("gwt", [D, 16], F32, kind="ExternalInput")
    tids_d = nc.dram_tensor("tids", [128, t_loc // 128], mybir.dt.uint32, kind="ExternalInput")
    idxinit_d = nc.dram_tensor("idxinit", [NE, cap, 1], mybir.dt.uint32, kind="ExternalInput")
    ecap_d = nc.dram_tensor("ecap", [16, 1], F32, kind="ExternalInput")
    wk_d = nc.dram_tensor("wk_cm", [16, 128, 2048], F32, kind="ExternalInput")
    wv_d = nc.dram_tensor("wv_cm", [16, 128, 2048], F32, kind="ExternalInput")
    wr_d = nc.dram_tensor("wr_cm", [16, 128, 2048], F32, kind="ExternalInput")
    wo_d = nc.dram_tensor("wo_cm", [16, 128, 2048], F32, kind="ExternalInput")
    eup_d = nc.dram_tensor("eup_cm", [NE, NIC, 2, 128, 2048], F32, kind="ExternalInput")
    edn_d = nc.dram_tensor("edn_cm", [NE, 16, 128, 1408], F32, kind="ExternalInput")
    sup_d = nc.dram_tensor("sup_cm", [NSI, 2, 128, 2048], F32, kind="ExternalInput")
    sdn_d = nc.dram_tensor("sdn_cm", [16, 128, 2816], F32, kind="ExternalInput")
    y_d = nc.dram_tensor("y", [t_loc, D], F32, kind="ExternalOutput")

    wk_b = nc.dram_tensor("wk_b", [16, 128, 2048], BF16)
    wv_b = nc.dram_tensor("wv_b", [16, 128, 2048], BF16)
    wr_b = nc.dram_tensor("wr_b", [16, 128, 2048], BF16)
    wo_b = nc.dram_tensor("wo_b", [16, 128, 2048], BF16)
    eup_b = nc.dram_tensor("eup_b", [NE, NIC, 2, 128, 2048], BF16)
    edn_b = nc.dram_tensor("edn_b", [NE, 16, 128, 1408], BF16)
    sup_b = nc.dram_tensor("sup_b", [NSI, 2, 128, 2048], BF16)
    sdn_b = nc.dram_tensor("sdn_b", [16, 128, 2816], BF16)
    kv_scr = nc.dram_tensor("kv_scr", [D, t_loc], BF16)
    r_scr = nc.dram_tensor("r_scr", [D, t_loc], BF16)
    xn_scr = nc.dram_tensor("xn_scr", [t_loc, D], BF16)
    s_scr = nc.dram_tensor("s_scr", [t_loc, D], BF16)
    facc_scr = nc.dram_tensor("facc_scr", [t_loc, D], F32)
    idx_scr = nc.dram_tensor("idx_scr", [NE, cap, 1], mybir.dt.uint32)
    cwc_scr = nc.dram_tensor("cwc_scr", [NE, cap, 1], F32)
    cx_scr = nc.dram_tensor("cx_scr", [NE, cap, D], BF16)
    cc_in = nc.dram_tensor("cc_in", [128, 16], F32)
    cc_out = nc.dram_tensor("cc_out", [128 * 2, 16], F32)

    with ExitStack() as ctx:
        sb = lambda nm, sh, dt: ctx.enter_context(nc.sbuf_tensor(nm, sh, dt))
        mkpsum = lambda nm: ctx.enter_context(nc.psum_tensor(nm, [128, 512], F32))

        xtm = sb("xtm", [128, NTC * D], F32)
        bigB = sb("bigB", [128, NF * TT], F32)
        bigC = sb("bigC", [128, NF * TT], F32)
        bigCb = bigC.bitcast(BF16)
        mixA = sb("mixA", [128, NF * TT], BF16)
        mixB = sb("mixB", [128, NF * TT], BF16)
        abf = sb("abf", [128, 2 * NIC * TT], BF16)
        wb = [sb("wb0", [128, 4096], BF16), sb("wb1", [128, 4096], BF16)]
        xntma = sb("xntma", [128, D], F32)
        xntmb = xntma.bitcast(BF16)[:, 0:D]
        st = sb("st", [128, TT + 1], F32)
        sgt = sb("sgt", [128, 2 * 512], F32)
        usc = sb("usc", [128, 512], F32)
        bcs = sb("bcs", [128, 2 * 512], F32)
        gsil = sb("gsil", [128, 2 * 512], F32)
        kbuf = gsil
        mbf = sgt.bitcast(BF16)[:, 0:1024]
        rows = sb("rows", [128, 16], F32)
        gat = sb("gat", [128, 128], F32)
        cwt = sb("cwt", [128, 2 * NTC * 16], F32)
        lgb = sb("lgb", [128, 2 * NTC * 16], F32)
        cw_fm = sb("cw_fm", [16, TT], F32)
        gfm = sb("gfm", [16, 3 * TT], F32)
        ebase = sb("ebase", [16, 2], F32)
        onecol = sb("onecol", [128, 1], F32)
        onesT = sb("onesT", [33, 128], F32)
        offsu = sb("offsu", [128, 256], mybir.dt.uint32)
        idxb = sb("idxb", [128, 2 * ngrp], mybir.dt.uint32)
        cc = sb("cc", [128, 160], F32)
        ident = sb("identt", [128, 128], F32)
        identb = sb("identtb", [128, 128], BF16)
        gw_sb = sb("gw_sb", [128, NF * 16], F32)
        tids_sb = sb("tids_sb", [128, t_loc // 128], mybir.dt.uint32)
        sg_in = sb("sg_in", [128, 16], F32)

        xprev_sb = bigB[0:1, 0:D]
        xnprev = bigB[0:1, D:2 * D]
        cwc_sb = [bigC[0:1, 0:cap], bigC[32:33, 0:cap]]
        ps = [mkpsum("ps%d" % i) for i in range(8)]

        mksem = lambda nm: Sem(ctx.enter_context(nc.semaphore(nm)), nm)
        cS = mksem("cS")
        tS = mksem("tS")
        vS = mksem("vS")
        aS = mksem("aS")
        SETUP = DmaStream([mksem("dSet")])
        WGT = DmaStream([mksem("dW%d" % k) for k in range(4)])
        XL = DmaStream([mksem("dXL")])
        KVL = DmaStream([mksem("dKVL")])
        RL = DmaStream([mksem("dRL")])
        CWL = DmaStream([mksem("dCWL%d" % k) for k in range(2)])
        CWST = DmaStream([mksem("dCWS")])
        KVST = DmaStream([mksem("dKS%d" % k) for k in range(2)])
        RST = DmaStream([mksem("dRS%d" % k) for k in range(2)])
        YST = DmaStream([mksem("dYS")])
        CAST = DmaStream([mksem("dCast")])
        GPD = DmaStream([mksem("dGP")])
        SCT = DmaStream([mksem("dSCT")])
        GTH = DmaStream([mksem("dGT%d" % k) for k in range(2)])
        CXS = DmaStream([mksem("dCXS%d" % k) for k in range(2)])
        SCA = DmaStream([mksem("dSCA")])
        TRLp = [DmaStream([mksem("dTRL%d_%d" % (pp, k)) for k in range(4)])
                for pp in range(2)]
        FST = DmaStream([mksem("dFST")])
        SST = DmaStream([mksem("dSST")])
        XNST = DmaStream([mksem("dXNS")])

        WC, UC, CK, CV, CR, SC, PC = 0, 16, 32, 48, 64, 80, 96
        IO = 112
        ET = 128
        R_SUM, R_SSQ, R_MEAN, R_MSQ, R_VAR, R_SQV, R_RSTD = 0, 1, 2, 3, 4, 5, 6
        G_LG, G_ES, G_SCO, G_MASK, G_CUR = 0, 16, 32, 48, 64
        G_GMAX, G_M4, G_GSEL, G_MK1 = 80, 84, 88, 92
        G_SMAX, G_NM, G_SSUM, G_RS, G_M1, G_M2, G_G2A, G_SW, G_RSW = (
            96, 97, 98, 99, 100, 101, 102, 104, 108)

        p = Prog()
        rec = {}
        bank_free = {}

        bndreg = nc.gpsimd.alloc_register("bndreg")

        def emit_casts(cast_list):
            for nm, csrc, cdst in cast_list:
                if len(csrc.shape) == 3:
                    s2 = csrc.rearrange("a b c -> (a b) c")
                    d2 = cdst.rearrange("a b c -> (a b) c")
                else:
                    s2 = csrc.rearrange("a b c d -> (a b c) d")
                    d2 = cdst.rearrange("a b c d -> (a b c) d")
                rec["cast_" + nm] = CAST.issue(
                    p, "gpsimd", lambda E, a=d2, b=s2: E.dma_start(a[:, :], b[:, :]),
                    serialize=True)

        rec["idxinit"] = CAST.issue(
            p, "gpsimd", lambda E: E.dma_start(
                idx_scr.rearrange("e s o -> (e s) o")[:, :],
                idxinit_d.rearrange("e s o -> (e s) o")[:, :]))
        p.raw("gpsimd", lambda E: E.reg_mov(bndreg, NE * cap - 1))
        emit_casts([("wk", wk_d, wk_b), ("wv", wv_d, wv_b), ("wr", wr_d, wr_b),
                    ("sup", sup_d, sup_b), ("sdn", sdn_d, sdn_b), ("wo", wo_d, wo_b)])

        def sdma(fn):
            return SETUP.issue(p, "sync", fn)

        sdma(lambda E: E.dma_start(cc[:, WC:WC + 16], tdT_d[:, :]))
        sdma(lambda E: E.dma_start(cc[:, UC:UC + 16], tfT_d[:, :]))
        sdma(lambda E: E.dma_start(cc[:, CK:CK + 16], tmk_d[:, :]))
        sdma(lambda E: E.dma_start(cc[:, CV:CV + 16], tmv_d[:, :]))
        sdma(lambda E: E.dma_start(cc[:, CR:CR + 16], tmr_d[:, :]))
        sdma(lambda E: E.dma_start(cc[:, IO:IO + 1], isodd_d[:, :]))
        sdma(lambda E: E.dma_start(ident[:, :], ident_d[:, :]))
        sdma(lambda E: E.dma_start(tids_sb[:, :], tids_d[:, :]))
        sdma(lambda E: E.dma_start(ebase[0:16, 1:2], ecap_d[:, :]))
        sdma(lambda E: E.dma_start(
            gw_sb[:, :].rearrange("p (f e) -> p f e", f=NF),
            gwt_d.rearrange("(f p) e -> p f e", p=128)[:, :, :]))
        sdma(lambda E: E.dma_start(xprev_sb, xprev_d[:, :]))
        rec["setup_dma"] = (SETUP.sems[0], SETUP.sems[0].cnt)

        p.wait2("scalar", rec["setup_dma"])
        p.op("scalar", lambda E: E.activation(cc[:, ET:ET + 16], cc[:, WC:WC + 16], AF.Exp),
             reads=("cc",), writes=("cc",))
        rec["wc1"] = p.mile("scalar", aS)
        p.wait("vector", aS, rec["wc1"])
        p.op("vector", lambda E: E.tensor_scalar(
            cc[:, ET:ET + 16], cc[:, ET:ET + 16], -1.0, None, OP.mult),
             reads=("cc",), writes=("cc",))
        rec["wc2"] = p.mile("vector", vS)
        p.wait("scalar", vS, rec["wc2"])
        p.op("scalar", lambda E: E.activation(cc[:, WC:WC + 16], cc[:, ET:ET + 16], AF.Exp),
             reads=("cc",), writes=("cc",))
        rec["wcols"] = p.mile("scalar", aS)

        p.wait2("vector", rec["setup_dma"])
        for off in (CK, CV, CR):
            p.op("vector", lambda E, o=off: E.tensor_scalar(
                cc[:, o:o + 16], cc[:, o:o + 16], -1.0, 1.0, OP.mult, OP.add),
                 reads=("cc",), writes=("cc",))
        p.op("vector", lambda E: E.tensor_copy(identb[:, :], ident[:, :]),
             writes=("identb",))
        p.op("vector", lambda E: E.memset(cc[:, SC:SC + 16], 0.0),
             reads=("cc",), writes=("cc",))
        p.op("vector", lambda E: E.memset(onecol[:, :], 1.0), writes=("onecol",))
        p.op("vector", lambda E: E.memset(onesT[:, :], 1.0), writes=("onesT",))
        p.op("vector", lambda E: E.memset(ebase[:, 0:1], 0.0), writes=("ebase",))
        p.op("vector", lambda E: E.reduce_sum(rows[0:1, 8:9], xprev_sb, axis=AX),
             writes=("prow",))
        rec["prow_sum"] = p.mile("vector", vS)
        p.wait("scalar", vS, rec["prow_sum"])
        p.op("scalar", lambda E: E.activation(xnprev, xprev_sb, AF.Square,
                                              accum_out=rows[0:1, 9:10]),
             writes=("xnprev", "prow"))
        rec["prow_ssq"] = p.mile("scalar", aS)
        p.wait("vector", aS, rec["prow_ssq"])
        p.op("vector", lambda E: E.tensor_scalar(
            rows[0:1, 10:11], rows[0:1, 8:9], 1.0 / D, None, OP.mult),
             reads=("prow",), writes=("prow",))
        p.op("vector", lambda E: E.tensor_scalar(
            rows[0:1, 11:12], rows[0:1, 10:11], rows[0:1, 10:11], None, OP.mult),
             reads=("prow",), writes=("prow",))
        p.op("vector", lambda E: E.scalar_tensor_tensor(
            rows[0:1, 12:13], rows[0:1, 9:10], 1.0 / D, rows[0:1, 11:12], OP.mult, OP.subtract),
             reads=("prow",), writes=("prow",))
        p.op("vector", lambda E: E.tensor_scalar(
            rows[0:1, 12:13], rows[0:1, 12:13], EPS, None, OP.add),
             reads=("prow",), writes=("prow",))
        rec["prow_var"] = p.mile("vector", vS)
        p.wait("scalar", vS, rec["prow_var"])
        p.op("scalar", lambda E: E.activation(
            rows[0:1, 13:14], rows[0:1, 12:13], AF.Sqrt),
             reads=("prow",), writes=("prow",))
        rec["prow_sqv"] = p.mile("scalar", aS)
        p.wait("vector", aS, rec["prow_sqv"])
        p.op("vector", lambda E: E.reciprocal(rows[0:1, 14:15], rows[0:1, 13:14]),
             reads=("prow",), writes=("prow",))
        p.op("vector", lambda E: E.tensor_scalar(
            xnprev, xprev_sb, rows[0:1, 10:11], rows[0:1, 14:15],
            OP.subtract, OP.mult),
             reads=("prow",), writes=("xnprev",))
        rec["xnprev"] = p.mile("vector", vS)

        p.wait2("tensor", rec["setup_dma"])
        p.wait("tensor", vS, rec["xnprev"])
        for k in range(NF):
            last = (k == NF - 1)
            fn = lambda E, k=k, last=last: E.matmul(
                ps[0][:, k:k + 1], bigB[0:1, D + 128 * k:D + 128 * (k + 1)],
                ident[0:1, 0:1], start=(k == 0), stop=last)
            if last:
                rec["pcolT"] = p.op("tensor", fn, inc=tS)
            else:
                p.raw("tensor", fn)
        p.wait("vector", tS, rec["pcolT"])
        p.op("vector", lambda E: E.tensor_copy(cc[:, PC:PC + 16], ps[0][:, 0:16]),
             reads=("cc",), writes=("cc",))
        bank_free[id(ps[0])] = (vS, p.mile("vector", vS))

        wload_i = [0]
        wload_rec = {}
        slot_cons = {}
        slot_cur = [0]

        def load_col(kind, src_ap, extra_waits=(), width=2048):
            nhalf = 1 if width <= 2048 else 2
            if nhalf == 2 and slot_cur[0] % 2 == 1:
                slot_cur[0] += 1
            s0 = slot_cur[0] % 4
            slots = [(slot_cur[0] + k) % 4 for k in range(nhalf)]
            slot_cur[0] += nhalf
            g = wload_i[0]
            wload_i[0] += 1
            wbt = wb[s0 // 2]
            off = (s0 % 2) * 2048
            for s in slots:
                cv = slot_cons.get(s)
                if cv is not None:
                    p.wait("sync", tS, cv)
            for sv in extra_waits:
                p.wait2("sync", sv)
            if kind == "pair":
                dst = bass.AP(wbt, 0, [[4096, 128], [2048, 2], [1, 2048]])
            else:
                dst = wbt[:, off:off + width]
            wload_rec[g] = WGT.issue(
                p, "sync", lambda E, a=dst, b=src_ap: E.dma_start(a, b))
            region = wbt[:, off:off + (4096 if nhalf == 2 else width)]
            return (g, slots), region

        def col_wait(gs):
            p.wait2("tensor", wload_rec[gs[0]])

        def col_done(gs, tsval):
            for s in gs[1]:
                slot_cons[s] = tsval

        def mm_group(bank, width, lhsT_fn, rhs_fn, kk):
            out = bank[:, 0:width]
            for k in range(kk):
                last = (k == kk - 1)
                fn = lambda E, k=k, last=last, o=out: E.matmul(
                    o, lhsT_fn(k), rhs_fn(k), start=(k == 0), stop=last)
                if last:
                    return p.op("tensor", fn, inc=tS)
                p.raw("tensor", fn)

        def emit_ln(tc, out_tile, out_name, tag, wait_ts=None):
            if wait_ts is not None:
                p.wait("vector", tS, wait_ts)
                p.wait("scalar", tS, wait_ts)
            p.op("vector", lambda E, tc=tc: E.reduce_sum(
                rows[:, R_SUM:R_SUM + 1], xtm[:, tc * D:(tc + 1) * D], axis=AX),
                 reads=("xtm",), writes=("rows",))
            rec[("lnsum", tag)] = p.mile("vector", vS)
            p.op("scalar", lambda E, tc=tc: E.activation(
                xntma[:, :], xtm[:, tc * D:(tc + 1) * D], AF.Square,
                accum_out=rows[:, R_SSQ:R_SSQ + 1]),
                 reads=("xtm",), writes=("xntma", "rows"))
            rec[("lnssq", tag)] = p.mile("scalar", aS)
            p.wait("vector", aS, rec[("lnssq", tag)])
            p.op("vector", lambda E: E.tensor_scalar(
                rows[:, R_MEAN:R_MEAN + 1], rows[:, R_SUM:R_SUM + 1],
                1.0 / D, None, OP.mult),
                 reads=("rows",), writes=("rows",))
            p.op("vector", lambda E: E.tensor_scalar(
                rows[:, R_MSQ:R_MSQ + 1], rows[:, R_MEAN:R_MEAN + 1],
                rows[:, R_MEAN:R_MEAN + 1], None, OP.mult),
                 reads=("rows",), writes=("rows",))
            p.op("vector", lambda E: E.scalar_tensor_tensor(
                rows[:, R_VAR:R_VAR + 1], rows[:, R_SSQ:R_SSQ + 1], 1.0 / D,
                rows[:, R_MSQ:R_MSQ + 1], OP.mult, OP.subtract),
                 reads=("rows",), writes=("rows",))
            p.op("vector", lambda E: E.tensor_scalar(
                rows[:, R_VAR:R_VAR + 1], rows[:, R_VAR:R_VAR + 1], EPS, None, OP.add),
                 reads=("rows",), writes=("rows",))
            rec[("lnvar", tag)] = p.mile("vector", vS)
            p.wait("scalar", vS, rec[("lnvar", tag)])
            p.op("scalar", lambda E: E.activation(
                rows[:, R_SQV:R_SQV + 1], rows[:, R_VAR:R_VAR + 1], AF.Sqrt),
                 reads=("rows",), writes=("rows",))
            rec[("lnsqv", tag)] = p.mile("scalar", aS)
            p.wait("vector", aS, rec[("lnsqv", tag)])
            p.op("vector", lambda E: E.reciprocal(
                rows[:, R_RSTD:R_RSTD + 1], rows[:, R_SQV:R_SQV + 1]),
                 reads=("rows",), writes=("rows",))
            p.op("vector", lambda E, tc=tc, o=out_tile: E.tensor_scalar(
                o, xtm[:, tc * D:(tc + 1) * D], rows[:, R_MEAN:R_MEAN + 1],
                rows[:, R_RSTD:R_RSTD + 1], OP.subtract, OP.mult),
                 reads=("rows", "xtm"), writes=(out_name,))
            rec[("ln", tag)] = p.mile("vector", vS)

        def emit_transpose_tc(tc, src_tile, dst_tile, dst_name, idt, tag):
            for grp in range(4):
                bank = ps[0] if grp % 2 == 0 else ps[7]
                p.wait2("tensor", bank_free.get(id(bank)))
                for j in range(4):
                    f = grp * 4 + j
                    last = (j == 3)
                    fn = lambda E, f=f, j=j, b=bank, last=last: E.matmul(
                        b[:, 128 * j:128 * (j + 1)],
                        src_tile[:, 128 * f:128 * (f + 1)], idt[:, :],
                        start=(j == 0), stop=last)
                    if last:
                        v = p.op("tensor", fn, inc=tS)
                    else:
                        p.raw("tensor", fn)
                p.wait("vector", tS, v)
                f0 = grp * 4
                dst_ap = bass.AP(dst_tile, f0 * TT + tc * 128,
                                 [[NF * TT, 128], [TT, 4], [1, 128]])
                src_ap = bass.AP(bank, 0, [[512, 128], [128, 4], [1, 128]])
                p.op("vector", lambda E, d=dst_ap, s=src_ap: E.tensor_copy(d, s),
                     writes=(dst_name,))
                bank_free[id(bank)] = (vS, p.mile("vector", vS))
            rec[("T", tag)] = bank_free[id(ps[7])][1]
            rec[("Tts", tag)] = v

        def emit_gating(j):
            p.wait("vector", aS, rec[("gatelg", j)])
            scprev0 = rec.get(("scat", j - 2))
            if scprev0 is not None:
                p.wait2("vector", scprev0)
            for tc in range(NTC):
                p.op("vector", lambda E, tc=tc: E.reduce_max(
                    gat[:, G_SMAX:G_SMAX + 1], lgb[:, (j % 2) * 64 + tc * 16:(j % 2) * 64 + (tc + 1) * 16], axis=AX),
                     reads=("lgb",), writes=("gat_r",))
                p.op("vector", lambda E: E.tensor_scalar(
                    gat[:, G_NM:G_NM + 1], gat[:, G_SMAX:G_SMAX + 1], -1.0, None, OP.mult),
                     reads=("gat_r",), writes=("gat_r",))
                vnm = p.mile("vector", vS)
                p.wait("scalar", vS, vnm)
                p.op("scalar", lambda E, tc=tc: E.activation(
                    gat[:, G_ES:G_ES + 16], lgb[:, (j % 2) * 64 + tc * 16:(j % 2) * 64 + (tc + 1) * 16], AF.Exp,
                    bias=gat[:, G_NM:G_NM + 1], accum_out=gat[:, G_SSUM:G_SSUM + 1]),
                     reads=("lgb",), writes=("gat_es",))
                ves = p.mile("scalar", aS)
                p.wait("vector", aS, ves)
                p.op("vector", lambda E: E.reciprocal(
                    gat[:, G_RS:G_RS + 1], gat[:, G_SSUM:G_SSUM + 1]),
                     reads=("gat_es",), writes=("gat_r",))
                p.op("vector", lambda E: E.tensor_scalar(
                    gat[:, G_SCO:G_SCO + 16], gat[:, G_ES:G_ES + 16],
                    gat[:, G_RS:G_RS + 1], None, OP.mult),
                     reads=("gat_es", "gat_r"), writes=("gat_sco",))
                for g4 in range(4):
                    p.op("vector", lambda E, g4=g4: E.reduce_max(
                        gat[:, G_GMAX + g4:G_GMAX + g4 + 1],
                        gat[:, G_SCO + 4 * g4:G_SCO + 4 * (g4 + 1)], axis=AX),
                         reads=("gat_sco",), writes=("gat_gm",))
                p.op("vector", lambda E: E.reduce_max(
                    gat[:, G_M1:G_M1 + 1], gat[:, G_GMAX:G_GMAX + 4], axis=AX),
                     reads=("gat_gm",), writes=("gat_r",))
                p.op("vector", lambda E: E.tensor_scalar(
                    gat[:, G_MK1:G_MK1 + 4], gat[:, G_GMAX:G_GMAX + 4],
                    gat[:, G_M1:G_M1 + 1], None, OP.is_ge),
                     reads=("gat_gm", "gat_r"), writes=("gat_mk1",))
                p.op("vector", lambda E: E.scalar_tensor_tensor(
                    gat[:, G_G2A:G_G2A + 4], gat[:, G_MK1:G_MK1 + 4], NEGBIG,
                    gat[:, G_GMAX:G_GMAX + 4], OP.mult, OP.add),
                     reads=("gat_mk1", "gat_gm"), writes=("gat_g2",))
                p.op("vector", lambda E: E.reduce_max(
                    gat[:, G_M2:G_M2 + 1], gat[:, G_G2A:G_G2A + 4], axis=AX),
                     reads=("gat_g2",), writes=("gat_r",))
                p.op("vector", lambda E: E.tensor_scalar(
                    gat[:, G_GSEL:G_GSEL + 4], gat[:, G_GMAX:G_GMAX + 4],
                    gat[:, G_M2:G_M2 + 1], None, OP.is_ge),
                     reads=("gat_gm", "gat_r"), writes=("gat_gsel",))
                p.op("vector", lambda E: E.tensor_copy(
                    bass.AP(gat, G_MASK, [[128, 128], [4, 4], [1, 4]]),
                    bass.AP(gat, G_GSEL, [[128, 128], [1, 4], [0, 4]])),
                     reads=("gat_gsel",), writes=("gat_mask",))
                p.op("vector", lambda E: E.tensor_tensor(
                    gat[:, G_CUR:G_CUR + 16], gat[:, G_SCO:G_SCO + 16],
                    gat[:, G_MASK:G_MASK + 16], OP.mult),
                     reads=("gat_sco", "gat_mask"), writes=("gat_cur",))
                p.op("vector", lambda E: E.tensor_copy(
                    gat[:, G_MASK:G_MASK + 16], gat[:, G_CUR:G_CUR + 16]),
                     reads=("gat_cur",), writes=("gat_mask",))
                for tsel in range(4):
                    p.op("vector", lambda E, t2=tsel: E.reduce_max(
                        gat[:, G_M4 + t2:G_M4 + t2 + 1],
                        gat[:, G_CUR:G_CUR + 16], axis=AX),
                         reads=("gat_cur",), writes=("gat_m4",))
                    if tsel < 3:
                        p.op("vector", lambda E, t2=tsel: E.tensor_scalar(
                            gat[:, G_ES:G_ES + 16], gat[:, G_CUR:G_CUR + 16],
                            gat[:, G_M4 + t2:G_M4 + t2 + 1], None, OP.is_ge),
                             reads=("gat_cur", "gat_m4"), writes=("gat_es",))
                        p.op("vector", lambda E: E.scalar_tensor_tensor(
                            gat[:, G_CUR:G_CUR + 16], gat[:, G_ES:G_ES + 16], NEGBIG,
                            gat[:, G_CUR:G_CUR + 16], OP.mult, OP.add),
                             reads=("gat_es", "gat_cur"), writes=("gat_cur",))
                p.op("vector", lambda E: E.reduce_sum(
                    gat[:, G_SW:G_SW + 1], gat[:, G_M4:G_M4 + 4], axis=AX),
                     reads=("gat_m4",), writes=("gat_r",))
                p.op("vector", lambda E: E.tensor_scalar(
                    gat[:, G_SW + 1:G_SW + 2], gat[:, G_SW:G_SW + 1], 1e-20, None, OP.add),
                     reads=("gat_r",), writes=("gat_r",))
                p.op("vector", lambda E: E.reciprocal(
                    gat[:, G_RSW:G_RSW + 1], gat[:, G_SW + 1:G_SW + 2]),
                     reads=("gat_r",), writes=("gat_r",))
                p.op("vector", lambda E: E.tensor_scalar(
                    gat[:, G_ES:G_ES + 16], gat[:, G_MASK:G_MASK + 16],
                    gat[:, G_M4 + 3:G_M4 + 4], None, OP.is_ge),
                     reads=("gat_mask", "gat_m4"), writes=("gat_es",))
                p.op("vector", lambda E, tc=tc: E.scalar_tensor_tensor(
                    cwt[:, (j % 2) * 64 + tc * 16:(j % 2) * 64 + (tc + 1) * 16], gat[:, G_MASK:G_MASK + 16],
                    gat[:, G_RSW:G_RSW + 1], gat[:, G_ES:G_ES + 16],
                    OP.mult, OP.mult),
                     reads=("gat_mask", "gat_es", "gat_r"), writes=("cwt",))
            rec[("cwt", j)] = p.mile("vector", vS)

            MK, RK, OF = 0, TT, 2 * TT
            p.wait("tensor", vS, rec[("cwt", j)])
            p.wait2("tensor", bank_free.get(id(ps[0])))
            for tc in range(NTC):
                last = (tc == NTC - 1)
                fn = lambda E, tc=tc, last=last: E.matmul(
                    ps[0][0:16, tc * 128:(tc + 1) * 128],
                    cwt[:, (j % 2) * 64 + tc * 16:(j % 2) * 64 + (tc + 1) * 16], ident[:, :],
                    start=(tc == 0), stop=last)
                if last:
                    vcw = p.op("tensor", fn, inc=tS)
                else:
                    p.raw("tensor", fn)
            p.wait("vector", tS, vcw)
            p.op("vector", lambda E: E.tensor_copy(cw_fm[:, :], ps[0][0:16, 0:512]),
                 writes=("cw_fm",))
            bank_free[id(ps[0])] = (vS, p.mile("vector", vS))
            p.op("vector", lambda E: E.tensor_scalar(
                gfm[:, MK:MK + TT], cw_fm[:, :], 0.0, None, OP.is_gt),
                 reads=("cw_fm",), writes=("gfm",))
            p.op("vector", lambda E: E.tensor_tensor_scan(
                gfm[:, RK:RK + TT], bass.AP(onecol, 0, [[1, 16], [0, TT]]),
                gfm[:, MK:MK + TT], ebase[:, 0:1], OP.mult, OP.add),
                 reads=("gfm", "onecol", "ebase"), writes=("gfm",))
            p.op("vector", lambda E: E.tensor_copy(
                ebase[:, 0:1], gfm[:, RK + TT - 1:RK + TT]),
                 reads=("gfm",), writes=("ebase",))
            p.op("vector", lambda E: E.tensor_scalar(
                gfm[:, RK:RK + TT], gfm[:, RK:RK + TT], -1.0, None, OP.add),
                 reads=("gfm",), writes=("gfm",))
            p.op("vector", lambda E: E.tensor_scalar(
                gfm[:, OF:OF + TT], gfm[:, RK:RK + TT], 1.0 / 128.0, -0.496,
                OP.mult, OP.add),
                 reads=("gfm",), writes=("gfm",))
            p.op("vector", lambda E: E.tensor_scalar(
                gfm[:, OF:OF + TT], gfm[:, OF:OF + TT], 12582912.0, None, OP.add),
                 reads=("gfm",), writes=("gfm",))
            p.op("vector", lambda E: E.tensor_scalar(
                gfm[:, OF:OF + TT], gfm[:, OF:OF + TT], -12582912.0, None, OP.add),
                 reads=("gfm",), writes=("gfm",))
            p.op("vector", lambda E, ng=ngrp: E.tensor_scalar(
                cw_fm[:, :], gfm[:, RK:RK + TT], float(ng), None, OP.mult),
                 reads=("gfm",), writes=("cw_fm",))
            p.op("vector", lambda E, ng=ngrp: E.scalar_tensor_tensor(
                gfm[:, OF:OF + TT], gfm[:, OF:OF + TT], -float(128 * ng - 1),
                cw_fm[:, :], OP.mult, OP.add),
                 reads=("gfm", "cw_fm"), writes=("gfm",))
            p.op("vector", lambda E: E.tensor_tensor(
                gfm[:, OF:OF + TT], gfm[:, OF:OF + TT], gfm[:, MK:MK + TT], OP.mult),
                 reads=("gfm",), writes=("gfm",))
            p.op("vector", lambda E: E.tensor_tensor(
                gfm[:, RK:RK + TT], gfm[:, RK:RK + TT], gfm[:, MK:MK + TT], OP.mult),
                 reads=("gfm",), writes=("gfm",))
            p.op("vector", lambda E: E.tensor_scalar(
                gfm[:, OF:OF + TT], gfm[:, OF:OF + TT], ebase[:, 1:2], None, OP.add),
                 reads=("gfm", "ebase"), writes=("gfm",))
            p.op("vector", lambda E: E.tensor_scalar(
                gfm[:, RK:RK + TT], gfm[:, RK:RK + TT], ebase[:, 1:2], None, OP.add),
                 reads=("gfm", "ebase"), writes=("gfm",))
            p.op("vector", lambda E: E.tensor_scalar(
                gfm[:, MK:MK + TT], gfm[:, MK:MK + TT], -1.0, 1.0, OP.mult, OP.add),
                 reads=("gfm",), writes=("gfm",))
            p.op("vector", lambda E: E.scalar_tensor_tensor(
                gfm[:, OF:OF + TT], gfm[:, MK:MK + TT], 1.0e9, gfm[:, OF:OF + TT],
                OP.mult, OP.add),
                 reads=("gfm",), writes=("gfm",))
            p.op("vector", lambda E: E.scalar_tensor_tensor(
                gfm[:, RK:RK + TT], gfm[:, MK:MK + TT], 1.0e9, gfm[:, RK:RK + TT],
                OP.mult, OP.add),
                 reads=("gfm",), writes=("gfm",))
            rec[("offsfm", j)] = p.mile("vector", vS)
            p.wait("tensor", vS, rec[("offsfm", j)])
            p.wait2("tensor", bank_free.get(id(ps[7])))
            nT = 0
            for base, srcoff in ((0, OF), (64, RK)):
                for tc in range(NTC):
                    nT += 1
                    last = (nT == 8)
                    fn = lambda E, b2=base, so=srcoff, tc=tc, last=last, first=(nT == 1): \
                        E.matmul(ps[7][:, b2 + tc * 16:b2 + (tc + 1) * 16],
                                 gfm[0:16, so + tc * 128:so + (tc + 1) * 128],
                                 ident[0:16, 0:16], start=first, stop=last)
                    if last:
                        voff = p.op("tensor", fn, inc=tS)
                    else:
                        p.raw("tensor", fn)
            p.wait("vector", tS, voff)
            scprev = rec.get(("scat", j - 2))
            if scprev is not None:
                p.wait2("vector", scprev)
            p.op("vector", lambda E: E.tensor_copy(offsu[:, (j % 2) * 128:(j % 2) * 128 + 128], ps[7][:, 0:128]),
                 writes=("offsu",))
            bank_free[id(ps[7])] = (vS, p.mile("vector", vS))
            rec[("offsu", j)] = bank_free[id(ps[7])][1]
            p.wait("gpsimd", vS, rec[("offsu", j)])
            if j == 0:
                p.wait2("gpsimd", rec["idxinit"])
            for tc in range(NTC):
                g = j * NTC + tc
                for e in range(NE):
                    SCT.issue(p, "gpsimd", lambda E, e=e, tc=tc, g=g: E.indirect_dma_start(
                        idx_scr.rearrange("e s o -> (e s) o"),
                        bass.IndirectOffsetOnAxis(
                            ap=offsu[:, (j % 2) * 128 + tc * 16 + e:(j % 2) * 128 + tc * 16 + e + 1], axis=0),
                        tids_sb[:, g:g + 1], None,
                        bounds_check=bndreg, oob_is_err=False), serialize=True)
                    SCT.issue(p, "gpsimd", lambda E, e=e, tc=tc: E.indirect_dma_start(
                        cwc_scr.rearrange("e s o -> (e s) o"),
                        bass.IndirectOffsetOnAxis(
                            ap=offsu[:, (j % 2) * 128 + 64 + tc * 16 + e:(j % 2) * 128 + 64 + tc * 16 + e + 1], axis=0),
                        cwt[:, (j % 2) * 64 + tc * 16 + e:(j % 2) * 64 + tc * 16 + e + 1],
                        None,
                        bounds_check=bndreg, oob_is_err=False), serialize=True)
            rec[("scat", j)] = (SCT.sems[0], SCT.sems[0].cnt)

            rec[("lgused", j % 2)] = rec[("offsu", j)]

        p.wait("vector", aS, rec["wcols"])
        for i in range(ntile):
            if i > 0:
                p.wait("sync", vS, rec[("ln", ("p1", i - 1, NTC - 1))])
                p.wait("sync", aS, rec[("lnssq", ("p1", i - 1, NTC - 1))])
            rec[("xload", i)] = XL.issue(
                p, "sync", lambda E, i=i: E.dma_start(
                    xtm[:, :].rearrange("p (c f) -> p c f", c=NTC),
                    x_d[i * TT:(i + 1) * TT, :].rearrange("(c p) f -> p c f", p=128)))
            p.wait2("vector", rec[("xload", i)])
            p.wait2("scalar", rec[("xload", i)])

            for tc in range(NTC):
                wts = rec.get(("Tts", ("p1", i, tc - 1))) if tc > 0 else \
                    (rec.get(("Tts", ("p1", i - 1, NTC - 1))) if i > 0 else None)
                emit_ln(tc, xntma[:, :], "xntma", ("p1", i, tc), wait_ts=wts)
                p.wait("tensor", vS, rec[("ln", ("p1", i, tc))])
                emit_transpose_tc(tc, xntma, bigB, "bigB", ident, ("p1", i, tc))

            p.wait("vector", vS, rec[("T", ("p1", i, NTC - 1))])
            if i > 0:
                p.wait("vector", tS, rec[("rmm", i - 1)])
            p.op("vector", lambda E: E.tensor_copy(
                cc[:, ET:ET + 16], cc[:, PC:PC + 16]),
                 reads=("cc",), writes=("cc",))
            def _dprime(f):
                p.op("vector", lambda E, f=f: E.tensor_tensor(
                    sgt[:, 1:TT], bigB[:, f * TT:f * TT + TT - 1],
                    bigB[:, f * TT + 1:f * TT + TT], OP.subtract),
                     reads=("bigB",), writes=("sgt",))
                p.op("vector", lambda E, f=f: E.tensor_tensor(
                    sgt[:, 0:1], cc[:, ET + f:ET + f + 1],
                    bigB[:, f * TT:f * TT + 1], OP.subtract),
                     reads=("bigB", "cc"), writes=("sgt",))
            for f in range(NF):
                _dprime(f)
                p.op("vector", lambda E, f=f: E.scalar_tensor_tensor(
                    mixA[:, f * TT:(f + 1) * TT], sgt[:, 0:TT],
                    cc[:, CK + f:CK + f + 1], bigB[:, f * TT:(f + 1) * TT],
                    OP.mult, OP.add),
                     reads=("sgt", "cc", "bigB"), writes=("mixA",))
            rec[("mixk", i)] = p.mile("vector", vS)
            for f in range(NF):
                _dprime(f)
                p.op("vector", lambda E, f=f: E.scalar_tensor_tensor(
                    mixB[:, f * TT:(f + 1) * TT], sgt[:, 0:TT],
                    cc[:, CV + f:CV + f + 1], bigB[:, f * TT:(f + 1) * TT],
                    OP.mult, OP.add),
                     reads=("sgt", "cc", "bigB"), writes=("mixB",))
                p.op("vector", lambda E, f=f: E.scalar_tensor_tensor(
                    bigCb[:, 8192 + f * TT:8192 + (f + 1) * TT], sgt[:, 0:TT],
                    cc[:, CR + f:CR + f + 1], bigB[:, f * TT:(f + 1) * TT],
                    OP.mult, OP.add),
                     reads=("sgt", "cc", "bigB"), writes=("bigC",))
                p.op("vector", lambda E, f=f: E.tensor_copy(
                    cc[:, PC + f:PC + f + 1], bigB[:, f * TT + TT - 1:f * TT + TT]),
                     reads=("bigB",), writes=("cc",))
            rec[("mixkv", i)] = p.mile("vector", vS)

            for m in range(NF):
                xw = [rec["cast_wk"]] if (i == 0 and m == 0) else []
                gk, bufk = load_col("plain", wk_b[m][:, :], xw, width=2048)
                xw = [rec["cast_wv"]] if (i == 0 and m == 0) else []
                gv, bufv = load_col("plain", wv_b[m][:, :], xw, width=2048)
                col_wait(gk)
                if m == 0:
                    p.wait("tensor", vS, rec[("mixk", i)])
                bkk = ps[1] if m % 2 == 0 else ps[3]
                bvv = ps[2] if m % 2 == 0 else ps[4]
                p.wait2("tensor", bank_free.get(id(bkk)))
                vk = mm_group(bkk, TT,
                              lambda k, b=bufk: b[:, 128 * k:128 * (k + 1)],
                              lambda k: mixA[:, k * TT:(k + 1) * TT], NF)
                col_done(gk, vk)
                col_wait(gv)
                if m == 0:
                    p.wait("tensor", vS, rec[("mixkv", i)])
                p.wait2("tensor", bank_free.get(id(bvv)))
                vv = mm_group(bvv, TT,
                              lambda k, b=bufv: b[:, 128 * k:128 * (k + 1)],
                              lambda k: mixB[:, k * TT:(k + 1) * TT], NF)
                col_done(gv, vv)
                rec[("kvmm_last", i)] = vv
                p.wait("scalar", tS, vk)
                kprev = rec.get(("kvtt", m % 2))
                if kprev is not None:
                    p.wait("scalar", vS, kprev)
                p.op("scalar", lambda E, m=m, bkk=bkk: E.copy(
                    kbuf[:, (m % 2) * 512:(m % 2) * 512 + 512], bkk[:, :]),
                     writes=("kbuf",))
                bank_free[id(bkk)] = (aS, p.mile("scalar", aS))
                gidx = m // 4
                gkey = i * 4 + gidx
                slot = (m % 4) * 512 + (gidx % 2) * 2048
                p.wait("vector", aS, bank_free[id(bkk)][1])
                p.wait("vector", tS, vv)
                dprev = rec.get(("kvdma_g", gkey - 2))
                if dprev is not None and m % 4 == 0:
                    p.wait2("vector", dprev)
                p.op("vector", lambda E, m=m, bvv=bvv, slot=slot: E.tensor_tensor(
                    bigCb[:, slot:slot + 512], bvv[:, :],
                    kbuf[:, (m % 2) * 512:(m % 2) * 512 + 512], OP.mult),
                     reads=("kbuf",), writes=("bigC",))
                kvtt = p.mile("vector", vS)
                rec[("kvtt", m % 2)] = kvtt
                bank_free[id(bvv)] = (vS, kvtt)
                p.op("vector", lambda E, m=m, slot=slot: E.tensor_tensor_scan(
                    st[:, 1:TT + 1],
                    bass.AP(cc, WC + m, [[160, 128], [0, TT]]),
                    bigCb[:, slot:slot + 512],
                    cc[:, SC + m:SC + m + 1], OP.mult, OP.add),
                     reads=("bigC", "cc"), writes=("st",))
                p.op("vector", lambda E, m=m: E.tensor_copy(
                    cc[:, SC + m:SC + m + 1], st[:, TT:TT + 1]),
                     reads=("st",), writes=("cc",))
                if m % 4 == 3:
                    vgrp = p.mile("vector", vS)
                    p.wait("scalar", vS, vgrp)
                    rec[("kvdma_g", gkey)] = KVST.issue(
                        p, "scalar", lambda E, i=i, gi=gidx: E.dma_start(
                            kv_scr.rearrange("(m p) T -> p m T", p=128)[
                                :, gi * 4:(gi + 1) * 4, i * TT:(i + 1) * TT],
                            bigCb[:, (gi % 2) * 2048:(gi % 2) * 2048 + 2048].rearrange(
                                "p (m t) -> p m t", m=4)))

            for m in range(NF):
                xw = [rec["cast_wr"]] if (i == 0 and m == 0) else []
                gr, bufr = load_col("plain", wr_b[m][:, :], xw, width=2048)
                col_wait(gr)
                brr = ps[5] if m % 2 == 0 else ps[6]
                p.wait2("tensor", bank_free.get(id(brr)))
                vr = mm_group(brr, TT,
                              lambda k, b=bufr: b[:, 128 * k:128 * (k + 1)],
                              lambda k: bigCb[:, 8192 + k * TT:8192 + (k + 1) * TT], NF)
                col_done(gr, vr)
                p.wait("scalar", tS, vr)
                gidx = m // 4
                gkey = i * 4 + gidx
                slot = 4096 + (m % 4) * 512 + (gidx % 2) * 2048
                dprev = rec.get(("rdma_g", gkey - 2))
                if dprev is not None and m % 4 == 0:
                    p.wait2("scalar", dprev)
                p.op("scalar", lambda E, slot=slot, brr=brr: E.activation(
                    bigCb[:, slot:slot + 512], brr[:, :], AF.Sigmoid),
                     writes=("bigC_r",))
                bank_free[id(brr)] = (aS, p.mile("scalar", aS))
                if m % 4 == 3:
                    rec[("rdma_g", gkey)] = RST.issue(
                        p, "scalar", lambda E, i=i, gi=gidx: E.dma_start(
                            r_scr.rearrange("(m p) T -> p m T", p=128)[
                                :, gi * 4:(gi + 1) * 4, i * TT:(i + 1) * TT],
                            bigCb[:, 4096 + (gi % 2) * 2048:
                                  4096 + (gi % 2) * 2048 + 2048].rearrange(
                                "p (m t) -> p m t", m=4)))
            p.wait2("tensor", bank_free.get(id(ps[7])))
            for tc in range(NTC):
                for k in range(NF):
                    last_mm = (tc == NTC - 1 and k == NF - 1)
                    fn = lambda E, tc=tc, k=k, last_mm=last_mm: E.matmul(
                        ps[7][:, tc * 16:(tc + 1) * 16],
                        bigB[:, k * TT + tc * 128:k * TT + (tc + 1) * 128],
                        gw_sb[:, k * 16:(k + 1) * 16],
                        start=(k == 0), stop=(k == NF - 1))
                    if last_mm:
                        vgt = p.op("tensor", fn, inc=tS)
                    else:
                        p.raw("tensor", fn)
            p.wait("scalar", tS, vgt)
            glprev = rec.get(("lgused", i % 2))
            if glprev is not None:
                p.wait("scalar", vS, glprev)
            p.op("scalar", lambda E, i=i: E.copy(
                lgb[:, (i % 2) * 64:(i % 2) * 64 + 64], ps[7][:, 0:64]),
                 writes=("lgb",))
            bank_free[id(ps[7])] = (aS, p.mile("scalar", aS))
            rec[("gatelg", i)] = bank_free[id(ps[7])][1]
            if i > 0:
                emit_gating(i - 1)

            rec[("rmm", i)] = vr

        emit_gating(ntile - 1)
        rec["p1_scan_done"] = p.mile("vector", vS)

        p.raw("gpsimd", lambda E: E.reg_mov(bndreg, t_loc - 1))

        p.wait("gpsimd", vS, rec["p1_scan_done"])
        rec["ccin"] = GPD.issue(p, "gpsimd",
                                lambda E: E.dma_start(cc_in[:, :], cc[:, SC:SC + 16]))
        p.wait2("gpsimd", rec["ccin"])
        rec["coll"] = p.op("gpsimd", lambda E: E.collective_compute(
            "AllGather", mybir.AluOpType.bypass, replica_groups=replica_groups,
            ins=[cc_in.ap().opt()], outs=[cc_out.ap().opt()]), inc=cS)
        p.wait("gpsimd", cS, rec["coll"])
        rec["sgin"] = GPD.issue(p, "gpsimd",
                                lambda E: E.dma_start(sg_in[:, :], cc_out[0:128, :]))
        p.wait2("vector", rec["sgin"])
        p.op("vector", lambda E: E.tensor_scalar(
            cc[:, SC:SC + 16], sg_in[:, :], cc[:, IO:IO + 1], None, OP.mult),
             reads=("cc",), writes=("cc",))
        rec["sinit"] = p.mile("vector", vS)

        ecl = []
        for e in range(NE):
            ecl.append(("eup%d" % e, eup_d[e], eup_b[e]))
            ecl.append(("edn%d" % e, edn_d[e], edn_b[e]))
        emit_casts(ecl)

        p.wait("tensor", aS, aS.cnt)
        p.wait("tensor", vS, rec["sinit"])

        for i in range(ntile):
            wl = rec.get(("faccst", i - 1))
            if wl is not None:
                p.wait2("sync", wl)
            else:
                p.wait("sync", vS, rec[("ln", ("p1", ntile - 1, NTC - 1))])
                p.wait("sync", aS, rec[("lnssq", ("p1", ntile - 1, NTC - 1))])
            rec[("xload2", i)] = XL.issue(
                p, "sync", lambda E, i=i: E.dma_start(
                    xtm[:, :].rearrange("p (c f) -> p c f", c=NTC),
                    x_d[i * TT:(i + 1) * TT, :].rearrange("(c p) f -> p c f", p=128)))
            if i == 0:
                for gk in (ntile * 4 - 1, ntile * 4 - 2):
                    p.wait2("sync", rec[("kvdma_g", gk)])
                    p.wait2("sync", rec[("rdma_g", gk)])
            else:
                p.wait("sync", vS, rec[("wkvdone", i - 1)])
            rec[("kvload", i)] = KVL.issue(
                p, "sync", lambda E, i=i: E.dma_start(
                    bigCb[:, 0:8192].rearrange("p (m t) -> p m t", m=16),
                    kv_scr.rearrange("(m p) T -> p m T", p=128)[
                        :, 0:16, i * TT:(i + 1) * TT]))
            rec[("rload", i)] = RL.issue(
                p, "sync", lambda E, i=i: E.dma_start(
                    bigCb[:, 8192:16384].rearrange("p (m t) -> p m t", m=16),
                    r_scr.rearrange("(m p) T -> p m T", p=128)[
                        :, 0:16, i * TT:(i + 1) * TT]))

            p.wait2("vector", rec[("xload2", i)])
            p.wait2("scalar", rec[("xload2", i)])
            for tc in range(NTC):
                wts = rec.get(("Tts", ("p2", i, tc - 1))) if tc > 0 else \
                    (rec.get(("Tts", ("p2", i - 1, NTC - 1))) if i > 0 else
                     rec.get(("Tts", ("p1", ntile - 1, NTC - 1))))
                xprev_st = rec.get(("xnst", i, tc - 1)) if tc > 0 else \
                    rec.get(("xnst", i - 1, NTC - 1))
                if xprev_st is not None:
                    p.wait2("vector", xprev_st)
                    p.wait2("scalar", xprev_st)
                emit_ln(tc, xntmb, "xntma", ("p2", i, tc), wait_ts=wts)
                p.wait("scalar", vS, rec[("ln", ("p2", i, tc))])
                rec[("xnst", i, tc)] = XNST.issue(
                    p, "scalar", lambda E, i=i, tc=tc: E.dma_start(
                        xn_scr[i * TT + tc * 128:i * TT + (tc + 1) * 128, :],
                        xntmb))
                p.wait("tensor", vS, rec[("ln", ("p2", i, tc))])
                emit_transpose_tc(tc, xntmb, mixA, "mixA", identb, ("p2", i, tc))

            if i > 0:
                p.wait("vector", tS, rec[("smm", i - 1)])
            p.wait2("vector", rec[("kvload", i)])
            p.wait2("vector", rec[("rload", i)])
            for f in range(NF):
                p.op("vector", lambda E, f=f: E.tensor_copy(
                    st[:, 0:1], cc[:, SC + f:SC + f + 1]),
                     reads=("cc",), writes=("st",))
                p.op("vector", lambda E, f=f: E.tensor_tensor_scan(
                    st[:, 1:TT + 1],
                    bass.AP(cc, WC + f, [[160, 128], [0, TT]]),
                    bigCb[:, f * 512:(f + 1) * 512],
                    cc[:, SC + f:SC + f + 1], OP.mult, OP.add),
                     reads=("bigC", "cc"), writes=("st",))
                p.op("vector", lambda E, f=f: E.tensor_copy(
                    cc[:, SC + f:SC + f + 1], st[:, TT:TT + 1]),
                     reads=("st",), writes=("cc",))
                p.op("vector", lambda E, f=f: E.scalar_tensor_tensor(
                    usc[:, :], bigCb[:, f * 512:(f + 1) * 512],
                    cc[:, UC + f:UC + f + 1], st[:, 0:TT], OP.mult, OP.add),
                     reads=("bigC", "cc", "st"), writes=("usc",))
                p.op("vector", lambda E, f=f: E.tensor_tensor(
                    mixB[:, f * TT:(f + 1) * TT], usc[:, :],
                    bigCb[:, 8192 + f * 512:8192 + (f + 1) * 512], OP.mult),
                     reads=("usc", "bigC"), writes=("mixB",))
            rec[("wkvdone", i)] = p.mile("vector", vS)
            rec[("attn", i)] = rec[("wkvdone", i)]

            for j in range(NSI):
                xw = []
                if i == 0 and j == 0:
                    xw.append(rec["cast_sup"])
                gcol, buf = load_col("pair", sup_b[j].rearrange("g k x -> k g x")[:, :, :],
                                     xw, width=4096)
                col_wait(gcol)
                if j == 0:
                    p.wait("tensor", vS, rec[("T", ("p2", i, NTC - 1))])
                pg = ps[1] if j % 2 == 0 else ps[3]
                pu = ps[2] if j % 2 == 0 else ps[4]
                p.wait2("tensor", bank_free.get(id(pg)))
                p.wait2("tensor", bank_free.get(id(pu)))
                vg = mm_group(pg, TT,
                              lambda k, b=buf: b[:, 128 * k:128 * (k + 1)],
                              lambda k: mixA[:, k * TT:(k + 1) * TT], NF)
                vu = mm_group(pu, TT,
                              lambda k, b=buf: b[:, 2048 + 128 * k:2048 + 128 * (k + 1)],
                              lambda k: mixA[:, k * TT:(k + 1) * TT], NF)
                col_done(gcol, vu)
                p.wait("scalar", tS, vg)
                ju = rec.get(("jdone", j % 2))
                if ju is not None:
                    p.wait("scalar", vS, ju)
                p.op("scalar", lambda E, j=j, pg=pg: E.activation(
                    sgt[:, (j % 2) * 512:(j % 2) * 512 + 512], pg[:, :], AF.Silu),
                     writes=("sgt",))
                bank_free[id(pg)] = (aS, p.mile("scalar", aS))
                p.wait("vector", tS, vu)
                p.wait("vector", aS, bank_free[id(pg)][1])
                if j == 0:
                    sprev = rec.get(("sstore", i - 1))
                    if sprev is not None:
                        p.wait2("vector", sprev)
                p.op("vector", lambda E, pu=pu: E.tensor_copy(usc[:, :], pu[:, :]),
                     writes=("usc",))
                p.op("vector", lambda E, j=j: E.tensor_tensor(
                    abf[:, j * TT:(j + 1) * TT],
                    sgt[:, (j % 2) * 512:(j % 2) * 512 + 512], usc[:, :], OP.mult),
                     reads=("sgt", "usc"), writes=("abf",))
                jv = p.mile("vector", vS)
                rec[("jdone", j % 2)] = jv
                bank_free[id(pu)] = (vS, jv)
            rec[("adone", i)] = jv

            p.wait("tensor", vS, rec[("adone", i)])
            for m in range(16):
                xw = []
                if i == 0 and m == 0:
                    xw.append(rec["cast_sdn"])
                gcol, buf = load_col("plain", sdn_b[m][:, :], xw, width=2816)
                col_wait(gcol)
                dbank = ps[5] if m % 2 == 0 else ps[6]
                p.wait2("tensor", bank_free.get(id(dbank)))
                vd = mm_group(dbank, TT,
                              lambda k, b=buf: b[:, 128 * k:128 * (k + 1)],
                              lambda k: abf[:, k * TT:(k + 1) * TT], NSI)
                col_done(gcol, vd)
                p.wait("vector", tS, vd)
                p.op("vector", lambda E, m=m, db=dbank: E.tensor_copy(
                    bigB[:, m * TT:(m + 1) * TT], db[:, :]),
                     writes=("acc%d" % m,))
                bank_free[id(dbank)] = (vS, p.mile("vector", vS))
                rec[("sdacc", m)] = bank_free[id(dbank)][1]

            for m in range(NF):
                xw = [rec["cast_wo"]] if (i == 0 and m == 0) else []
                gcol, buf = load_col("plain", wo_b[m][:, :], xw, width=2048)
                col_wait(gcol)
                if m == 0:
                    p.wait("tensor", vS, rec[("attn", i)])
                sbank = ps[7] if m % 2 == 0 else ps[0]
                p.wait2("tensor", bank_free.get(id(sbank)))
                vsm = mm_group(sbank, TT,
                               lambda k, b=buf: b[:, 128 * k:128 * (k + 1)],
                               lambda k: mixB[:, k * TT:(k + 1) * TT], NF)
                col_done(gcol, vsm)
                mtprev = rec.get(("mbfT", m % 2))
                if mtprev is not None:
                    p.wait("vector", tS, mtprev)
                p.wait("vector", tS, vsm)
                p.op("vector", lambda E, m=m, sbank=sbank: E.tensor_copy(
                    mbf[:, (m % 2) * 512:(m % 2) * 512 + 512], sbank[:, :]),
                     writes=("mbf",))
                mv = p.mile("vector", vS)
                bank_free[id(sbank)] = (vS, mv)
                tbank = ps[5] if m % 2 == 0 else ps[6]
                p.wait("tensor", vS, mv)
                p.wait2("tensor", bank_free.get(id(tbank)))
                for tc in range(NTC):
                    last = (tc == NTC - 1)
                    fn = lambda E, m=m, tc=tc, tb=tbank, last=last: E.matmul(
                        tb[:, tc * 128:(tc + 1) * 128],
                        mbf[:, (m % 2) * 512 + tc * 128:(m % 2) * 512 + (tc + 1) * 128],
                        identb[:, :], start=(tc == 0), stop=last)
                    if last:
                        vtm = p.op("tensor", fn, inc=tS)
                    else:
                        p.raw("tensor", fn)
                rec[("mbfT", m % 2)] = vtm
                p.wait("vector", tS, vtm)
                sdst = bass.AP(abf, m * 128, [[2 * NIC * TT, 128], [2048, NTC], [1, 128]])
                psrc = bass.AP(tbank, 0, [[512, 128], [128, NTC], [1, 128]])
                p.op("vector", lambda E, d=sdst, s2=psrc: E.tensor_copy(d, s2),
                     writes=("abf",))
                bank_free[id(tbank)] = (vS, p.mile("vector", vS))
                rec[("scomb", m)] = bank_free[id(tbank)][1]
            rec[("smm", i)] = vsm
            p.wait("scalar", vS, rec[("scomb", NF - 1)])
            rec[("sstore", i)] = SST.issue(
                p, "scalar", lambda E, i=i: E.dma_start(
                    s_scr[i * TT:(i + 1) * TT, :].rearrange("(c p) f -> p c f", p=128),
                    abf[:, 0:8192].rearrange("p (c f) -> p c f", c=NTC)))

            for m in range(NF):
                tbank = ps[5] if m % 2 == 0 else ps[6]
                p.wait("tensor", vS, rec[("sdacc", m)])
                p.wait2("tensor", bank_free.get(id(tbank)))
                for tc in range(NTC):
                    last = (tc == NTC - 1)
                    fn = lambda E, m=m, tc=tc, tb=tbank, last=last: E.matmul(
                        tb[:, tc * 128:(tc + 1) * 128],
                        bigB[:, m * TT + tc * 128:m * TT + (tc + 1) * 128],
                        ident[:, :], start=(tc == 0), stop=last)
                    if last:
                        vft = p.op("tensor", fn, inc=tS)
                    else:
                        p.raw("tensor", fn)
                rec[("faccT", i)] = vft
                p.wait("vector", tS, vft)
                fdst = bass.AP(xtm, m * 128, [[NTC * D, 128], [D, NTC], [1, 128]])
                psrc = bass.AP(tbank, 0, [[512, 128], [128, NTC], [1, 128]])
                p.op("vector", lambda E, d=fdst, s2=psrc: E.tensor_copy(d, s2),
                     writes=("xtm",))
                bank_free[id(tbank)] = (vS, p.mile("vector", vS))
                rec[("fcomb", m)] = bank_free[id(tbank)][1]
            p.wait("scalar", vS, rec[("fcomb", NF - 1)])
            rec[("faccst", i)] = FST.issue(
                p, "scalar", lambda E, i=i: E.dma_start(
                    facc_scr[i * TT:(i + 1) * TT, :].rearrange("(c p) f -> p c f", p=128),
                    xtm[:, :].rearrange("p (c f) -> p c f", c=NTC)))

        gbufB = bigB.bitcast(BF16)
        nt0s = []
        _o = 0
        for _n in nts_list:
            nt0s.append(_o)
            _o += _n
        gg = [0]
        rec[("upmm_last2", -2)] = rec[("smm", ntile - 1)]
        rec[("upmm_last2", -1)] = rec[("smm", ntile - 1)]
        p.wait("gpsimd", tS, rec[("faccT", ntile - 1)])
        p.wait2("vector", rec[("sstore", ntile - 1)])
        p.wait2("scalar", rec[("sstore", ntile - 1)])
        p.wait2("vector", rec[("faccst", ntile - 1)])
        p.wait2("gpsimd", rec[("xnst", ntile - 1, NTC - 1)])
        for e in range(NE):
            if e == 0:
                p.wait2("sync", rec[("scat", ntile - 1)])
                p.wait("sync", vS, rec[("wkvdone", ntile - 1)])
            iw = rec.get(("idxused", e - 2))
            if iw is not None:
                p.wait("sync", tS, iw[0])
                p.wait2("sync", iw[1])
            rec[("idxl", e)] = CWL.issue(
                p, "sync", lambda E, e=e: E.dma_start(
                    idxb[:, (e % 2) * ngrp:(e % 2) * ngrp + ngrp],
                    idx_scr[e].rearrange("(p k) o -> p (k o)", p=128)))
            rec[("cwcl", e)] = CWL.issue(
                p, "sync", lambda E, e=e: E.dma_start(
                    cwc_sb[e % 2][:, :],
                    cwc_scr[e].rearrange("s o -> o s")))
            p.wait2("gpsimd", rec[("idxl", e)])
            for k in range(ngrp):
                g2 = gg[0]
                gg[0] += 1
                pv = rec.get(("cxst", g2 - 2))
                if pv is not None:
                    p.wait2("gpsimd", pv)
                gth = GTH.issue(
                    p, "gpsimd", lambda E, e=e, k=k, g2=g2: E.indirect_dma_start(
                        gbufB[:, (g2 % 2) * 2048:(g2 % 2) * 2048 + 2048], None,
                        xn_scr[:, :],
                        bass.IndirectOffsetOnAxis(
                            ap=idxb[:, (e % 2) * ngrp + k:(e % 2) * ngrp + k + 1], axis=0),
                        bounds_check=bndreg, oob_is_err=False))
                p.wait2("sync", gth)
                rec[("cxst", g2)] = CXS.issue(
                    p, "sync", lambda E, e=e, k=k, g2=g2: E.dma_start(
                        cx_scr[e, k * 128:(k + 1) * 128, :],
                        gbufB[:, (g2 % 2) * 2048:(g2 % 2) * 2048 + 2048]))
            rec[("cx_all", e)] = rec[("cxst", gg[0] - 1)]
            def emit_trl(nti2):
                nts2 = nts_list[nti2]
                nt02 = nt0s[nti2]
                gnt2 = e * len(nts_list) + nti2
                tw = rec.get(("upmm_last2", gnt2 - 2))
                lastg = (nt02 + nts2) // 128 - 1
                for f in range(NF):
                    if f == 0:
                        p.wait2("sync", rec[("cxst", gg[0] - ngrp + lastg)])
                        if tw is not None:
                            p.wait("sync", tS, tw)
                    if gnt2 % 2 == 0:
                        d = mixA[:, f * TT:f * TT + nts2]
                    else:
                        d = bigCb[:, 8192 + f * 512:8192 + f * 512 + nts2]
                    rec[("trl", e, nti2, f)] = TRLp[gnt2 % 2].issue(
                        p, "sync", lambda E, e=e, nt02=nt02, nts2=nts2, f=f, d=d:
                        E.dma_start(d, cx_scr[e, nt02:nt02 + nts2,
                                              128 * f:128 * (f + 1)],
                                    transpose=True))

            for nti, nts in enumerate(nts_list):
                nt0 = nt0s[nti]
                gnt = e * len(nts_list) + nti
                if nti == 0:
                    emit_trl(0)
                    if len(nts_list) > 1:
                        emit_trl(1)
                elif nti + 1 < len(nts_list):
                    emit_trl(nti + 1)
                p.wait2("tensor", bank_free.get(id(ps[7])))
                p.wait2("tensor", rec[("cwcl", e)])
                vbc = p.op("tensor", lambda E, e=e, nt0=nt0, nts=nts: E.matmul(
                    ps[7][:, 0:nts], onesT[(e % 2) * 32:(e % 2) * 32 + 1, :],
                    cwc_sb[e % 2][:, nt0:nt0 + nts],
                    start=True, stop=True), inc=tS)
                p.wait("scalar", tS, vbc)
                bprev = rec.get(("bcused", nti % 2))
                if bprev is not None:
                    p.wait("scalar", vS, bprev)
                p.op("scalar", lambda E, nti=nti, nts=nts: E.copy(
                    bcs[:, (nti % 2) * 512:(nti % 2) * 512 + nts], ps[7][:, 0:nts]),
                     writes=("bcs",))
                bank_free[id(ps[7])] = (aS, p.mile("scalar", aS))
                rec[("bcevac", e, nti)] = bank_free[id(ps[7])][1]
                for j in range(NIC):
                    xw = []
                    if nti == 0 and j == 0:
                        xw.append(rec["cast_eup%d" % e])
                    gcol, buf = load_col(
                        "pair", eup_b[e, j].rearrange("g k x -> k g x")[:, :, :],
                        xw, width=4096)
                    col_wait(gcol)
                    if j == 0:
                        for f in range(NF - 4, NF):
                            p.wait2("tensor", rec[("trl", e, nti, f)])
                    pg = ps[1] if j % 2 == 0 else ps[3]
                    pu = ps[2] if j % 2 == 0 else ps[4]
                    p.wait2("tensor", bank_free.get(id(pg)))
                    p.wait2("tensor", bank_free.get(id(pu)))
                    if gnt % 2 == 0:
                        rhsf = lambda k, nts=nts: mixA[:, k * TT:k * TT + nts]
                    else:
                        rhsf = lambda k, nts=nts: bigCb[:, 8192 + k * 512:
                                                        8192 + k * 512 + nts]
                    vg = mm_group(pg, nts,
                                  lambda k, b=buf: b[:, 128 * k:128 * (k + 1)],
                                  rhsf, NF)
                    vu = mm_group(pu, nts,
                                  lambda k, b=buf: b[:, 2048 + 128 * k:2048 + 128 * (k + 1)],
                                  rhsf, NF)
                    col_done(gcol, vu)
                    rec[("upmm_last2", gnt)] = vu
                    p.wait("scalar", tS, vg)
                    ju = rec.get(("jdone", j % 2))
                    if ju is not None:
                        p.wait("scalar", vS, ju)
                    p.op("scalar", lambda E, j=j, pg=pg, nts=nts: E.activation(
                        sgt[:, (j % 2) * 512:(j % 2) * 512 + nts], pg[:, 0:nts], AF.Silu),
                         writes=("sgt",))
                    bank_free[id(pg)] = (aS, p.mile("scalar", aS))
                    p.wait("vector", tS, vu)
                    p.wait("vector", aS, bank_free[id(pg)][1])
                    p.wait("vector", aS, rec[("bcevac", e, nti)])
                    p.op("vector", lambda E, pu=pu, nti=nti, nts=nts: E.tensor_tensor(
                        usc[:, 0:nts], pu[:, 0:nts],
                        bcs[:, (nti % 2) * 512:(nti % 2) * 512 + nts], OP.mult),
                         reads=("bcs",), writes=("usc",))
                    p.op("vector", lambda E, j=j, nti=nti, nts=nts: E.tensor_tensor(
                        abf[:, (nti % 2) * NIC * TT + j * TT:
                            (nti % 2) * NIC * TT + j * TT + nts],
                        sgt[:, (j % 2) * 512:(j % 2) * 512 + nts],
                        usc[:, 0:nts], OP.mult),
                         reads=("sgt", "usc"), writes=("abf",))
                    jv = p.mile("vector", vS)
                    rec[("jdone", j % 2)] = jv
                    bank_free[id(pu)] = (vS, jv)
                rec[("bcused", nti % 2)] = jv
                p.wait("tensor", vS, jv)
                for m in range(16):
                    xw = []
                    if nti == 0 and m == 0:
                        xw.append(rec["cast_edn%d" % e])
                    gcol, buf = load_col("plain", edn_b[e, m][:, :], xw, width=1408)
                    col_wait(gcol)
                    dbank = ps[5] if m % 2 == 0 else ps[6]
                    p.wait2("tensor", bank_free.get(id(dbank)))
                    vd = mm_group(dbank, nts,
                                  lambda k, b=buf: b[:, 128 * k:128 * (k + 1)],
                                  lambda k, nti=nti, nts=nts: abf[
                                      :, (nti % 2) * NIC * TT + k * TT:
                                      (nti % 2) * NIC * TT + k * TT + nts],
                                  NIC)
                    col_done(gcol, vd)
                    rec["dn_last"] = vd
                    p.wait("scalar", tS, vd)
                    dprev = rec.get(("dnTm", m))
                    if dprev is not None:
                        p.wait("scalar", tS, dprev)
                    p.op("scalar", lambda E, m=m, db=dbank, nts=nts: E.copy(
                        mixB[:, m * TT:m * TT + nts], db[:, 0:nts]),
                         writes=("mixB",))
                    bank_free[id(dbank)] = (aS, p.mile("scalar", aS))
                    rec[("dnstg", m)] = bank_free[id(dbank)][1]
                for sg in range(nts // 128):
                    saprev = rec.get(("sca", sg))
                    for mg in range(4):
                        tbank = ps[0] if mg % 2 == 0 else ps[7]
                        p.wait2("tensor", bank_free.get(id(tbank)))
                        if mg == 0 and sg == 0:
                            for m2 in range(16):
                                p.wait("tensor", aS, rec[("dnstg", m2)])
                        for mm in range(4):
                            m = mg * 4 + mm
                            last = (mm == 3)
                            fn = lambda E, m=m, mm=mm, sg=sg, tb=tbank, last=last: \
                                E.matmul(tb[:, mm * 128:(mm + 1) * 128],
                                         mixB[:, m * TT + sg * 128:m * TT + (sg + 1) * 128],
                                         identb[:, :], start=(mm == 0), stop=last)
                            if last:
                                vt2 = p.op("tensor", fn, inc=tS)
                            else:
                                p.raw("tensor", fn)
                        if sg == nts // 128 - 1:
                            for mm2 in range(4):
                                rec[("dnTm", mg * 4 + mm2)] = vt2
                        p.wait("vector", tS, vt2)
                        if saprev is not None and mg == 0:
                            p.wait2("vector", saprev)
                        p.op("vector", lambda E, sg=sg, mg=mg, tb=tbank: E.tensor_copy(
                            xtm[:, sg * 2048 + mg * 512:sg * 2048 + (mg + 1) * 512],
                            tb[:, :]),
                             writes=("xtm",))
                        bank_free[id(tbank)] = (vS, p.mile("vector", vS))
                    rec[("xstg", sg)] = bank_free[id(tbank)][1]
                    p.wait("gpsimd", vS, rec[("xstg", sg)])
                    if e == 0 and nti == 0 and sg == 0:
                        p.wait2("gpsimd", rec[("faccst", ntile - 1)])
                    kk2 = nt0 // 128 + sg
                    rec[("sca", sg)] = SCA.issue(
                        p, "gpsimd", lambda E, e=e, sg=sg, kk2=kk2: E.indirect_dma_start(
                            facc_scr[:, :],
                            bass.IndirectOffsetOnAxis(
                                ap=idxb[:, (e % 2) * ngrp + kk2:(e % 2) * ngrp + kk2 + 1],
                                axis=0),
                            xtm[:, sg * 2048:(sg + 1) * 2048], None,
                            bounds_check=bndreg, oob_is_err=False,
                            compute_op=OP.add), serialize=True)
            rec[("idxused", e)] = (rec[("upmm_last2", e * len(nts_list) + len(nts_list) - 1)],
                                   rec[("sca", nts_list[-1] // 128 - 1)])
        rec["sca_final"] = (SCA.sems[0], SCA.sems[0].cnt)

        for i in range(ntile):
            if i == 0:
                p.wait2("sync", rec["sca_final"])
                p.wait2("sync", rec[("cx_all", NE - 1)])
                p.wait("sync", tS, rec["dn_last"])
                p.wait2("sync", rec[("sstore", ntile - 1)])
            else:
                p.wait2("sync", rec[("ystore", i - 1)])
                p.wait("sync", vS, rec[("cdone", i - 1)])
            rec[("fload", i)] = KVL.issue(
                p, "sync", lambda E, i=i: E.dma_start(
                    bigB[:, :].rearrange("p (c f) -> p c f", c=NTC),
                    facc_scr[i * TT:(i + 1) * TT, :].rearrange("(c p) f -> p c f", p=128)))
            rec[("sload", i)] = RL.issue(
                p, "sync", lambda E, i=i: E.dma_start(
                    abf[:, 0:8192].rearrange("p (c f) -> p c f", c=NTC),
                    s_scr[i * TT:(i + 1) * TT, :].rearrange("(c p) f -> p c f", p=128)))
            rec[("xload3", i)] = XL.issue(
                p, "sync", lambda E, i=i: E.dma_start(
                    xtm[:, :].rearrange("p (c f) -> p c f", c=NTC),
                    x_d[i * TT:(i + 1) * TT, :].rearrange("(c p) f -> p c f", p=128)))
            p.wait2("scalar", rec[("fload", i)])
            p.wait2("vector", rec[("fload", i)])
            p.wait2("vector", rec[("sload", i)])
            p.wait2("vector", rec[("xload3", i)])
            for tc in range(NTC):
                if i == 0 and tc == 0:
                    p.wait("scalar", tS, rec[("idxused", NE - 1)][0])
                p.op("scalar", lambda E, tc=tc: E.activation(
                    mixA[:, tc * D:(tc + 1) * D],
                    bigB[:, tc * D:(tc + 1) * D], AF.Silu),
                     writes=("mixA",))
                gsr = p.mile("scalar", aS)
                p.wait("vector", aS, gsr)
                p.op("vector", lambda E, tc=tc: E.tensor_tensor(
                    bigB[:, tc * D:(tc + 1) * D], mixA[:, tc * D:(tc + 1) * D],
                    abf[:, tc * D:(tc + 1) * D], OP.mult),
                     reads=("mixA", "abf"), writes=("acc%d" % tc,))
                p.op("vector", lambda E, tc=tc: E.tensor_tensor(
                    xtm[:, tc * D:(tc + 1) * D], bigB[:, tc * D:(tc + 1) * D],
                    xtm[:, tc * D:(tc + 1) * D], OP.add),
                     reads=("acc%d" % tc, "xtm"), writes=("xtm",))
            rec[("cdone", i)] = p.mile("vector", vS)
            p.wait("scalar", vS, rec[("cdone", i)])
            rec[("ystore", i)] = YST.issue(
                p, "scalar", lambda E, i=i: E.dma_start(
                    y_d[i * TT:(i + 1) * TT, :].rearrange("(c p) f -> p c f", p=128),
                    xtm[:, :].rearrange("p (c f) -> p c f", c=NTC)))

        for stream, eng in ((YST, "scalar"), (CWST, "scalar"), (KVST, "scalar"),
                            (RST, "scalar"), (FST, "scalar"), (SST, "scalar"),
                            (XNST, "scalar"), (SETUP, "sync"), (WGT, "sync"),
                            (XL, "sync"), (KVL, "sync"), (RL, "sync"),
                            (CWL, "sync"), (CXS, "sync"),
                            (TRLp[0], "sync"), (TRLp[1], "sync"),
                            (CAST, "gpsimd"), (GPD, "gpsimd"), (SCT, "gpsimd"),
                            (GTH, "gpsimd"), (SCA, "gpsimd")):
            for s in stream.sems:
                if s.cnt > 0:
                    p.wait(eng, s, s.cnt)

        block = ctx.enter_context(nc.Block())
        for eng in Prog.ENGS:
            if p.st[eng]:
                getattr(block, eng)(_mk_runner(p.st[eng]))
    return nc


def _emit_down(p, rec, bank_free, load_col, col_wait, col_done, mm_group,
               ps, abf, bigB, edn_b, sdn_b, tS, vS, aS, i, e):
    shared = (e == NE)
    kk = NSI if shared else NIC
    w = 2816 if shared else 1408
    first_wait_done = False
    for m in range(16):
        src = (sdn_b[m] if shared else edn_b[e, m])
        xw = []
        if i == 0 and m == 0:
            xw.append(rec["cast_sdn" if shared else "cast_edn%d" % e])
        gcol, buf = load_col("plain", src[:, :], xw, width=w)
        col_wait(gcol)
        if not first_wait_done:
            p.wait("tensor", vS, rec[("adone", i, e)])
            first_wait_done = True
        dbank = ps[5] if m % 2 == 0 else ps[6]
        p.wait2("tensor", bank_free.get(id(dbank)))
        if shared:
            rhs = lambda k: abf[:, k * TT:(k + 1) * TT]
        else:
            rhs = lambda k, e=e: abf[:, (e % 2) * NIC * TT + k * TT:
                                     (e % 2) * NIC * TT + (k + 1) * TT]
        vd = mm_group(dbank, TT,
                      lambda k, b=buf: b[:, 128 * k:128 * (k + 1)], rhs, kk)
        col_done(gcol, vd)
        p.wait("vector", tS, vd)
        if e == 0:
            ga = rec.get(("gsil_done", m))
            if ga is not None:
                p.wait("vector", aS, ga)
            p.op("vector", lambda E, m=m, db=dbank: E.tensor_copy(
                bigB[:, m * TT:(m + 1) * TT], db[:, :]),
                 writes=("acc%d" % m,))
        else:
            p.op("vector", lambda E, m=m, db=dbank: E.tensor_tensor(
                bigB[:, m * TT:(m + 1) * TT], db[:, :],
                bigB[:, m * TT:(m + 1) * TT], OP.add),
                 reads=("acc%d" % m,), writes=("acc%d" % m,))
        bank_free[id(dbank)] = (vS, p.mile("vector", vS))
        rec[("dacc", e, m)] = bank_free[id(dbank)][1]



def kernel(**inputs):
    x = np.asarray(inputs["L0_attn_hidden_states"], np.float32)
    B, T, _ = x.shape
    t_loc = T // 2
    cap = 1280
    shared = _stage_shared_inputs(inputs)
    shared["tids"] = np.ascontiguousarray(
        np.arange(t_loc, dtype=np.uint32).reshape(t_loc // 128, 128).T)
    shared["idxinit"] = np.full((NE, cap, 1), 0x3FFFFFFF, np.uint32)
    shared["ecap"] = (np.arange(16, dtype=np.float32) * cap).reshape(16, 1)
    nc = build_program(t_loc, [[0, 1], [2, 3], [4, 5], [6, 7]], cap)
    in_maps = []
    for c in range(8):
        b, h = c // 2, c % 2
        m = dict(shared)
        m["x"] = np.ascontiguousarray(x[b, h * t_loc:(h + 1) * t_loc, :])
        m["xprev"] = (np.ascontiguousarray(x[b, t_loc - 1:t_loc, :]) if h == 1
                      else np.zeros((1, D), np.float32))
        m["isodd"] = np.full((128, 1), float(h), np.float32)
        in_maps.append(m)
    res = run_bass_kernel_spmd(nc, in_maps, core_ids=list(range(8)))
    out = np.empty((B, T, D), np.float32)
    for c in range(8):
        b, h = c // 2, c % 2
        out[b, h * t_loc:(h + 1) * t_loc, :] = np.asarray(res.results[c]["y"])
    return out


# version 10
# speedup vs baseline: 1.0501x; 1.0501x over previous
import numpy as np
from contextlib import ExitStack

import concourse.bass as bass
import concourse.mybir as mybir
from concourse.bass_utils import run_bass_kernel_spmd
from concourse.alu_op_type import AluOpType as OP

F32 = mybir.dt.float32
BF16 = mybir.dt.bfloat16
AF = mybir.ActivationFunctionType
AX = mybir.AxisListType.X

D = 2048
NF = 16
TT = 512
NTC = 4
NE = 16
NIC = 11
NSI = 22
EPS = 1e-5
NEGBIG = -1.0e30



def _cm(W):
    O, I = W.shape
    M, F = O // 128, I // 128
    W4 = W.reshape(M, 128, F, 128)
    return np.ascontiguousarray(W4.transpose(0, 3, 2, 1).reshape(M, 128, F * 128))


def _pairs(cm_grid):
    G = cm_grid.shape[0] // 2
    order = []
    for j in range(G):
        order += [j, G + j]
    return np.ascontiguousarray(cm_grid[order].reshape(G, 2, 128, cm_grid.shape[2]))


def _colT(v):
    return np.ascontiguousarray(v.reshape(16, 128).T.astype(np.float32))


def _stage_shared_inputs(inputs):
    f32 = np.float32
    g = {}
    g["wk_cm"] = _cm(np.asarray(inputs["L0_attn_Wk"], f32))
    g["wv_cm"] = _cm(np.asarray(inputs["L0_attn_Wv"], f32))
    g["wr_cm"] = _cm(np.asarray(inputs["L0_attn_Wr"], f32))
    g["wo_cm"] = _cm(np.asarray(inputs["L0_attn_Wo"], f32))
    eup = np.asarray(inputs["L0_ffn_expert_up_w"], f32)
    edn = np.asarray(inputs["L0_ffn_expert_down_w"], f32)
    g["eup_cm"] = np.ascontiguousarray(
        np.stack([_pairs(_cm(eup[e])) for e in range(NE)]))
    g["edn_cm"] = np.ascontiguousarray(
        np.stack([_cm(edn[e]) for e in range(NE)]))
    g["sup_cm"] = _pairs(_cm(np.asarray(inputs["L0_ffn_shared_up_w"], f32)))
    g["sdn_cm"] = _cm(np.asarray(inputs["L0_ffn_shared_down_w"], f32))
    g["gwt"] = np.ascontiguousarray(np.asarray(inputs["L0_ffn_gate_w"], f32).T)
    g["tdT"] = _colT(np.asarray(inputs["L0_attn_time_decay"], f32).reshape(-1))
    g["tfT"] = _colT(np.asarray(inputs["L0_attn_time_first"], f32).reshape(-1))
    g["tmk"] = _colT(np.asarray(inputs["L0_attn_time_mix_k"], f32).reshape(-1))
    g["tmv"] = _colT(np.asarray(inputs["L0_attn_time_mix_v"], f32).reshape(-1))
    g["tmr"] = _colT(np.asarray(inputs["L0_attn_time_mix_r"], f32).reshape(-1))
    g["ident"] = np.eye(128, dtype=f32)
    return g



class Sem:
    __slots__ = ("h", "cnt", "name")

    def __init__(self, h, name):
        self.h = h
        self.cnt = 0
        self.name = name


class Prog:
    ENGS = ("sync", "scalar", "vector", "tensor", "gpsimd")

    def __init__(self):
        self.st = {e: [] for e in self.ENGS}
        self.wr = {e: set() for e in self.ENGS}
        self.rd = {e: set() for e in self.ENGS}

    def wait(self, eng, sem, val):
        if val is None or val <= 0:
            return
        self.st[eng].append(lambda E, h=sem.h, v=val: E.wait_ge(h, v))

    def wait2(self, eng, sv):
        if sv is None:
            return
        self.wait(eng, sv[0], sv[1])

    def raw(self, eng, fn):
        self.st[eng].append(fn)

    def op(self, eng, fn, reads=(), writes=(), inc=None, inc_n=1):
        w = self.wr[eng]
        r = self.rd[eng]
        if eng in ("vector", "scalar", "gpsimd"):
            if (any(x in w for x in reads) or any(x in w for x in writes)
                    or any(x in r for x in writes)):
                self.st[eng].append(lambda E: E.drain())
                w.clear()
                r.clear()
        if inc is not None:
            inc.cnt += inc_n
            self.st[eng].append(
                lambda E, f=fn, h=inc.h, n=inc_n: f(E).then_inc(h, n))
            tgt = inc.cnt
        else:
            self.st[eng].append(fn)
            tgt = None
        w.update(writes)
        r.update(reads)
        return tgt

    def mile(self, eng, sem, n=1):
        assert eng != "tensor"
        self.wr[eng].clear()
        self.rd[eng].clear()
        sem.cnt += n
        self.st[eng].append(lambda E, h=sem.h, n2=n: E.drain().then_inc(h, n2))
        return sem.cnt


class DmaStream:

    def __init__(self, sems):
        self.sems = sems
        self.i = 0

    def issue(self, p, eng, fn, serialize=False):
        s = self.sems[self.i % len(self.sems)]
        self.i += 1
        if serialize and s.cnt > 0:
            p.wait(eng, s, s.cnt)
        tgt = p.op(eng, fn, inc=s, inc_n=16)
        return (s, tgt)


def _mk_runner(ops):
    def _run(E):
        for f in ops:
            f(E)
    return _run



def build_program(t_loc, replica_groups, cap):
    ntile = t_loc // TT
    ngrp = cap // 128
    nts_list = []
    _r = cap
    while _r > 0:
        nts_list.append(min(512, _r))
        _r -= min(512, _r)
    nc = bass.Bass("TRN2", target_bir_lowering=False)

    x_d = nc.dram_tensor("x", [t_loc, D], F32, kind="ExternalInput")
    xprev_d = nc.dram_tensor("xprev", [1, D], F32, kind="ExternalInput")
    isodd_d = nc.dram_tensor("isodd", [128, 1], F32, kind="ExternalInput")
    tdT_d = nc.dram_tensor("tdT", [128, 16], F32, kind="ExternalInput")
    tfT_d = nc.dram_tensor("tfT", [128, 16], F32, kind="ExternalInput")
    tmk_d = nc.dram_tensor("tmk", [128, 16], F32, kind="ExternalInput")
    tmv_d = nc.dram_tensor("tmv", [128, 16], F32, kind="ExternalInput")
    tmr_d = nc.dram_tensor("tmr", [128, 16], F32, kind="ExternalInput")
    ident_d = nc.dram_tensor("ident", [128, 128], F32, kind="ExternalInput")
    gwt_d = nc.dram_tensor("gwt", [D, 16], F32, kind="ExternalInput")
    tids_d = nc.dram_tensor("tids", [128, t_loc // 128], mybir.dt.uint32, kind="ExternalInput")
    idxinit_d = nc.dram_tensor("idxinit", [NE, cap, 1], mybir.dt.uint32, kind="ExternalInput")
    ecap_d = nc.dram_tensor("ecap", [16, 1], F32, kind="ExternalInput")
    wk_d = nc.dram_tensor("wk_cm", [16, 128, 2048], F32, kind="ExternalInput")
    wv_d = nc.dram_tensor("wv_cm", [16, 128, 2048], F32, kind="ExternalInput")
    wr_d = nc.dram_tensor("wr_cm", [16, 128, 2048], F32, kind="ExternalInput")
    wo_d = nc.dram_tensor("wo_cm", [16, 128, 2048], F32, kind="ExternalInput")
    eup_d = nc.dram_tensor("eup_cm", [NE, NIC, 2, 128, 2048], F32, kind="ExternalInput")
    edn_d = nc.dram_tensor("edn_cm", [NE, 16, 128, 1408], F32, kind="ExternalInput")
    sup_d = nc.dram_tensor("sup_cm", [NSI, 2, 128, 2048], F32, kind="ExternalInput")
    sdn_d = nc.dram_tensor("sdn_cm", [16, 128, 2816], F32, kind="ExternalInput")
    y_d = nc.dram_tensor("y", [t_loc, D], F32, kind="ExternalOutput")

    wk_b = nc.dram_tensor("wk_b", [16, 128, 2048], BF16)
    wv_b = nc.dram_tensor("wv_b", [16, 128, 2048], BF16)
    wr_b = nc.dram_tensor("wr_b", [16, 128, 2048], BF16)
    wo_b = nc.dram_tensor("wo_b", [16, 128, 2048], BF16)
    eup_b = nc.dram_tensor("eup_b", [NE, NIC, 2, 128, 2048], BF16)
    edn_b = nc.dram_tensor("edn_b", [NE, 16, 128, 1408], BF16)
    sup_b = nc.dram_tensor("sup_b", [NSI, 2, 128, 2048], BF16)
    sdn_b = nc.dram_tensor("sdn_b", [16, 128, 2816], BF16)
    kv_scr = nc.dram_tensor("kv_scr", [D, t_loc], BF16)
    r_scr = nc.dram_tensor("r_scr", [D, t_loc], BF16)
    xn_scr = nc.dram_tensor("xn_scr", [t_loc, D], BF16)
    s_scr = nc.dram_tensor("s_scr", [t_loc, D], BF16)
    facc_scr = nc.dram_tensor("facc_scr", [t_loc, D], F32)
    idx_scr = nc.dram_tensor("idx_scr", [NE, cap, 1], mybir.dt.uint32)
    cwc_scr = nc.dram_tensor("cwc_scr", [NE, cap, 1], F32)
    cx_scr = nc.dram_tensor("cx_scr", [NE, cap, D], BF16)
    cc_in = nc.dram_tensor("cc_in", [128, 16], F32)
    cc_out = nc.dram_tensor("cc_out", [128 * 2, 16], F32)

    with ExitStack() as ctx:
        sb = lambda nm, sh, dt: ctx.enter_context(nc.sbuf_tensor(nm, sh, dt))
        mkpsum = lambda nm: ctx.enter_context(nc.psum_tensor(nm, [128, 512], F32))

        xtm = sb("xtm", [128, NTC * D], F32)
        bigB = sb("bigB", [128, NF * TT], F32)
        bigC = sb("bigC", [128, NF * TT], F32)
        bigCb = bigC.bitcast(BF16)
        mixA = sb("mixA", [128, NF * TT], BF16)
        mixB = sb("mixB", [128, NF * TT], BF16)
        abf = sb("abf", [128, 2 * NIC * TT], BF16)
        wb = [sb("wb0", [128, 4096], BF16), sb("wb1", [128, 4096], BF16)]
        xntma = sb("xntma", [128, D], F32)
        xntmb = xntma.bitcast(BF16)[:, 0:D]
        st = sb("st", [128, TT + 1], F32)
        sgt = sb("sgt", [128, 2 * 512], F32)
        usc = sb("usc", [128, 512], F32)
        bcs = sb("bcs", [128, 2 * 512], F32)
        gsil = sb("gsil", [128, 2 * 512], F32)
        kbuf = gsil
        mbf = sgt.bitcast(BF16)[:, 0:1024]
        rows = sb("rows", [128, 16], F32)
        gat = sb("gat", [128, 128], F32)
        cwt = sb("cwt", [128, 2 * NTC * 16], F32)
        lgb = sb("lgb", [128, 2 * NTC * 16], F32)
        cw_fm = sb("cw_fm", [16, TT], F32)
        gfm = sb("gfm", [16, 3 * TT], F32)
        ebase = sb("ebase", [16, 2], F32)
        onecol = sb("onecol", [128, 1], F32)
        onesT = sb("onesT", [33, 128], F32)
        offsu = sb("offsu", [128, 256], mybir.dt.uint32)
        idxb = sb("idxb", [128, 2 * ngrp], mybir.dt.uint32)
        cc = sb("cc", [128, 160], F32)
        ident = sb("identt", [128, 128], F32)
        identb = sb("identtb", [128, 128], BF16)
        gw_sb = sb("gw_sb", [128, NF * 16], F32)
        tids_sb = sb("tids_sb", [128, t_loc // 128], mybir.dt.uint32)
        sg_in = sb("sg_in", [128, 16], F32)

        xprev_sb = bigB[0:1, 0:D]
        xnprev = bigB[0:1, D:2 * D]
        cwc_sb = [bigC[0:1, 0:cap], bigC[32:33, 0:cap]]
        ps = [mkpsum("ps%d" % i) for i in range(8)]

        mksem = lambda nm: Sem(ctx.enter_context(nc.semaphore(nm)), nm)
        cS = mksem("cS")
        tS = mksem("tS")
        vS = mksem("vS")
        aS = mksem("aS")
        SETUP = DmaStream([mksem("dSet")])
        WGT = DmaStream([mksem("dW%d" % k) for k in range(4)])
        XL = DmaStream([mksem("dXL")])
        KVL = DmaStream([mksem("dKVL")])
        RL = DmaStream([mksem("dRL")])
        CWL = DmaStream([mksem("dCWL%d" % k) for k in range(2)])
        CWST = DmaStream([mksem("dCWS")])
        KVST = DmaStream([mksem("dKS%d" % k) for k in range(2)])
        RST = DmaStream([mksem("dRS%d" % k) for k in range(2)])
        YST = DmaStream([mksem("dYS")])
        CAST = DmaStream([mksem("dCast")])
        GPD = DmaStream([mksem("dGP")])
        SCT = DmaStream([mksem("dSCT")])
        GTH = DmaStream([mksem("dGT%d" % k) for k in range(2)])
        CXS = DmaStream([mksem("dCXS%d" % k) for k in range(2)])
        SCA = DmaStream([mksem("dSCA")])
        TRLp = [DmaStream([mksem("dTRL%d_%d" % (pp, k)) for k in range(4)])
                for pp in range(2)]
        FST = DmaStream([mksem("dFST")])
        SST = DmaStream([mksem("dSST")])
        XNST = DmaStream([mksem("dXNS")])

        WC, UC, CK, CV, CR, SC, PC = 0, 16, 32, 48, 64, 80, 96
        IO = 112
        ET = 128
        R_SUM, R_SSQ, R_MEAN, R_MSQ, R_VAR, R_SQV, R_RSTD = 0, 1, 2, 3, 4, 5, 6
        G_LG, G_ES, G_SCO, G_MASK, G_CUR = 0, 16, 32, 48, 64
        G_GMAX, G_M4, G_GSEL, G_MK1 = 80, 84, 88, 92
        G_SMAX, G_NM, G_SSUM, G_RS, G_M1, G_M2, G_G2A, G_SW, G_RSW = (
            96, 97, 98, 99, 100, 101, 102, 104, 108)

        p = Prog()
        rec = {}
        bank_free = {}

        bndreg = nc.gpsimd.alloc_register("bndreg")

        def emit_casts(cast_list):
            for nm, csrc, cdst in cast_list:
                if len(csrc.shape) == 3:
                    s2 = csrc.rearrange("a b c -> (a b) c")
                    d2 = cdst.rearrange("a b c -> (a b) c")
                else:
                    s2 = csrc.rearrange("a b c d -> (a b c) d")
                    d2 = cdst.rearrange("a b c d -> (a b c) d")
                rec["cast_" + nm] = CAST.issue(
                    p, "gpsimd", lambda E, a=d2, b=s2: E.dma_start(a[:, :], b[:, :]),
                    serialize=True)

        rec["idxinit"] = CAST.issue(
            p, "gpsimd", lambda E: E.dma_start(
                idx_scr.rearrange("e s o -> (e s) o")[:, :],
                idxinit_d.rearrange("e s o -> (e s) o")[:, :]))
        p.raw("gpsimd", lambda E: E.reg_mov(bndreg, NE * cap - 1))
        emit_casts([("wk", wk_d, wk_b), ("wv", wv_d, wv_b), ("wr", wr_d, wr_b),
                    ("sup", sup_d, sup_b), ("sdn", sdn_d, sdn_b), ("wo", wo_d, wo_b)])

        def sdma(fn):
            return SETUP.issue(p, "sync", fn)

        sdma(lambda E: E.dma_start(cc[:, WC:WC + 16], tdT_d[:, :]))
        sdma(lambda E: E.dma_start(cc[:, UC:UC + 16], tfT_d[:, :]))
        sdma(lambda E: E.dma_start(cc[:, CK:CK + 16], tmk_d[:, :]))
        sdma(lambda E: E.dma_start(cc[:, CV:CV + 16], tmv_d[:, :]))
        sdma(lambda E: E.dma_start(cc[:, CR:CR + 16], tmr_d[:, :]))
        sdma(lambda E: E.dma_start(cc[:, IO:IO + 1], isodd_d[:, :]))
        sdma(lambda E: E.dma_start(ident[:, :], ident_d[:, :]))
        sdma(lambda E: E.dma_start(tids_sb[:, :], tids_d[:, :]))
        sdma(lambda E: E.dma_start(ebase[0:16, 1:2], ecap_d[:, :]))
        sdma(lambda E: E.dma_start(
            gw_sb[:, :].rearrange("p (f e) -> p f e", f=NF),
            gwt_d.rearrange("(f p) e -> p f e", p=128)[:, :, :]))
        sdma(lambda E: E.dma_start(xprev_sb, xprev_d[:, :]))
        rec["setup_dma"] = (SETUP.sems[0], SETUP.sems[0].cnt)

        p.wait2("scalar", rec["setup_dma"])
        p.op("scalar", lambda E: E.activation(cc[:, ET:ET + 16], cc[:, WC:WC + 16], AF.Exp),
             reads=("cc",), writes=("cc",))
        rec["wc1"] = p.mile("scalar", aS)
        p.wait("vector", aS, rec["wc1"])
        p.op("vector", lambda E: E.tensor_scalar(
            cc[:, ET:ET + 16], cc[:, ET:ET + 16], -1.0, None, OP.mult),
             reads=("cc",), writes=("cc",))
        rec["wc2"] = p.mile("vector", vS)
        p.wait("scalar", vS, rec["wc2"])
        p.op("scalar", lambda E: E.activation(cc[:, WC:WC + 16], cc[:, ET:ET + 16], AF.Exp),
             reads=("cc",), writes=("cc",))
        rec["wcols"] = p.mile("scalar", aS)

        p.wait2("vector", rec["setup_dma"])
        for off in (CK, CV, CR):
            p.op("vector", lambda E, o=off: E.tensor_scalar(
                cc[:, o:o + 16], cc[:, o:o + 16], -1.0, 1.0, OP.mult, OP.add),
                 reads=("cc",), writes=("cc",))
        p.op("vector", lambda E: E.tensor_copy(identb[:, :], ident[:, :]),
             writes=("identb",))
        p.op("vector", lambda E: E.memset(cc[:, SC:SC + 16], 0.0),
             reads=("cc",), writes=("cc",))
        p.op("vector", lambda E: E.memset(onecol[:, :], 1.0), writes=("onecol",))
        p.op("vector", lambda E: E.memset(onesT[:, :], 1.0), writes=("onesT",))
        p.op("vector", lambda E: E.memset(ebase[:, 0:1], 0.0), writes=("ebase",))
        p.op("vector", lambda E: E.reduce_sum(rows[0:1, 8:9], xprev_sb, axis=AX),
             writes=("prow",))
        rec["prow_sum"] = p.mile("vector", vS)
        p.wait("scalar", vS, rec["prow_sum"])
        p.op("scalar", lambda E: E.activation(xnprev, xprev_sb, AF.Square,
                                              accum_out=rows[0:1, 9:10]),
             writes=("xnprev", "prow"))
        rec["prow_ssq"] = p.mile("scalar", aS)
        p.wait("vector", aS, rec["prow_ssq"])
        p.op("vector", lambda E: E.tensor_scalar(
            rows[0:1, 10:11], rows[0:1, 8:9], 1.0 / D, None, OP.mult),
             reads=("prow",), writes=("prow",))
        p.op("vector", lambda E: E.tensor_scalar(
            rows[0:1, 11:12], rows[0:1, 10:11], rows[0:1, 10:11], None, OP.mult),
             reads=("prow",), writes=("prow",))
        p.op("vector", lambda E: E.scalar_tensor_tensor(
            rows[0:1, 12:13], rows[0:1, 9:10], 1.0 / D, rows[0:1, 11:12], OP.mult, OP.subtract),
             reads=("prow",), writes=("prow",))
        p.op("vector", lambda E: E.tensor_scalar(
            rows[0:1, 12:13], rows[0:1, 12:13], EPS, None, OP.add),
             reads=("prow",), writes=("prow",))
        rec["prow_var"] = p.mile("vector", vS)
        p.wait("scalar", vS, rec["prow_var"])
        p.op("scalar", lambda E: E.activation(
            rows[0:1, 13:14], rows[0:1, 12:13], AF.Sqrt),
             reads=("prow",), writes=("prow",))
        rec["prow_sqv"] = p.mile("scalar", aS)
        p.wait("vector", aS, rec["prow_sqv"])
        p.op("vector", lambda E: E.reciprocal(rows[0:1, 14:15], rows[0:1, 13:14]),
             reads=("prow",), writes=("prow",))
        p.op("vector", lambda E: E.tensor_scalar(
            xnprev, xprev_sb, rows[0:1, 10:11], rows[0:1, 14:15],
            OP.subtract, OP.mult),
             reads=("prow",), writes=("xnprev",))
        rec["xnprev"] = p.mile("vector", vS)

        p.wait2("tensor", rec["setup_dma"])
        p.wait("tensor", vS, rec["xnprev"])
        for k in range(NF):
            last = (k == NF - 1)
            fn = lambda E, k=k, last=last: E.matmul(
                ps[0][:, k:k + 1], bigB[0:1, D + 128 * k:D + 128 * (k + 1)],
                ident[0:1, 0:1], start=(k == 0), stop=last)
            if last:
                rec["pcolT"] = p.op("tensor", fn, inc=tS)
            else:
                p.raw("tensor", fn)
        p.wait("vector", tS, rec["pcolT"])
        p.op("vector", lambda E: E.tensor_copy(cc[:, PC:PC + 16], ps[0][:, 0:16]),
             reads=("cc",), writes=("cc",))
        bank_free[id(ps[0])] = (vS, p.mile("vector", vS))

        wload_i = [0]
        wload_rec = {}
        slot_cons = {}
        slot_cur = [0]

        def load_col(kind, src_ap, extra_waits=(), width=2048):
            nhalf = 1 if width <= 2048 else 2
            if nhalf == 2 and slot_cur[0] % 2 == 1:
                slot_cur[0] += 1
            s0 = slot_cur[0] % 4
            slots = [(slot_cur[0] + k) % 4 for k in range(nhalf)]
            slot_cur[0] += nhalf
            g = wload_i[0]
            wload_i[0] += 1
            wbt = wb[s0 // 2]
            off = (s0 % 2) * 2048
            for s in slots:
                cv = slot_cons.get(s)
                if cv is not None:
                    p.wait("sync", tS, cv)
            for sv in extra_waits:
                p.wait2("sync", sv)
            if kind == "pair":
                dst = bass.AP(wbt, 0, [[4096, 128], [2048, 2], [1, 2048]])
            else:
                dst = wbt[:, off:off + width]
            wload_rec[g] = WGT.issue(
                p, "sync", lambda E, a=dst, b=src_ap: E.dma_start(a, b))
            region = wbt[:, off:off + (4096 if nhalf == 2 else width)]
            return (g, slots), region

        def col_wait(gs):
            p.wait2("tensor", wload_rec[gs[0]])

        def col_done(gs, tsval):
            for s in gs[1]:
                slot_cons[s] = tsval

        def mm_group(bank, width, lhsT_fn, rhs_fn, kk):
            out = bank[:, 0:width]
            for k in range(kk):
                last = (k == kk - 1)
                fn = lambda E, k=k, last=last, o=out: E.matmul(
                    o, lhsT_fn(k), rhs_fn(k), start=(k == 0), stop=last)
                if last:
                    return p.op("tensor", fn, inc=tS)
                p.raw("tensor", fn)

        def emit_ln(tc, out_tile, out_name, tag, wait_ts=None):
            if wait_ts is not None:
                p.wait("vector", tS, wait_ts)
                p.wait("scalar", tS, wait_ts)
            p.op("vector", lambda E, tc=tc: E.reduce_sum(
                rows[:, R_SUM:R_SUM + 1], xtm[:, tc * D:(tc + 1) * D], axis=AX),
                 reads=("xtm",), writes=("rows",))
            rec[("lnsum", tag)] = p.mile("vector", vS)
            p.op("scalar", lambda E, tc=tc: E.activation(
                xntma[:, :], xtm[:, tc * D:(tc + 1) * D], AF.Square,
                accum_out=rows[:, R_SSQ:R_SSQ + 1]),
                 reads=("xtm",), writes=("xntma", "rows"))
            rec[("lnssq", tag)] = p.mile("scalar", aS)
            p.wait("vector", aS, rec[("lnssq", tag)])
            p.op("vector", lambda E: E.tensor_scalar(
                rows[:, R_MEAN:R_MEAN + 1], rows[:, R_SUM:R_SUM + 1],
                1.0 / D, None, OP.mult),
                 reads=("rows",), writes=("rows",))
            p.op("vector", lambda E: E.tensor_scalar(
                rows[:, R_MSQ:R_MSQ + 1], rows[:, R_MEAN:R_MEAN + 1],
                rows[:, R_MEAN:R_MEAN + 1], None, OP.mult),
                 reads=("rows",), writes=("rows",))
            p.op("vector", lambda E: E.scalar_tensor_tensor(
                rows[:, R_VAR:R_VAR + 1], rows[:, R_SSQ:R_SSQ + 1], 1.0 / D,
                rows[:, R_MSQ:R_MSQ + 1], OP.mult, OP.subtract),
                 reads=("rows",), writes=("rows",))
            p.op("vector", lambda E: E.tensor_scalar(
                rows[:, R_VAR:R_VAR + 1], rows[:, R_VAR:R_VAR + 1], EPS, None, OP.add),
                 reads=("rows",), writes=("rows",))
            rec[("lnvar", tag)] = p.mile("vector", vS)
            p.wait("scalar", vS, rec[("lnvar", tag)])
            p.op("scalar", lambda E: E.activation(
                rows[:, R_SQV:R_SQV + 1], rows[:, R_VAR:R_VAR + 1], AF.Sqrt),
                 reads=("rows",), writes=("rows",))
            rec[("lnsqv", tag)] = p.mile("scalar", aS)
            p.wait("vector", aS, rec[("lnsqv", tag)])
            p.op("vector", lambda E: E.reciprocal(
                rows[:, R_RSTD:R_RSTD + 1], rows[:, R_SQV:R_SQV + 1]),
                 reads=("rows",), writes=("rows",))
            p.op("vector", lambda E, tc=tc, o=out_tile: E.tensor_scalar(
                o, xtm[:, tc * D:(tc + 1) * D], rows[:, R_MEAN:R_MEAN + 1],
                rows[:, R_RSTD:R_RSTD + 1], OP.subtract, OP.mult),
                 reads=("rows", "xtm"), writes=(out_name,))
            rec[("ln", tag)] = p.mile("vector", vS)

        def emit_transpose_tc(tc, src_tile, dst_tile, dst_name, idt, tag):
            for grp in range(4):
                bank = ps[0] if grp % 2 == 0 else ps[7]
                p.wait2("tensor", bank_free.get(id(bank)))
                for j in range(4):
                    f = grp * 4 + j
                    last = (j == 3)
                    fn = lambda E, f=f, j=j, b=bank, last=last: E.matmul(
                        b[:, 128 * j:128 * (j + 1)],
                        src_tile[:, 128 * f:128 * (f + 1)], idt[:, :],
                        start=(j == 0), stop=last)
                    if last:
                        v = p.op("tensor", fn, inc=tS)
                    else:
                        p.raw("tensor", fn)
                p.wait("vector", tS, v)
                f0 = grp * 4
                dst_ap = bass.AP(dst_tile, f0 * TT + tc * 128,
                                 [[NF * TT, 128], [TT, 4], [1, 128]])
                src_ap = bass.AP(bank, 0, [[512, 128], [128, 4], [1, 128]])
                p.op("vector", lambda E, d=dst_ap, s=src_ap: E.tensor_copy(d, s),
                     writes=(dst_name,))
                bank_free[id(bank)] = (vS, p.mile("vector", vS))
            rec[("T", tag)] = bank_free[id(ps[7])][1]
            rec[("Tts", tag)] = v

        def emit_gating(j):
            p.wait("vector", aS, rec[("gatelg", j)])
            scprev0 = rec.get(("scat", j - 2))
            if scprev0 is not None:
                p.wait2("vector", scprev0)
            for tc in range(NTC):
                p.op("vector", lambda E, tc=tc: E.reduce_max(
                    gat[:, G_SMAX:G_SMAX + 1], lgb[:, (j % 2) * 64 + tc * 16:(j % 2) * 64 + (tc + 1) * 16], axis=AX),
                     reads=("lgb",), writes=("gat_r",))
                p.op("vector", lambda E: E.tensor_scalar(
                    gat[:, G_NM:G_NM + 1], gat[:, G_SMAX:G_SMAX + 1], -1.0, None, OP.mult),
                     reads=("gat_r",), writes=("gat_r",))
                vnm = p.mile("vector", vS)
                p.wait("scalar", vS, vnm)
                p.op("scalar", lambda E, tc=tc: E.activation(
                    gat[:, G_ES:G_ES + 16], lgb[:, (j % 2) * 64 + tc * 16:(j % 2) * 64 + (tc + 1) * 16], AF.Exp,
                    bias=gat[:, G_NM:G_NM + 1], accum_out=gat[:, G_SSUM:G_SSUM + 1]),
                     reads=("lgb",), writes=("gat_es",))
                ves = p.mile("scalar", aS)
                p.wait("vector", aS, ves)
                p.op("vector", lambda E: E.reciprocal(
                    gat[:, G_RS:G_RS + 1], gat[:, G_SSUM:G_SSUM + 1]),
                     reads=("gat_es",), writes=("gat_r",))
                p.op("vector", lambda E: E.tensor_scalar(
                    gat[:, G_SCO:G_SCO + 16], gat[:, G_ES:G_ES + 16],
                    gat[:, G_RS:G_RS + 1], None, OP.mult),
                     reads=("gat_es", "gat_r"), writes=("gat_sco",))
                for g4 in range(4):
                    p.op("vector", lambda E, g4=g4: E.reduce_max(
                        gat[:, G_GMAX + g4:G_GMAX + g4 + 1],
                        gat[:, G_SCO + 4 * g4:G_SCO + 4 * (g4 + 1)], axis=AX),
                         reads=("gat_sco",), writes=("gat_gm",))
                p.op("vector", lambda E: E.reduce_max(
                    gat[:, G_M1:G_M1 + 1], gat[:, G_GMAX:G_GMAX + 4], axis=AX),
                     reads=("gat_gm",), writes=("gat_r",))
                p.op("vector", lambda E: E.tensor_scalar(
                    gat[:, G_MK1:G_MK1 + 4], gat[:, G_GMAX:G_GMAX + 4],
                    gat[:, G_M1:G_M1 + 1], None, OP.is_ge),
                     reads=("gat_gm", "gat_r"), writes=("gat_mk1",))
                p.op("vector", lambda E: E.scalar_tensor_tensor(
                    gat[:, G_G2A:G_G2A + 4], gat[:, G_MK1:G_MK1 + 4], NEGBIG,
                    gat[:, G_GMAX:G_GMAX + 4], OP.mult, OP.add),
                     reads=("gat_mk1", "gat_gm"), writes=("gat_g2",))
                p.op("vector", lambda E: E.reduce_max(
                    gat[:, G_M2:G_M2 + 1], gat[:, G_G2A:G_G2A + 4], axis=AX),
                     reads=("gat_g2",), writes=("gat_r",))
                p.op("vector", lambda E: E.tensor_scalar(
                    gat[:, G_GSEL:G_GSEL + 4], gat[:, G_GMAX:G_GMAX + 4],
                    gat[:, G_M2:G_M2 + 1], None, OP.is_ge),
                     reads=("gat_gm", "gat_r"), writes=("gat_gsel",))
                p.op("vector", lambda E: E.tensor_copy(
                    bass.AP(gat, G_MASK, [[128, 128], [4, 4], [1, 4]]),
                    bass.AP(gat, G_GSEL, [[128, 128], [1, 4], [0, 4]])),
                     reads=("gat_gsel",), writes=("gat_mask",))
                p.op("vector", lambda E: E.tensor_tensor(
                    gat[:, G_CUR:G_CUR + 16], gat[:, G_SCO:G_SCO + 16],
                    gat[:, G_MASK:G_MASK + 16], OP.mult),
                     reads=("gat_sco", "gat_mask"), writes=("gat_cur",))
                p.op("vector", lambda E: E.tensor_copy(
                    gat[:, G_MASK:G_MASK + 16], gat[:, G_CUR:G_CUR + 16]),
                     reads=("gat_cur",), writes=("gat_mask",))
                for tsel in range(4):
                    p.op("vector", lambda E, t2=tsel: E.reduce_max(
                        gat[:, G_M4 + t2:G_M4 + t2 + 1],
                        gat[:, G_CUR:G_CUR + 16], axis=AX),
                         reads=("gat_cur",), writes=("gat_m4",))
                    if tsel < 3:
                        p.op("vector", lambda E, t2=tsel: E.tensor_scalar(
                            gat[:, G_ES:G_ES + 16], gat[:, G_CUR:G_CUR + 16],
                            gat[:, G_M4 + t2:G_M4 + t2 + 1], None, OP.is_ge),
                             reads=("gat_cur", "gat_m4"), writes=("gat_es",))
                        p.op("vector", lambda E: E.scalar_tensor_tensor(
                            gat[:, G_CUR:G_CUR + 16], gat[:, G_ES:G_ES + 16], NEGBIG,
                            gat[:, G_CUR:G_CUR + 16], OP.mult, OP.add),
                             reads=("gat_es", "gat_cur"), writes=("gat_cur",))
                p.op("vector", lambda E: E.reduce_sum(
                    gat[:, G_SW:G_SW + 1], gat[:, G_M4:G_M4 + 4], axis=AX),
                     reads=("gat_m4",), writes=("gat_r",))
                p.op("vector", lambda E: E.tensor_scalar(
                    gat[:, G_SW + 1:G_SW + 2], gat[:, G_SW:G_SW + 1], 1e-20, None, OP.add),
                     reads=("gat_r",), writes=("gat_r",))
                p.op("vector", lambda E: E.reciprocal(
                    gat[:, G_RSW:G_RSW + 1], gat[:, G_SW + 1:G_SW + 2]),
                     reads=("gat_r",), writes=("gat_r",))
                p.op("vector", lambda E: E.tensor_scalar(
                    gat[:, G_ES:G_ES + 16], gat[:, G_MASK:G_MASK + 16],
                    gat[:, G_M4 + 3:G_M4 + 4], None, OP.is_ge),
                     reads=("gat_mask", "gat_m4"), writes=("gat_es",))
                p.op("vector", lambda E, tc=tc: E.scalar_tensor_tensor(
                    cwt[:, (j % 2) * 64 + tc * 16:(j % 2) * 64 + (tc + 1) * 16], gat[:, G_MASK:G_MASK + 16],
                    gat[:, G_RSW:G_RSW + 1], gat[:, G_ES:G_ES + 16],
                    OP.mult, OP.mult),
                     reads=("gat_mask", "gat_es", "gat_r"), writes=("cwt",))
            rec[("cwt", j)] = p.mile("vector", vS)

            MK, RK, OF = 0, TT, 2 * TT
            p.wait("tensor", vS, rec[("cwt", j)])
            p.wait2("tensor", bank_free.get(id(ps[0])))
            for tc in range(NTC):
                last = (tc == NTC - 1)
                fn = lambda E, tc=tc, last=last: E.matmul(
                    ps[0][0:16, tc * 128:(tc + 1) * 128],
                    cwt[:, (j % 2) * 64 + tc * 16:(j % 2) * 64 + (tc + 1) * 16], ident[:, :],
                    start=(tc == 0), stop=last)
                if last:
                    vcw = p.op("tensor", fn, inc=tS)
                else:
                    p.raw("tensor", fn)
            p.wait("vector", tS, vcw)
            p.op("vector", lambda E: E.tensor_copy(cw_fm[:, :], ps[0][0:16, 0:512]),
                 writes=("cw_fm",))
            bank_free[id(ps[0])] = (vS, p.mile("vector", vS))
            p.op("vector", lambda E: E.tensor_scalar(
                gfm[:, MK:MK + TT], cw_fm[:, :], 0.0, None, OP.is_gt),
                 reads=("cw_fm",), writes=("gfm",))
            p.op("vector", lambda E: E.tensor_tensor_scan(
                gfm[:, RK:RK + TT], bass.AP(onecol, 0, [[1, 16], [0, TT]]),
                gfm[:, MK:MK + TT], ebase[:, 0:1], OP.mult, OP.add),
                 reads=("gfm", "onecol", "ebase"), writes=("gfm",))
            p.op("vector", lambda E: E.tensor_copy(
                ebase[:, 0:1], gfm[:, RK + TT - 1:RK + TT]),
                 reads=("gfm",), writes=("ebase",))
            p.op("vector", lambda E: E.tensor_scalar(
                gfm[:, RK:RK + TT], gfm[:, RK:RK + TT], -1.0, None, OP.add),
                 reads=("gfm",), writes=("gfm",))
            p.op("vector", lambda E: E.tensor_scalar(
                gfm[:, OF:OF + TT], gfm[:, RK:RK + TT], 1.0 / 128.0, -0.496,
                OP.mult, OP.add),
                 reads=("gfm",), writes=("gfm",))
            p.op("vector", lambda E: E.tensor_scalar(
                gfm[:, OF:OF + TT], gfm[:, OF:OF + TT], 12582912.0, None, OP.add),
                 reads=("gfm",), writes=("gfm",))
            p.op("vector", lambda E: E.tensor_scalar(
                gfm[:, OF:OF + TT], gfm[:, OF:OF + TT], -12582912.0, None, OP.add),
                 reads=("gfm",), writes=("gfm",))
            p.op("vector", lambda E, ng=ngrp: E.tensor_scalar(
                cw_fm[:, :], gfm[:, RK:RK + TT], float(ng), None, OP.mult),
                 reads=("gfm",), writes=("cw_fm",))
            p.op("vector", lambda E, ng=ngrp: E.scalar_tensor_tensor(
                gfm[:, OF:OF + TT], gfm[:, OF:OF + TT], -float(128 * ng - 1),
                cw_fm[:, :], OP.mult, OP.add),
                 reads=("gfm", "cw_fm"), writes=("gfm",))
            p.op("vector", lambda E: E.tensor_tensor(
                gfm[:, OF:OF + TT], gfm[:, OF:OF + TT], gfm[:, MK:MK + TT], OP.mult),
                 reads=("gfm",), writes=("gfm",))
            p.op("vector", lambda E: E.tensor_tensor(
                gfm[:, RK:RK + TT], gfm[:, RK:RK + TT], gfm[:, MK:MK + TT], OP.mult),
                 reads=("gfm",), writes=("gfm",))
            p.op("vector", lambda E: E.tensor_scalar(
                gfm[:, OF:OF + TT], gfm[:, OF:OF + TT], ebase[:, 1:2], None, OP.add),
                 reads=("gfm", "ebase"), writes=("gfm",))
            p.op("vector", lambda E: E.tensor_scalar(
                gfm[:, RK:RK + TT], gfm[:, RK:RK + TT], ebase[:, 1:2], None, OP.add),
                 reads=("gfm", "ebase"), writes=("gfm",))
            p.op("vector", lambda E: E.tensor_scalar(
                gfm[:, MK:MK + TT], gfm[:, MK:MK + TT], -1.0, 1.0, OP.mult, OP.add),
                 reads=("gfm",), writes=("gfm",))
            p.op("vector", lambda E: E.scalar_tensor_tensor(
                gfm[:, OF:OF + TT], gfm[:, MK:MK + TT], 1.0e9, gfm[:, OF:OF + TT],
                OP.mult, OP.add),
                 reads=("gfm",), writes=("gfm",))
            p.op("vector", lambda E: E.scalar_tensor_tensor(
                gfm[:, RK:RK + TT], gfm[:, MK:MK + TT], 1.0e9, gfm[:, RK:RK + TT],
                OP.mult, OP.add),
                 reads=("gfm",), writes=("gfm",))
            rec[("offsfm", j)] = p.mile("vector", vS)
            p.wait("tensor", vS, rec[("offsfm", j)])
            p.wait2("tensor", bank_free.get(id(ps[7])))
            nT = 0
            for base, srcoff in ((0, OF), (64, RK)):
                for tc in range(NTC):
                    nT += 1
                    last = (nT == 8)
                    fn = lambda E, b2=base, so=srcoff, tc=tc, last=last, first=(nT == 1): \
                        E.matmul(ps[7][:, b2 + tc * 16:b2 + (tc + 1) * 16],
                                 gfm[0:16, so + tc * 128:so + (tc + 1) * 128],
                                 ident[0:16, 0:16], start=first, stop=last)
                    if last:
                        voff = p.op("tensor", fn, inc=tS)
                    else:
                        p.raw("tensor", fn)
            p.wait("vector", tS, voff)
            scprev = rec.get(("scat", j - 2))
            if scprev is not None:
                p.wait2("vector", scprev)
            p.op("vector", lambda E: E.tensor_copy(offsu[:, (j % 2) * 128:(j % 2) * 128 + 128], ps[7][:, 0:128]),
                 writes=("offsu",))
            bank_free[id(ps[7])] = (vS, p.mile("vector", vS))
            rec[("offsu", j)] = bank_free[id(ps[7])][1]
            p.wait("gpsimd", vS, rec[("offsu", j)])
            if j == 0:
                p.wait2("gpsimd", rec["idxinit"])
            for tc in range(NTC):
                g = j * NTC + tc
                for e in range(NE):
                    SCT.issue(p, "gpsimd", lambda E, e=e, tc=tc, g=g: E.indirect_dma_start(
                        idx_scr.rearrange("e s o -> (e s) o"),
                        bass.IndirectOffsetOnAxis(
                            ap=offsu[:, (j % 2) * 128 + tc * 16 + e:(j % 2) * 128 + tc * 16 + e + 1], axis=0),
                        tids_sb[:, g:g + 1], None,
                        bounds_check=bndreg, oob_is_err=False), serialize=True)
                    SCT.issue(p, "gpsimd", lambda E, e=e, tc=tc: E.indirect_dma_start(
                        cwc_scr.rearrange("e s o -> (e s) o"),
                        bass.IndirectOffsetOnAxis(
                            ap=offsu[:, (j % 2) * 128 + 64 + tc * 16 + e:(j % 2) * 128 + 64 + tc * 16 + e + 1], axis=0),
                        cwt[:, (j % 2) * 64 + tc * 16 + e:(j % 2) * 64 + tc * 16 + e + 1],
                        None,
                        bounds_check=bndreg, oob_is_err=False), serialize=True)
            rec[("scat", j)] = (SCT.sems[0], SCT.sems[0].cnt)

            rec[("lgused", j % 2)] = rec[("offsu", j)]

        ecl = []
        for e in range(NE):
            ecl.append(("eup%d" % e, eup_d[e], eup_b[e]))
            ecl.append(("edn%d" % e, edn_d[e], edn_b[e]))

        p.wait("vector", aS, rec["wcols"])
        for i in range(ntile):
            if i > 0:
                p.wait("sync", vS, rec[("ln", ("p1", i - 1, NTC - 1))])
                p.wait("sync", aS, rec[("lnssq", ("p1", i - 1, NTC - 1))])
            rec[("xload", i)] = XL.issue(
                p, "sync", lambda E, i=i: E.dma_start(
                    xtm[:, :].rearrange("p (c f) -> p c f", c=NTC),
                    x_d[i * TT:(i + 1) * TT, :].rearrange("(c p) f -> p c f", p=128)))
            p.wait2("vector", rec[("xload", i)])
            p.wait2("scalar", rec[("xload", i)])

            for tc in range(NTC):
                wts = rec.get(("Tts", ("p1", i, tc - 1))) if tc > 0 else \
                    (rec.get(("Tts", ("p1", i - 1, NTC - 1))) if i > 0 else None)
                emit_ln(tc, xntma[:, :], "xntma", ("p1", i, tc), wait_ts=wts)
                p.wait("tensor", vS, rec[("ln", ("p1", i, tc))])
                emit_transpose_tc(tc, xntma, bigB, "bigB", ident, ("p1", i, tc))

            p.wait("vector", vS, rec[("T", ("p1", i, NTC - 1))])
            if i > 0:
                p.wait("vector", tS, rec[("rmm", i - 1)])
            p.op("vector", lambda E: E.tensor_copy(
                cc[:, ET:ET + 16], cc[:, PC:PC + 16]),
                 reads=("cc",), writes=("cc",))
            def _dprime(f):
                p.op("vector", lambda E, f=f: E.tensor_tensor(
                    sgt[:, 1:TT], bigB[:, f * TT:f * TT + TT - 1],
                    bigB[:, f * TT + 1:f * TT + TT], OP.subtract),
                     reads=("bigB",), writes=("sgt",))
                p.op("vector", lambda E, f=f: E.tensor_tensor(
                    sgt[:, 0:1], cc[:, ET + f:ET + f + 1],
                    bigB[:, f * TT:f * TT + 1], OP.subtract),
                     reads=("bigB", "cc"), writes=("sgt",))
            for f in range(NF):
                _dprime(f)
                p.op("vector", lambda E, f=f: E.scalar_tensor_tensor(
                    mixA[:, f * TT:(f + 1) * TT], sgt[:, 0:TT],
                    cc[:, CK + f:CK + f + 1], bigB[:, f * TT:(f + 1) * TT],
                    OP.mult, OP.add),
                     reads=("sgt", "cc", "bigB"), writes=("mixA",))
            rec[("mixk", i)] = p.mile("vector", vS)
            for f in range(NF):
                _dprime(f)
                p.op("vector", lambda E, f=f: E.scalar_tensor_tensor(
                    mixB[:, f * TT:(f + 1) * TT], sgt[:, 0:TT],
                    cc[:, CV + f:CV + f + 1], bigB[:, f * TT:(f + 1) * TT],
                    OP.mult, OP.add),
                     reads=("sgt", "cc", "bigB"), writes=("mixB",))
                p.op("vector", lambda E, f=f: E.scalar_tensor_tensor(
                    bigCb[:, 8192 + f * TT:8192 + (f + 1) * TT], sgt[:, 0:TT],
                    cc[:, CR + f:CR + f + 1], bigB[:, f * TT:(f + 1) * TT],
                    OP.mult, OP.add),
                     reads=("sgt", "cc", "bigB"), writes=("bigC",))
                p.op("vector", lambda E, f=f: E.tensor_copy(
                    cc[:, PC + f:PC + f + 1], bigB[:, f * TT + TT - 1:f * TT + TT]),
                     reads=("bigB",), writes=("cc",))
            rec[("mixkv", i)] = p.mile("vector", vS)

            for m in range(NF):
                xw = [rec["cast_wk"]] if (i == 0 and m == 0) else []
                gk, bufk = load_col("plain", wk_b[m][:, :], xw, width=2048)
                xw = [rec["cast_wv"]] if (i == 0 and m == 0) else []
                gv, bufv = load_col("plain", wv_b[m][:, :], xw, width=2048)
                col_wait(gk)
                if m == 0:
                    p.wait("tensor", vS, rec[("mixk", i)])
                bkk = ps[1] if m % 2 == 0 else ps[3]
                bvv = ps[2] if m % 2 == 0 else ps[4]
                p.wait2("tensor", bank_free.get(id(bkk)))
                vk = mm_group(bkk, TT,
                              lambda k, b=bufk: b[:, 128 * k:128 * (k + 1)],
                              lambda k: mixA[:, k * TT:(k + 1) * TT], NF)
                col_done(gk, vk)
                col_wait(gv)
                if m == 0:
                    p.wait("tensor", vS, rec[("mixkv", i)])
                p.wait2("tensor", bank_free.get(id(bvv)))
                vv = mm_group(bvv, TT,
                              lambda k, b=bufv: b[:, 128 * k:128 * (k + 1)],
                              lambda k: mixB[:, k * TT:(k + 1) * TT], NF)
                col_done(gv, vv)
                rec[("kvmm_last", i)] = vv
                p.wait("scalar", tS, vk)
                kprev = rec.get(("kvtt", m % 2))
                if kprev is not None:
                    p.wait("scalar", vS, kprev)
                p.op("scalar", lambda E, m=m, bkk=bkk: E.copy(
                    kbuf[:, (m % 2) * 512:(m % 2) * 512 + 512], bkk[:, :]),
                     writes=("kbuf",))
                bank_free[id(bkk)] = (aS, p.mile("scalar", aS))
                gidx = m // 4
                gkey = i * 4 + gidx
                slot = (m % 4) * 512 + (gidx % 2) * 2048
                p.wait("vector", aS, bank_free[id(bkk)][1])
                p.wait("vector", tS, vv)
                dprev = rec.get(("kvdma_g", gkey - 2))
                if dprev is not None and m % 4 == 0:
                    p.wait2("vector", dprev)
                p.op("vector", lambda E, m=m, bvv=bvv, slot=slot: E.tensor_tensor(
                    bigCb[:, slot:slot + 512], bvv[:, :],
                    kbuf[:, (m % 2) * 512:(m % 2) * 512 + 512], OP.mult),
                     reads=("kbuf",), writes=("bigC",))
                kvtt = p.mile("vector", vS)
                rec[("kvtt", m % 2)] = kvtt
                bank_free[id(bvv)] = (vS, kvtt)
                p.op("vector", lambda E, m=m, slot=slot: E.tensor_tensor_scan(
                    st[:, 1:TT + 1],
                    bass.AP(cc, WC + m, [[160, 128], [0, TT]]),
                    bigCb[:, slot:slot + 512],
                    cc[:, SC + m:SC + m + 1], OP.mult, OP.add),
                     reads=("bigC", "cc"), writes=("st",))
                p.op("vector", lambda E, m=m: E.tensor_copy(
                    cc[:, SC + m:SC + m + 1], st[:, TT:TT + 1]),
                     reads=("st",), writes=("cc",))
                if m % 4 == 3:
                    vgrp = p.mile("vector", vS)
                    p.wait("scalar", vS, vgrp)
                    rec[("kvdma_g", gkey)] = KVST.issue(
                        p, "scalar", lambda E, i=i, gi=gidx: E.dma_start(
                            kv_scr.rearrange("(m p) T -> p m T", p=128)[
                                :, gi * 4:(gi + 1) * 4, i * TT:(i + 1) * TT],
                            bigCb[:, (gi % 2) * 2048:(gi % 2) * 2048 + 2048].rearrange(
                                "p (m t) -> p m t", m=4)))

            for m in range(NF):
                xw = [rec["cast_wr"]] if (i == 0 and m == 0) else []
                gr, bufr = load_col("plain", wr_b[m][:, :], xw, width=2048)
                col_wait(gr)
                brr = ps[5] if m % 2 == 0 else ps[6]
                p.wait2("tensor", bank_free.get(id(brr)))
                vr = mm_group(brr, TT,
                              lambda k, b=bufr: b[:, 128 * k:128 * (k + 1)],
                              lambda k: bigCb[:, 8192 + k * TT:8192 + (k + 1) * TT], NF)
                col_done(gr, vr)
                p.wait("scalar", tS, vr)
                gidx = m // 4
                gkey = i * 4 + gidx
                slot = 4096 + (m % 4) * 512 + (gidx % 2) * 2048
                dprev = rec.get(("rdma_g", gkey - 2))
                if dprev is not None and m % 4 == 0:
                    p.wait2("scalar", dprev)
                p.op("scalar", lambda E, slot=slot, brr=brr: E.activation(
                    bigCb[:, slot:slot + 512], brr[:, :], AF.Sigmoid),
                     writes=("bigC_r",))
                bank_free[id(brr)] = (aS, p.mile("scalar", aS))
                if m % 4 == 3:
                    rec[("rdma_g", gkey)] = RST.issue(
                        p, "scalar", lambda E, i=i, gi=gidx: E.dma_start(
                            r_scr.rearrange("(m p) T -> p m T", p=128)[
                                :, gi * 4:(gi + 1) * 4, i * TT:(i + 1) * TT],
                            bigCb[:, 4096 + (gi % 2) * 2048:
                                  4096 + (gi % 2) * 2048 + 2048].rearrange(
                                "p (m t) -> p m t", m=4)))
            p.wait2("tensor", bank_free.get(id(ps[7])))
            for tc in range(NTC):
                for k in range(NF):
                    last_mm = (tc == NTC - 1 and k == NF - 1)
                    fn = lambda E, tc=tc, k=k, last_mm=last_mm: E.matmul(
                        ps[7][:, tc * 16:(tc + 1) * 16],
                        bigB[:, k * TT + tc * 128:k * TT + (tc + 1) * 128],
                        gw_sb[:, k * 16:(k + 1) * 16],
                        start=(k == 0), stop=(k == NF - 1))
                    if last_mm:
                        vgt = p.op("tensor", fn, inc=tS)
                    else:
                        p.raw("tensor", fn)
            p.wait("scalar", tS, vgt)
            glprev = rec.get(("lgused", i % 2))
            if glprev is not None:
                p.wait("scalar", vS, glprev)
            p.op("scalar", lambda E, i=i: E.copy(
                lgb[:, (i % 2) * 64:(i % 2) * 64 + 64], ps[7][:, 0:64]),
                 writes=("lgb",))
            bank_free[id(ps[7])] = (aS, p.mile("scalar", aS))
            rec[("gatelg", i)] = bank_free[id(ps[7])][1]
            if i > 0:
                emit_gating(i - 1)
            emit_casts(ecl[(len(ecl) * i) // ntile:
                           (len(ecl) * (i + 1)) // ntile])

            rec[("rmm", i)] = vr

        emit_gating(ntile - 1)
        rec["p1_scan_done"] = p.mile("vector", vS)

        p.raw("gpsimd", lambda E: E.reg_mov(bndreg, t_loc - 1))

        p.wait("gpsimd", vS, rec["p1_scan_done"])
        rec["ccin"] = GPD.issue(p, "gpsimd",
                                lambda E: E.dma_start(cc_in[:, :], cc[:, SC:SC + 16]))
        p.wait2("gpsimd", rec["ccin"])
        rec["coll"] = p.op("gpsimd", lambda E: E.collective_compute(
            "AllGather", mybir.AluOpType.bypass, replica_groups=replica_groups,
            ins=[cc_in.ap().opt()], outs=[cc_out.ap().opt()]), inc=cS)
        p.wait("gpsimd", cS, rec["coll"])
        rec["sgin"] = GPD.issue(p, "gpsimd",
                                lambda E: E.dma_start(sg_in[:, :], cc_out[0:128, :]))
        p.wait2("vector", rec["sgin"])
        p.op("vector", lambda E: E.tensor_scalar(
            cc[:, SC:SC + 16], sg_in[:, :], cc[:, IO:IO + 1], None, OP.mult),
             reads=("cc",), writes=("cc",))
        rec["sinit"] = p.mile("vector", vS)

        p.wait("tensor", aS, aS.cnt)
        p.wait("tensor", vS, rec["sinit"])

        for i in range(ntile):
            wl = rec.get(("faccst", i - 1))
            if wl is not None:
                p.wait2("sync", wl)
            else:
                p.wait("sync", vS, rec[("ln", ("p1", ntile - 1, NTC - 1))])
                p.wait("sync", aS, rec[("lnssq", ("p1", ntile - 1, NTC - 1))])
            rec[("xload2", i)] = XL.issue(
                p, "sync", lambda E, i=i: E.dma_start(
                    xtm[:, :].rearrange("p (c f) -> p c f", c=NTC),
                    x_d[i * TT:(i + 1) * TT, :].rearrange("(c p) f -> p c f", p=128)))
            if i == 0:
                for gk in (ntile * 4 - 1, ntile * 4 - 2):
                    p.wait2("sync", rec[("kvdma_g", gk)])
                    p.wait2("sync", rec[("rdma_g", gk)])
            else:
                p.wait("sync", vS, rec[("wkvdone", i - 1)])
            rec[("kvload", i)] = KVL.issue(
                p, "sync", lambda E, i=i: E.dma_start(
                    bigCb[:, 0:8192].rearrange("p (m t) -> p m t", m=16),
                    kv_scr.rearrange("(m p) T -> p m T", p=128)[
                        :, 0:16, i * TT:(i + 1) * TT]))
            rec[("rload", i)] = RL.issue(
                p, "sync", lambda E, i=i: E.dma_start(
                    bigCb[:, 8192:16384].rearrange("p (m t) -> p m t", m=16),
                    r_scr.rearrange("(m p) T -> p m T", p=128)[
                        :, 0:16, i * TT:(i + 1) * TT]))

            p.wait2("vector", rec[("xload2", i)])
            p.wait2("scalar", rec[("xload2", i)])
            for tc in range(NTC):
                wts = rec.get(("Tts", ("p2", i, tc - 1))) if tc > 0 else \
                    (rec.get(("Tts", ("p2", i - 1, NTC - 1))) if i > 0 else
                     rec.get(("Tts", ("p1", ntile - 1, NTC - 1))))
                xprev_st = rec.get(("xnst", i, tc - 1)) if tc > 0 else \
                    rec.get(("xnst", i - 1, NTC - 1))
                if xprev_st is not None:
                    p.wait2("vector", xprev_st)
                    p.wait2("scalar", xprev_st)
                emit_ln(tc, xntmb, "xntma", ("p2", i, tc), wait_ts=wts)
                p.wait("scalar", vS, rec[("ln", ("p2", i, tc))])
                rec[("xnst", i, tc)] = XNST.issue(
                    p, "scalar", lambda E, i=i, tc=tc: E.dma_start(
                        xn_scr[i * TT + tc * 128:i * TT + (tc + 1) * 128, :],
                        xntmb))
                p.wait("tensor", vS, rec[("ln", ("p2", i, tc))])
                emit_transpose_tc(tc, xntmb, mixA, "mixA", identb, ("p2", i, tc))

            if i > 0:
                p.wait("vector", tS, rec[("smm", i - 1)])
            p.wait2("vector", rec[("kvload", i)])
            p.wait2("vector", rec[("rload", i)])
            for f in range(NF):
                p.op("vector", lambda E, f=f: E.tensor_copy(
                    st[:, 0:1], cc[:, SC + f:SC + f + 1]),
                     reads=("cc",), writes=("st",))
                p.op("vector", lambda E, f=f: E.tensor_tensor_scan(
                    st[:, 1:TT + 1],
                    bass.AP(cc, WC + f, [[160, 128], [0, TT]]),
                    bigCb[:, f * 512:(f + 1) * 512],
                    cc[:, SC + f:SC + f + 1], OP.mult, OP.add),
                     reads=("bigC", "cc"), writes=("st",))
                p.op("vector", lambda E, f=f: E.tensor_copy(
                    cc[:, SC + f:SC + f + 1], st[:, TT:TT + 1]),
                     reads=("st",), writes=("cc",))
                p.op("vector", lambda E, f=f: E.scalar_tensor_tensor(
                    usc[:, :], bigCb[:, f * 512:(f + 1) * 512],
                    cc[:, UC + f:UC + f + 1], st[:, 0:TT], OP.mult, OP.add),
                     reads=("bigC", "cc", "st"), writes=("usc",))
                p.op("vector", lambda E, f=f: E.tensor_tensor(
                    mixB[:, f * TT:(f + 1) * TT], usc[:, :],
                    bigCb[:, 8192 + f * 512:8192 + (f + 1) * 512], OP.mult),
                     reads=("usc", "bigC"), writes=("mixB",))
            rec[("wkvdone", i)] = p.mile("vector", vS)
            rec[("attn", i)] = rec[("wkvdone", i)]

            for j in range(NSI):
                xw = []
                if i == 0 and j == 0:
                    xw.append(rec["cast_sup"])
                gcol, buf = load_col("pair", sup_b[j].rearrange("g k x -> k g x")[:, :, :],
                                     xw, width=4096)
                col_wait(gcol)
                if j == 0:
                    p.wait("tensor", vS, rec[("T", ("p2", i, NTC - 1))])
                pg = ps[1] if j % 2 == 0 else ps[3]
                pu = ps[2] if j % 2 == 0 else ps[4]
                p.wait2("tensor", bank_free.get(id(pg)))
                p.wait2("tensor", bank_free.get(id(pu)))
                vg = mm_group(pg, TT,
                              lambda k, b=buf: b[:, 128 * k:128 * (k + 1)],
                              lambda k: mixA[:, k * TT:(k + 1) * TT], NF)
                vu = mm_group(pu, TT,
                              lambda k, b=buf: b[:, 2048 + 128 * k:2048 + 128 * (k + 1)],
                              lambda k: mixA[:, k * TT:(k + 1) * TT], NF)
                col_done(gcol, vu)
                p.wait("scalar", tS, vg)
                ju = rec.get(("jdone", j % 2))
                if ju is not None:
                    p.wait("scalar", vS, ju)
                p.op("scalar", lambda E, j=j, pg=pg: E.activation(
                    sgt[:, (j % 2) * 512:(j % 2) * 512 + 512], pg[:, :], AF.Silu),
                     writes=("sgt",))
                bank_free[id(pg)] = (aS, p.mile("scalar", aS))
                p.wait("vector", tS, vu)
                p.wait("vector", aS, bank_free[id(pg)][1])
                if j == 0:
                    sprev = rec.get(("sstore", i - 1))
                    if sprev is not None:
                        p.wait2("vector", sprev)
                p.op("vector", lambda E, pu=pu: E.tensor_copy(usc[:, :], pu[:, :]),
                     writes=("usc",))
                p.op("vector", lambda E, j=j: E.tensor_tensor(
                    abf[:, j * TT:(j + 1) * TT],
                    sgt[:, (j % 2) * 512:(j % 2) * 512 + 512], usc[:, :], OP.mult),
                     reads=("sgt", "usc"), writes=("abf",))
                jv = p.mile("vector", vS)
                rec[("jdone", j % 2)] = jv
                bank_free[id(pu)] = (vS, jv)
            rec[("adone", i)] = jv

            p.wait("tensor", vS, rec[("adone", i)])
            for m in range(16):
                xw = []
                if i == 0 and m == 0:
                    xw.append(rec["cast_sdn"])
                gcol, buf = load_col("plain", sdn_b[m][:, :], xw, width=2816)
                col_wait(gcol)
                dbank = ps[5] if m % 2 == 0 else ps[6]
                p.wait2("tensor", bank_free.get(id(dbank)))
                vd = mm_group(dbank, TT,
                              lambda k, b=buf: b[:, 128 * k:128 * (k + 1)],
                              lambda k: abf[:, k * TT:(k + 1) * TT], NSI)
                col_done(gcol, vd)
                p.wait("vector", tS, vd)
                p.op("vector", lambda E, m=m, db=dbank: E.tensor_copy(
                    bigB[:, m * TT:(m + 1) * TT], db[:, :]),
                     writes=("acc%d" % m,))
                bank_free[id(dbank)] = (vS, p.mile("vector", vS))
                rec[("sdacc", m)] = bank_free[id(dbank)][1]

            for m in range(NF):
                xw = [rec["cast_wo"]] if (i == 0 and m == 0) else []
                gcol, buf = load_col("plain", wo_b[m][:, :], xw, width=2048)
                col_wait(gcol)
                if m == 0:
                    p.wait("tensor", vS, rec[("attn", i)])
                sbank = ps[7] if m % 2 == 0 else ps[0]
                p.wait2("tensor", bank_free.get(id(sbank)))
                vsm = mm_group(sbank, TT,
                               lambda k, b=buf: b[:, 128 * k:128 * (k + 1)],
                               lambda k: mixB[:, k * TT:(k + 1) * TT], NF)
                col_done(gcol, vsm)
                mtprev = rec.get(("mbfT", m % 2))
                if mtprev is not None:
                    p.wait("vector", tS, mtprev)
                p.wait("vector", tS, vsm)
                p.op("vector", lambda E, m=m, sbank=sbank: E.tensor_copy(
                    mbf[:, (m % 2) * 512:(m % 2) * 512 + 512], sbank[:, :]),
                     writes=("mbf",))
                mv = p.mile("vector", vS)
                bank_free[id(sbank)] = (vS, mv)
                tbank = ps[5] if m % 2 == 0 else ps[6]
                p.wait("tensor", vS, mv)
                p.wait2("tensor", bank_free.get(id(tbank)))
                for tc in range(NTC):
                    last = (tc == NTC - 1)
                    fn = lambda E, m=m, tc=tc, tb=tbank, last=last: E.matmul(
                        tb[:, tc * 128:(tc + 1) * 128],
                        mbf[:, (m % 2) * 512 + tc * 128:(m % 2) * 512 + (tc + 1) * 128],
                        identb[:, :], start=(tc == 0), stop=last)
                    if last:
                        vtm = p.op("tensor", fn, inc=tS)
                    else:
                        p.raw("tensor", fn)
                rec[("mbfT", m % 2)] = vtm
                p.wait("vector", tS, vtm)
                sdst = bass.AP(abf, m * 128, [[2 * NIC * TT, 128], [2048, NTC], [1, 128]])
                psrc = bass.AP(tbank, 0, [[512, 128], [128, NTC], [1, 128]])
                p.op("vector", lambda E, d=sdst, s2=psrc: E.tensor_copy(d, s2),
                     writes=("abf",))
                bank_free[id(tbank)] = (vS, p.mile("vector", vS))
                rec[("scomb", m)] = bank_free[id(tbank)][1]
            rec[("smm", i)] = vsm
            p.wait("scalar", vS, rec[("scomb", NF - 1)])
            rec[("sstore", i)] = SST.issue(
                p, "scalar", lambda E, i=i: E.dma_start(
                    s_scr[i * TT:(i + 1) * TT, :].rearrange("(c p) f -> p c f", p=128),
                    abf[:, 0:8192].rearrange("p (c f) -> p c f", c=NTC)))

            for m in range(NF):
                tbank = ps[5] if m % 2 == 0 else ps[6]
                p.wait("tensor", vS, rec[("sdacc", m)])
                p.wait2("tensor", bank_free.get(id(tbank)))
                for tc in range(NTC):
                    last = (tc == NTC - 1)
                    fn = lambda E, m=m, tc=tc, tb=tbank, last=last: E.matmul(
                        tb[:, tc * 128:(tc + 1) * 128],
                        bigB[:, m * TT + tc * 128:m * TT + (tc + 1) * 128],
                        ident[:, :], start=(tc == 0), stop=last)
                    if last:
                        vft = p.op("tensor", fn, inc=tS)
                    else:
                        p.raw("tensor", fn)
                rec[("faccT", i)] = vft
                p.wait("vector", tS, vft)
                fdst = bass.AP(xtm, m * 128, [[NTC * D, 128], [D, NTC], [1, 128]])
                psrc = bass.AP(tbank, 0, [[512, 128], [128, NTC], [1, 128]])
                p.op("vector", lambda E, d=fdst, s2=psrc: E.tensor_copy(d, s2),
                     writes=("xtm",))
                bank_free[id(tbank)] = (vS, p.mile("vector", vS))
                rec[("fcomb", m)] = bank_free[id(tbank)][1]
            p.wait("scalar", vS, rec[("fcomb", NF - 1)])
            rec[("faccst", i)] = FST.issue(
                p, "scalar", lambda E, i=i: E.dma_start(
                    facc_scr[i * TT:(i + 1) * TT, :].rearrange("(c p) f -> p c f", p=128),
                    xtm[:, :].rearrange("p (c f) -> p c f", c=NTC)))

        gbufB = bigB.bitcast(BF16)
        nt0s = []
        _o = 0
        for _n in nts_list:
            nt0s.append(_o)
            _o += _n
        gg = [0]
        rec[("upmm_last2", -2)] = rec[("smm", ntile - 1)]
        rec[("upmm_last2", -1)] = rec[("smm", ntile - 1)]
        p.wait("gpsimd", tS, rec[("faccT", ntile - 1)])
        p.wait2("vector", rec[("sstore", ntile - 1)])
        p.wait2("scalar", rec[("sstore", ntile - 1)])
        p.wait2("vector", rec[("faccst", ntile - 1)])
        p.wait2("gpsimd", rec[("xnst", ntile - 1, NTC - 1)])
        for e in range(NE):
            if e == 0:
                p.wait2("sync", rec[("scat", ntile - 1)])
                p.wait("sync", vS, rec[("wkvdone", ntile - 1)])
            iw = rec.get(("idxused", e - 2))
            if iw is not None:
                p.wait("sync", tS, iw[0])
                p.wait2("sync", iw[1])
            rec[("idxl", e)] = CWL.issue(
                p, "sync", lambda E, e=e: E.dma_start(
                    idxb[:, (e % 2) * ngrp:(e % 2) * ngrp + ngrp],
                    idx_scr[e].rearrange("(p k) o -> p (k o)", p=128)))
            rec[("cwcl", e)] = CWL.issue(
                p, "sync", lambda E, e=e: E.dma_start(
                    cwc_sb[e % 2][:, :],
                    cwc_scr[e].rearrange("s o -> o s")))
            p.wait2("gpsimd", rec[("idxl", e)])
            for k in range(ngrp):
                g2 = gg[0]
                gg[0] += 1
                pv = rec.get(("cxst", g2 - 2))
                if pv is not None:
                    p.wait2("gpsimd", pv)
                gth = GTH.issue(
                    p, "gpsimd", lambda E, e=e, k=k, g2=g2: E.indirect_dma_start(
                        gbufB[:, (g2 % 2) * 2048:(g2 % 2) * 2048 + 2048], None,
                        xn_scr[:, :],
                        bass.IndirectOffsetOnAxis(
                            ap=idxb[:, (e % 2) * ngrp + k:(e % 2) * ngrp + k + 1], axis=0),
                        bounds_check=bndreg, oob_is_err=False))
                p.wait2("sync", gth)
                rec[("cxst", g2)] = CXS.issue(
                    p, "sync", lambda E, e=e, k=k, g2=g2: E.dma_start(
                        cx_scr[e, k * 128:(k + 1) * 128, :],
                        gbufB[:, (g2 % 2) * 2048:(g2 % 2) * 2048 + 2048]))
            rec[("cx_all", e)] = rec[("cxst", gg[0] - 1)]
            def emit_trl(nti2):
                nts2 = nts_list[nti2]
                nt02 = nt0s[nti2]
                gnt2 = e * len(nts_list) + nti2
                tw = rec.get(("upmm_last2", gnt2 - 2))
                lastg = (nt02 + nts2) // 128 - 1
                for f in range(NF):
                    if f == 0:
                        p.wait2("sync", rec[("cxst", gg[0] - ngrp + lastg)])
                        if tw is not None:
                            p.wait("sync", tS, tw)
                    if gnt2 % 2 == 0:
                        d = mixA[:, f * TT:f * TT + nts2]
                    else:
                        d = bigCb[:, 8192 + f * 512:8192 + f * 512 + nts2]
                    rec[("trl", e, nti2, f)] = TRLp[gnt2 % 2].issue(
                        p, "sync", lambda E, e=e, nt02=nt02, nts2=nts2, f=f, d=d:
                        E.dma_start(d, cx_scr[e, nt02:nt02 + nts2,
                                              128 * f:128 * (f + 1)],
                                    transpose=True))

            for nti, nts in enumerate(nts_list):
                nt0 = nt0s[nti]
                gnt = e * len(nts_list) + nti
                if nti == 0:
                    emit_trl(0)
                    if len(nts_list) > 1:
                        emit_trl(1)
                elif nti + 1 < len(nts_list):
                    emit_trl(nti + 1)
                p.wait2("tensor", bank_free.get(id(ps[7])))
                p.wait2("tensor", rec[("cwcl", e)])
                vbc = p.op("tensor", lambda E, e=e, nt0=nt0, nts=nts: E.matmul(
                    ps[7][:, 0:nts], onesT[(e % 2) * 32:(e % 2) * 32 + 1, :],
                    cwc_sb[e % 2][:, nt0:nt0 + nts],
                    start=True, stop=True), inc=tS)
                p.wait("scalar", tS, vbc)
                bprev = rec.get(("bcused", nti % 2))
                if bprev is not None:
                    p.wait("scalar", vS, bprev)
                p.op("scalar", lambda E, nti=nti, nts=nts: E.copy(
                    bcs[:, (nti % 2) * 512:(nti % 2) * 512 + nts], ps[7][:, 0:nts]),
                     writes=("bcs",))
                bank_free[id(ps[7])] = (aS, p.mile("scalar", aS))
                rec[("bcevac", e, nti)] = bank_free[id(ps[7])][1]
                for j in range(NIC):
                    xw = []
                    if nti == 0 and j == 0:
                        xw.append(rec["cast_eup%d" % e])
                    gcol, buf = load_col(
                        "pair", eup_b[e, j].rearrange("g k x -> k g x")[:, :, :],
                        xw, width=4096)
                    col_wait(gcol)
                    if j == 0:
                        for f in range(NF - 4, NF):
                            p.wait2("tensor", rec[("trl", e, nti, f)])
                    pg = ps[1] if j % 2 == 0 else ps[3]
                    pu = ps[2] if j % 2 == 0 else ps[4]
                    p.wait2("tensor", bank_free.get(id(pg)))
                    p.wait2("tensor", bank_free.get(id(pu)))
                    if gnt % 2 == 0:
                        rhsf = lambda k, nts=nts: mixA[:, k * TT:k * TT + nts]
                    else:
                        rhsf = lambda k, nts=nts: bigCb[:, 8192 + k * 512:
                                                        8192 + k * 512 + nts]
                    vg = mm_group(pg, nts,
                                  lambda k, b=buf: b[:, 128 * k:128 * (k + 1)],
                                  rhsf, NF)
                    vu = mm_group(pu, nts,
                                  lambda k, b=buf: b[:, 2048 + 128 * k:2048 + 128 * (k + 1)],
                                  rhsf, NF)
                    col_done(gcol, vu)
                    rec[("upmm_last2", gnt)] = vu
                    p.wait("scalar", tS, vg)
                    ju = rec.get(("jdone", j % 2))
                    if ju is not None:
                        p.wait("scalar", vS, ju)
                    p.op("scalar", lambda E, j=j, pg=pg, nts=nts: E.activation(
                        sgt[:, (j % 2) * 512:(j % 2) * 512 + nts], pg[:, 0:nts], AF.Silu),
                         writes=("sgt",))
                    bank_free[id(pg)] = (aS, p.mile("scalar", aS))
                    p.wait("vector", tS, vu)
                    p.wait("vector", aS, bank_free[id(pg)][1])
                    p.wait("vector", aS, rec[("bcevac", e, nti)])
                    p.op("vector", lambda E, pu=pu, nti=nti, nts=nts: E.tensor_tensor(
                        usc[:, 0:nts], pu[:, 0:nts],
                        bcs[:, (nti % 2) * 512:(nti % 2) * 512 + nts], OP.mult),
                         reads=("bcs",), writes=("usc",))
                    p.op("vector", lambda E, j=j, nti=nti, nts=nts: E.tensor_tensor(
                        abf[:, (nti % 2) * NIC * TT + j * TT:
                            (nti % 2) * NIC * TT + j * TT + nts],
                        sgt[:, (j % 2) * 512:(j % 2) * 512 + nts],
                        usc[:, 0:nts], OP.mult),
                         reads=("sgt", "usc"), writes=("abf",))
                    jv = p.mile("vector", vS)
                    rec[("jdone", j % 2)] = jv
                    bank_free[id(pu)] = (vS, jv)
                rec[("bcused", nti % 2)] = jv
                p.wait("tensor", vS, jv)
                for m in range(16):
                    xw = []
                    if nti == 0 and m == 0:
                        xw.append(rec["cast_edn%d" % e])
                    gcol, buf = load_col("plain", edn_b[e, m][:, :], xw, width=1408)
                    col_wait(gcol)
                    dbank = ps[5] if m % 2 == 0 else ps[6]
                    p.wait2("tensor", bank_free.get(id(dbank)))
                    vd = mm_group(dbank, nts,
                                  lambda k, b=buf: b[:, 128 * k:128 * (k + 1)],
                                  lambda k, nti=nti, nts=nts: abf[
                                      :, (nti % 2) * NIC * TT + k * TT:
                                      (nti % 2) * NIC * TT + k * TT + nts],
                                  NIC)
                    col_done(gcol, vd)
                    rec["dn_last"] = vd
                    p.wait("scalar", tS, vd)
                    dprev = rec.get(("dnTm", m))
                    if dprev is not None:
                        p.wait("scalar", tS, dprev)
                    p.op("scalar", lambda E, m=m, db=dbank, nts=nts: E.copy(
                        mixB[:, m * TT:m * TT + nts], db[:, 0:nts]),
                         writes=("mixB",))
                    bank_free[id(dbank)] = (aS, p.mile("scalar", aS))
                    rec[("dnstg", m)] = bank_free[id(dbank)][1]
                for sg in range(nts // 128):
                    saprev = rec.get(("sca", sg))
                    for mg in range(4):
                        tbank = ps[0] if mg % 2 == 0 else ps[7]
                        p.wait2("tensor", bank_free.get(id(tbank)))
                        if mg == 0 and sg == 0:
                            for m2 in range(16):
                                p.wait("tensor", aS, rec[("dnstg", m2)])
                        for mm in range(4):
                            m = mg * 4 + mm
                            last = (mm == 3)
                            fn = lambda E, m=m, mm=mm, sg=sg, tb=tbank, last=last: \
                                E.matmul(tb[:, mm * 128:(mm + 1) * 128],
                                         mixB[:, m * TT + sg * 128:m * TT + (sg + 1) * 128],
                                         identb[:, :], start=(mm == 0), stop=last)
                            if last:
                                vt2 = p.op("tensor", fn, inc=tS)
                            else:
                                p.raw("tensor", fn)
                        if sg == nts // 128 - 1:
                            for mm2 in range(4):
                                rec[("dnTm", mg * 4 + mm2)] = vt2
                        p.wait("vector", tS, vt2)
                        if saprev is not None and mg == 0:
                            p.wait2("vector", saprev)
                        p.op("vector", lambda E, sg=sg, mg=mg, tb=tbank: E.tensor_copy(
                            xtm[:, sg * 2048 + mg * 512:sg * 2048 + (mg + 1) * 512],
                            tb[:, :]),
                             writes=("xtm",))
                        bank_free[id(tbank)] = (vS, p.mile("vector", vS))
                    rec[("xstg", sg)] = bank_free[id(tbank)][1]
                    p.wait("gpsimd", vS, rec[("xstg", sg)])
                    if e == 0 and nti == 0 and sg == 0:
                        p.wait2("gpsimd", rec[("faccst", ntile - 1)])
                    kk2 = nt0 // 128 + sg
                    rec[("sca", sg)] = SCA.issue(
                        p, "gpsimd", lambda E, e=e, sg=sg, kk2=kk2: E.indirect_dma_start(
                            facc_scr[:, :],
                            bass.IndirectOffsetOnAxis(
                                ap=idxb[:, (e % 2) * ngrp + kk2:(e % 2) * ngrp + kk2 + 1],
                                axis=0),
                            xtm[:, sg * 2048:(sg + 1) * 2048], None,
                            bounds_check=bndreg, oob_is_err=False,
                            compute_op=OP.add), serialize=True)
            rec[("idxused", e)] = (rec[("upmm_last2", e * len(nts_list) + len(nts_list) - 1)],
                                   rec[("sca", nts_list[-1] // 128 - 1)])
        rec["sca_final"] = (SCA.sems[0], SCA.sems[0].cnt)

        for i in range(ntile):
            if i == 0:
                p.wait2("sync", rec["sca_final"])
                p.wait2("sync", rec[("cx_all", NE - 1)])
                p.wait("sync", tS, rec["dn_last"])
                p.wait2("sync", rec[("sstore", ntile - 1)])
            else:
                p.wait2("sync", rec[("ystore", i - 1)])
                p.wait("sync", vS, rec[("cdone", i - 1)])
            rec[("fload", i)] = KVL.issue(
                p, "sync", lambda E, i=i: E.dma_start(
                    bigB[:, :].rearrange("p (c f) -> p c f", c=NTC),
                    facc_scr[i * TT:(i + 1) * TT, :].rearrange("(c p) f -> p c f", p=128)))
            rec[("sload", i)] = RL.issue(
                p, "sync", lambda E, i=i: E.dma_start(
                    abf[:, 0:8192].rearrange("p (c f) -> p c f", c=NTC),
                    s_scr[i * TT:(i + 1) * TT, :].rearrange("(c p) f -> p c f", p=128)))
            rec[("xload3", i)] = XL.issue(
                p, "sync", lambda E, i=i: E.dma_start(
                    xtm[:, :].rearrange("p (c f) -> p c f", c=NTC),
                    x_d[i * TT:(i + 1) * TT, :].rearrange("(c p) f -> p c f", p=128)))
            p.wait2("scalar", rec[("fload", i)])
            p.wait2("vector", rec[("fload", i)])
            p.wait2("vector", rec[("sload", i)])
            p.wait2("vector", rec[("xload3", i)])
            for tc in range(NTC):
                if i == 0 and tc == 0:
                    p.wait("scalar", tS, rec[("idxused", NE - 1)][0])
                p.op("scalar", lambda E, tc=tc: E.activation(
                    mixA[:, tc * D:(tc + 1) * D],
                    bigB[:, tc * D:(tc + 1) * D], AF.Silu),
                     writes=("mixA",))
                gsr = p.mile("scalar", aS)
                p.wait("vector", aS, gsr)
                p.op("vector", lambda E, tc=tc: E.tensor_tensor(
                    bigB[:, tc * D:(tc + 1) * D], mixA[:, tc * D:(tc + 1) * D],
                    abf[:, tc * D:(tc + 1) * D], OP.mult),
                     reads=("mixA", "abf"), writes=("acc%d" % tc,))
                p.op("vector", lambda E, tc=tc: E.tensor_tensor(
                    xtm[:, tc * D:(tc + 1) * D], bigB[:, tc * D:(tc + 1) * D],
                    xtm[:, tc * D:(tc + 1) * D], OP.add),
                     reads=("acc%d" % tc, "xtm"), writes=("xtm",))
            rec[("cdone", i)] = p.mile("vector", vS)
            p.wait("scalar", vS, rec[("cdone", i)])
            rec[("ystore", i)] = YST.issue(
                p, "scalar", lambda E, i=i: E.dma_start(
                    y_d[i * TT:(i + 1) * TT, :].rearrange("(c p) f -> p c f", p=128),
                    xtm[:, :].rearrange("p (c f) -> p c f", c=NTC)))

        for stream, eng in ((YST, "scalar"), (CWST, "scalar"), (KVST, "scalar"),
                            (RST, "scalar"), (FST, "scalar"), (SST, "scalar"),
                            (XNST, "scalar"), (SETUP, "sync"), (WGT, "sync"),
                            (XL, "sync"), (KVL, "sync"), (RL, "sync"),
                            (CWL, "sync"), (CXS, "sync"),
                            (TRLp[0], "sync"), (TRLp[1], "sync"),
                            (CAST, "gpsimd"), (GPD, "gpsimd"), (SCT, "gpsimd"),
                            (GTH, "gpsimd"), (SCA, "gpsimd")):
            for s in stream.sems:
                if s.cnt > 0:
                    p.wait(eng, s, s.cnt)

        block = ctx.enter_context(nc.Block())
        for eng in Prog.ENGS:
            if p.st[eng]:
                getattr(block, eng)(_mk_runner(p.st[eng]))
    return nc


def _emit_down(p, rec, bank_free, load_col, col_wait, col_done, mm_group,
               ps, abf, bigB, edn_b, sdn_b, tS, vS, aS, i, e):
    shared = (e == NE)
    kk = NSI if shared else NIC
    w = 2816 if shared else 1408
    first_wait_done = False
    for m in range(16):
        src = (sdn_b[m] if shared else edn_b[e, m])
        xw = []
        if i == 0 and m == 0:
            xw.append(rec["cast_sdn" if shared else "cast_edn%d" % e])
        gcol, buf = load_col("plain", src[:, :], xw, width=w)
        col_wait(gcol)
        if not first_wait_done:
            p.wait("tensor", vS, rec[("adone", i, e)])
            first_wait_done = True
        dbank = ps[5] if m % 2 == 0 else ps[6]
        p.wait2("tensor", bank_free.get(id(dbank)))
        if shared:
            rhs = lambda k: abf[:, k * TT:(k + 1) * TT]
        else:
            rhs = lambda k, e=e: abf[:, (e % 2) * NIC * TT + k * TT:
                                     (e % 2) * NIC * TT + (k + 1) * TT]
        vd = mm_group(dbank, TT,
                      lambda k, b=buf: b[:, 128 * k:128 * (k + 1)], rhs, kk)
        col_done(gcol, vd)
        p.wait("vector", tS, vd)
        if e == 0:
            ga = rec.get(("gsil_done", m))
            if ga is not None:
                p.wait("vector", aS, ga)
            p.op("vector", lambda E, m=m, db=dbank: E.tensor_copy(
                bigB[:, m * TT:(m + 1) * TT], db[:, :]),
                 writes=("acc%d" % m,))
        else:
            p.op("vector", lambda E, m=m, db=dbank: E.tensor_tensor(
                bigB[:, m * TT:(m + 1) * TT], db[:, :],
                bigB[:, m * TT:(m + 1) * TT], OP.add),
                 reads=("acc%d" % m,), writes=("acc%d" % m,))
        bank_free[id(dbank)] = (vS, p.mile("vector", vS))
        rec[("dacc", e, m)] = bank_free[id(dbank)][1]



def kernel(**inputs):
    x = np.asarray(inputs["L0_attn_hidden_states"], np.float32)
    B, T, _ = x.shape
    t_loc = T // 2
    cap = 1280
    shared = _stage_shared_inputs(inputs)
    shared["tids"] = np.ascontiguousarray(
        np.arange(t_loc, dtype=np.uint32).reshape(t_loc // 128, 128).T)
    shared["idxinit"] = np.full((NE, cap, 1), 0x3FFFFFFF, np.uint32)
    shared["ecap"] = (np.arange(16, dtype=np.float32) * cap).reshape(16, 1)
    nc = build_program(t_loc, [[0, 1], [2, 3], [4, 5], [6, 7]], cap)
    in_maps = []
    for c in range(8):
        b, h = c // 2, c % 2
        m = dict(shared)
        m["x"] = np.ascontiguousarray(x[b, h * t_loc:(h + 1) * t_loc, :])
        m["xprev"] = (np.ascontiguousarray(x[b, t_loc - 1:t_loc, :]) if h == 1
                      else np.zeros((1, D), np.float32))
        m["isodd"] = np.full((128, 1), float(h), np.float32)
        in_maps.append(m)
    res = run_bass_kernel_spmd(nc, in_maps, core_ids=list(range(8)))
    out = np.empty((B, T, D), np.float32)
    for c in range(8):
        b, h = c // 2, c % 2
        out[b, h * t_loc:(h + 1) * t_loc, :] = np.asarray(res.results[c]["y"])
    return out


# version 11 (final)
# speedup vs baseline: 1.1431x; 1.0886x over previous
import numpy as np
from contextlib import ExitStack

import concourse.bass as bass
import concourse.mybir as mybir
from concourse.bass_utils import run_bass_kernel_spmd
from concourse.alu_op_type import AluOpType as OP

F32 = mybir.dt.float32
BF16 = mybir.dt.bfloat16
AF = mybir.ActivationFunctionType
AX = mybir.AxisListType.X

D = 2048
NF = 16
TT = 512
NTC = 4
NE = 16
NIC = 11
NSI = 22
EPS = 1e-5
NEGBIG = -1.0e30



def _cm(W):
    O, I = W.shape
    M, F = O // 128, I // 128
    W4 = W.reshape(M, 128, F, 128)
    return np.ascontiguousarray(W4.transpose(0, 3, 2, 1).reshape(M, 128, F * 128))


def _pairs(cm_grid):
    G = cm_grid.shape[0] // 2
    order = []
    for j in range(G):
        order += [j, G + j]
    return np.ascontiguousarray(cm_grid[order].reshape(G, 2, 128, cm_grid.shape[2]))


def _colT(v):
    return np.ascontiguousarray(v.reshape(16, 128).T.astype(np.float32))


def _stage_shared_inputs(inputs):
    f32 = np.float32
    g = {}
    g["wk_cm"] = _cm(np.asarray(inputs["L0_attn_Wk"], f32))
    g["wv_cm"] = _cm(np.asarray(inputs["L0_attn_Wv"], f32))
    g["wr_cm"] = _cm(np.asarray(inputs["L0_attn_Wr"], f32))
    g["wo_cm"] = _cm(np.asarray(inputs["L0_attn_Wo"], f32))
    eup = np.asarray(inputs["L0_ffn_expert_up_w"], f32)
    edn = np.asarray(inputs["L0_ffn_expert_down_w"], f32)
    g["eup_cm"] = np.ascontiguousarray(
        np.stack([_pairs(_cm(eup[e])) for e in range(NE)]))
    g["edn_cm"] = np.ascontiguousarray(
        np.stack([_cm(edn[e]) for e in range(NE)]))
    g["sup_cm"] = _pairs(_cm(np.asarray(inputs["L0_ffn_shared_up_w"], f32)))
    g["sdn_cm"] = _cm(np.asarray(inputs["L0_ffn_shared_down_w"], f32))
    g["gwt"] = np.ascontiguousarray(np.asarray(inputs["L0_ffn_gate_w"], f32).T)
    g["tdT"] = _colT(np.asarray(inputs["L0_attn_time_decay"], f32).reshape(-1))
    g["tfT"] = _colT(np.asarray(inputs["L0_attn_time_first"], f32).reshape(-1))
    g["tmk"] = _colT(np.asarray(inputs["L0_attn_time_mix_k"], f32).reshape(-1))
    g["tmv"] = _colT(np.asarray(inputs["L0_attn_time_mix_v"], f32).reshape(-1))
    g["tmr"] = _colT(np.asarray(inputs["L0_attn_time_mix_r"], f32).reshape(-1))
    g["ident"] = np.eye(128, dtype=f32)
    return g



class Sem:
    __slots__ = ("h", "cnt", "name")

    def __init__(self, h, name):
        self.h = h
        self.cnt = 0
        self.name = name


class Prog:
    ENGS = ("sync", "scalar", "vector", "tensor", "gpsimd")

    def __init__(self):
        self.st = {e: [] for e in self.ENGS}
        self.wr = {e: set() for e in self.ENGS}
        self.rd = {e: set() for e in self.ENGS}

    def wait(self, eng, sem, val):
        if val is None or val <= 0:
            return
        self.st[eng].append(lambda E, h=sem.h, v=val: E.wait_ge(h, v))

    def wait2(self, eng, sv):
        if sv is None:
            return
        self.wait(eng, sv[0], sv[1])

    def raw(self, eng, fn):
        self.st[eng].append(fn)

    def op(self, eng, fn, reads=(), writes=(), inc=None, inc_n=1):
        w = self.wr[eng]
        r = self.rd[eng]
        if eng in ("vector", "scalar", "gpsimd"):
            if (any(x in w for x in reads) or any(x in w for x in writes)
                    or any(x in r for x in writes)):
                self.st[eng].append(lambda E: E.drain())
                w.clear()
                r.clear()
        if inc is not None:
            inc.cnt += inc_n
            self.st[eng].append(
                lambda E, f=fn, h=inc.h, n=inc_n: f(E).then_inc(h, n))
            tgt = inc.cnt
        else:
            self.st[eng].append(fn)
            tgt = None
        w.update(writes)
        r.update(reads)
        return tgt

    def mile(self, eng, sem, n=1):
        assert eng != "tensor"
        self.wr[eng].clear()
        self.rd[eng].clear()
        sem.cnt += n
        self.st[eng].append(lambda E, h=sem.h, n2=n: E.drain().then_inc(h, n2))
        return sem.cnt


class DmaStream:

    def __init__(self, sems):
        self.sems = sems
        self.i = 0

    def issue(self, p, eng, fn, serialize=False):
        s = self.sems[self.i % len(self.sems)]
        self.i += 1
        if serialize and s.cnt > 0:
            p.wait(eng, s, s.cnt)
        tgt = p.op(eng, fn, inc=s, inc_n=16)
        return (s, tgt)


def _mk_runner(ops):
    def _run(E):
        for f in ops:
            f(E)
    return _run



def build_program(t_loc, replica_groups, cap):
    ntile = t_loc // TT
    ngrp = cap // 128
    nts_list = []
    _r = cap
    while _r > 0:
        nts_list.append(min(512, _r))
        _r -= min(512, _r)
    nc = bass.Bass("TRN2", target_bir_lowering=False)

    x_d = nc.dram_tensor("x", [t_loc, D], F32, kind="ExternalInput")
    xprev_d = nc.dram_tensor("xprev", [1, D], F32, kind="ExternalInput")
    isodd_d = nc.dram_tensor("isodd", [128, 1], F32, kind="ExternalInput")
    tdT_d = nc.dram_tensor("tdT", [128, 16], F32, kind="ExternalInput")
    tfT_d = nc.dram_tensor("tfT", [128, 16], F32, kind="ExternalInput")
    tmk_d = nc.dram_tensor("tmk", [128, 16], F32, kind="ExternalInput")
    tmv_d = nc.dram_tensor("tmv", [128, 16], F32, kind="ExternalInput")
    tmr_d = nc.dram_tensor("tmr", [128, 16], F32, kind="ExternalInput")
    ident_d = nc.dram_tensor("ident", [128, 128], F32, kind="ExternalInput")
    gwt_d = nc.dram_tensor("gwt", [D, 16], F32, kind="ExternalInput")
    tids_d = nc.dram_tensor("tids", [128, t_loc // 128], mybir.dt.uint32, kind="ExternalInput")
    idxinit_d = nc.dram_tensor("idxinit", [NE, cap, 1], mybir.dt.uint32, kind="ExternalInput")
    ecap_d = nc.dram_tensor("ecap", [16, 1], F32, kind="ExternalInput")
    wk_d = nc.dram_tensor("wk_cm", [16, 128, 2048], F32, kind="ExternalInput")
    wv_d = nc.dram_tensor("wv_cm", [16, 128, 2048], F32, kind="ExternalInput")
    wr_d = nc.dram_tensor("wr_cm", [16, 128, 2048], F32, kind="ExternalInput")
    wo_d = nc.dram_tensor("wo_cm", [16, 128, 2048], F32, kind="ExternalInput")
    eup_d = nc.dram_tensor("eup_cm", [NE, NIC, 2, 128, 2048], F32, kind="ExternalInput")
    edn_d = nc.dram_tensor("edn_cm", [NE, 16, 128, 1408], F32, kind="ExternalInput")
    sup_d = nc.dram_tensor("sup_cm", [NSI, 2, 128, 2048], F32, kind="ExternalInput")
    sdn_d = nc.dram_tensor("sdn_cm", [16, 128, 2816], F32, kind="ExternalInput")
    y_d = nc.dram_tensor("y", [t_loc, D], F32, kind="ExternalOutput")

    wk_b = nc.dram_tensor("wk_b", [16, 128, 2048], BF16)
    wv_b = nc.dram_tensor("wv_b", [16, 128, 2048], BF16)
    wr_b = nc.dram_tensor("wr_b", [16, 128, 2048], BF16)
    wo_b = nc.dram_tensor("wo_b", [16, 128, 2048], BF16)
    eup_b = nc.dram_tensor("eup_b", [NE, NIC, 2, 128, 2048], BF16)
    edn_b = nc.dram_tensor("edn_b", [NE, 16, 128, 1408], BF16)
    sup_b = nc.dram_tensor("sup_b", [NSI, 2, 128, 2048], BF16)
    sdn_b = nc.dram_tensor("sdn_b", [16, 128, 2816], BF16)
    kv_scr = nc.dram_tensor("kv_scr", [D, t_loc], BF16)
    r_scr = nc.dram_tensor("r_scr", [D, t_loc], BF16)
    xn_scr = nc.dram_tensor("xn_scr", [t_loc, D], BF16)
    s_scr = nc.dram_tensor("s_scr", [t_loc, D], BF16)
    facc_scr = nc.dram_tensor("facc_scr", [t_loc, D], F32)
    idx_scr = nc.dram_tensor("idx_scr", [NE, cap, 1], mybir.dt.uint32)
    cwc_scr = nc.dram_tensor("cwc_scr", [NE, cap, 1], F32)
    cx_scr = nc.dram_tensor("cx_scr", [NE, cap, D], BF16)
    cc_in = nc.dram_tensor("cc_in", [128, 16], F32)
    cc_out = nc.dram_tensor("cc_out", [128 * 2, 16], F32)

    with ExitStack() as ctx:
        sb = lambda nm, sh, dt: ctx.enter_context(nc.sbuf_tensor(nm, sh, dt))
        mkpsum = lambda nm: ctx.enter_context(nc.psum_tensor(nm, [128, 512], F32))

        xtm = sb("xtm", [128, NTC * D], F32)
        bigB = sb("bigB", [128, NF * TT], F32)
        bigC = sb("bigC", [128, NF * TT], F32)
        bigCb = bigC.bitcast(BF16)
        mixA = sb("mixA", [128, NF * TT], BF16)
        mixB = sb("mixB", [128, NF * TT], BF16)
        abf = sb("abf", [128, 2 * NIC * TT], BF16)
        wb = [sb("wb0", [128, 4096], BF16), sb("wb1", [128, 4096], BF16)]
        xntma = sb("xntma", [128, D], F32)
        xntmb = xntma.bitcast(BF16)[:, 0:D]
        st = sb("st", [128, TT + 1], F32)
        sgt = sb("sgt", [128, 2 * 512], F32)
        usc = sb("usc", [128, 512], F32)
        bcs = sb("bcs", [128, 2 * 512], F32)
        gsil = sb("gsil", [128, 2 * 512], F32)
        kbuf = gsil
        mbf = sgt.bitcast(BF16)[:, 0:1024]
        rows = sb("rows", [128, 16], F32)
        gat = sb("gat", [128, 128], F32)
        cwt = sb("cwt", [128, 2 * NTC * 16], F32)
        lgb = sb("lgb", [128, 2 * NTC * 16], F32)
        cw_fm = sb("cw_fm", [16, TT], F32)
        gfm = sb("gfm", [16, 3 * TT], F32)
        ebase = sb("ebase", [16, 2], F32)
        onecol = sb("onecol", [128, 1], F32)
        onesT = sb("onesT", [33, 128], F32)
        offsu = sb("offsu", [128, 256], mybir.dt.uint32)
        idxb = sb("idxb", [128, 2 * ngrp], mybir.dt.uint32)
        cc = sb("cc", [128, 160], F32)
        ident = sb("identt", [128, 128], F32)
        identb = sb("identtb", [128, 128], BF16)
        gw_sb = sb("gw_sb", [128, NF * 16], F32)
        tids_sb = sb("tids_sb", [128, t_loc // 128], mybir.dt.uint32)
        sg_in = sb("sg_in", [128, 16], F32)

        xprev_sb = bigB[0:1, 0:D]
        xnprev = bigB[0:1, D:2 * D]
        cwc_sb = [bigC[0:1, 0:cap], bigC[32:33, 0:cap]]
        ps = [mkpsum("ps%d" % i) for i in range(8)]

        mksem = lambda nm: Sem(ctx.enter_context(nc.semaphore(nm)), nm)
        cS = mksem("cS")
        tS = mksem("tS")
        vS = mksem("vS")
        aS = mksem("aS")
        SETUP = DmaStream([mksem("dSet")])
        WGT = DmaStream([mksem("dW%d" % k) for k in range(4)])
        XL = DmaStream([mksem("dXL")])
        KVL = DmaStream([mksem("dKVL")])
        RL = DmaStream([mksem("dRL")])
        CWL = DmaStream([mksem("dCWL%d" % k) for k in range(2)])
        CWST = DmaStream([mksem("dCWS")])
        KVST = DmaStream([mksem("dKS%d" % k) for k in range(2)])
        RST = DmaStream([mksem("dRS%d" % k) for k in range(2)])
        YST = DmaStream([mksem("dYS")])
        CAST = DmaStream([mksem("dCast")])
        GPD = DmaStream([mksem("dGP")])
        SCT = DmaStream([mksem("dSCT%d" % k) for k in range(2)])
        GTH = DmaStream([mksem("dGT%d" % k) for k in range(2)])
        CXS = DmaStream([mksem("dCXS%d" % k) for k in range(2)])
        SCA = DmaStream([mksem("dSCA")])
        TRLp = [DmaStream([mksem("dTRL%d_%d" % (pp, k)) for k in range(4)])
                for pp in range(2)]
        FST = DmaStream([mksem("dFST")])
        SST = DmaStream([mksem("dSST")])
        XNST = DmaStream([mksem("dXNS")])

        WC, UC, CK, CV, CR, SC, PC = 0, 16, 32, 48, 64, 80, 96
        IO = 112
        ET = 128
        R_SUM, R_SSQ, R_MEAN, R_MSQ, R_VAR, R_SQV, R_RSTD = 0, 1, 2, 3, 4, 5, 6
        G_LG, G_ES, G_SCO, G_MASK, G_CUR = 0, 16, 32, 48, 64
        G_GMAX, G_M4, G_GSEL, G_MK1 = 80, 84, 88, 92
        G_SMAX, G_NM, G_SSUM, G_RS, G_M1, G_M2, G_G2A, G_SW, G_RSW = (
            96, 97, 98, 99, 100, 101, 102, 104, 108)

        p = Prog()
        rec = {}
        bank_free = {}

        bndreg = nc.gpsimd.alloc_register("bndreg")

        def emit_casts(cast_list):
            for nm, csrc, cdst in cast_list:
                if len(csrc.shape) == 3:
                    s2 = csrc.rearrange("a b c -> (a b) c")
                    d2 = cdst.rearrange("a b c -> (a b) c")
                else:
                    s2 = csrc.rearrange("a b c d -> (a b c) d")
                    d2 = cdst.rearrange("a b c d -> (a b c) d")
                rec["cast_" + nm] = CAST.issue(
                    p, "gpsimd", lambda E, a=d2, b=s2: E.dma_start(a[:, :], b[:, :]),
                    serialize=True)

        rec["idxinit"] = CAST.issue(
            p, "gpsimd", lambda E: E.dma_start(
                idx_scr.rearrange("e s o -> (e s) o")[:, :],
                idxinit_d.rearrange("e s o -> (e s) o")[:, :]))
        p.raw("gpsimd", lambda E: E.reg_mov(bndreg, NE * cap - 1))
        emit_casts([("wk", wk_d, wk_b), ("wv", wv_d, wv_b), ("wr", wr_d, wr_b),
                    ("sup", sup_d, sup_b), ("sdn", sdn_d, sdn_b), ("wo", wo_d, wo_b)])

        def sdma(fn):
            return SETUP.issue(p, "sync", fn)

        sdma(lambda E: E.dma_start(cc[:, WC:WC + 16], tdT_d[:, :]))
        sdma(lambda E: E.dma_start(cc[:, UC:UC + 16], tfT_d[:, :]))
        sdma(lambda E: E.dma_start(cc[:, CK:CK + 16], tmk_d[:, :]))
        sdma(lambda E: E.dma_start(cc[:, CV:CV + 16], tmv_d[:, :]))
        sdma(lambda E: E.dma_start(cc[:, CR:CR + 16], tmr_d[:, :]))
        sdma(lambda E: E.dma_start(cc[:, IO:IO + 1], isodd_d[:, :]))
        sdma(lambda E: E.dma_start(ident[:, :], ident_d[:, :]))
        sdma(lambda E: E.dma_start(tids_sb[:, :], tids_d[:, :]))
        sdma(lambda E: E.dma_start(ebase[0:16, 1:2], ecap_d[:, :]))
        sdma(lambda E: E.dma_start(
            gw_sb[:, :].rearrange("p (f e) -> p f e", f=NF),
            gwt_d.rearrange("(f p) e -> p f e", p=128)[:, :, :]))
        sdma(lambda E: E.dma_start(xprev_sb, xprev_d[:, :]))
        rec["setup_dma"] = (SETUP.sems[0], SETUP.sems[0].cnt)

        p.wait2("scalar", rec["setup_dma"])
        p.op("scalar", lambda E: E.activation(cc[:, ET:ET + 16], cc[:, WC:WC + 16], AF.Exp),
             reads=("cc",), writes=("cc",))
        rec["wc1"] = p.mile("scalar", aS)
        p.wait("vector", aS, rec["wc1"])
        p.op("vector", lambda E: E.tensor_scalar(
            cc[:, ET:ET + 16], cc[:, ET:ET + 16], -1.0, None, OP.mult),
             reads=("cc",), writes=("cc",))
        rec["wc2"] = p.mile("vector", vS)
        p.wait("scalar", vS, rec["wc2"])
        p.op("scalar", lambda E: E.activation(cc[:, WC:WC + 16], cc[:, ET:ET + 16], AF.Exp),
             reads=("cc",), writes=("cc",))
        rec["wcols"] = p.mile("scalar", aS)

        p.wait2("vector", rec["setup_dma"])
        for off in (CK, CV, CR):
            p.op("vector", lambda E, o=off: E.tensor_scalar(
                cc[:, o:o + 16], cc[:, o:o + 16], -1.0, 1.0, OP.mult, OP.add),
                 reads=("cc",), writes=("cc",))
        p.op("vector", lambda E: E.tensor_copy(identb[:, :], ident[:, :]),
             writes=("identb",))
        p.op("vector", lambda E: E.memset(cc[:, SC:SC + 16], 0.0),
             reads=("cc",), writes=("cc",))
        p.op("vector", lambda E: E.memset(onecol[:, :], 1.0), writes=("onecol",))
        p.op("vector", lambda E: E.memset(onesT[:, :], 1.0), writes=("onesT",))
        p.op("vector", lambda E: E.memset(ebase[:, 0:1], 0.0), writes=("ebase",))
        p.op("vector", lambda E: E.reduce_sum(rows[0:1, 8:9], xprev_sb, axis=AX),
             writes=("prow",))
        rec["prow_sum"] = p.mile("vector", vS)
        p.wait("scalar", vS, rec["prow_sum"])
        p.op("scalar", lambda E: E.activation(xnprev, xprev_sb, AF.Square,
                                              accum_out=rows[0:1, 9:10]),
             writes=("xnprev", "prow"))
        rec["prow_ssq"] = p.mile("scalar", aS)
        p.wait("vector", aS, rec["prow_ssq"])
        p.op("vector", lambda E: E.tensor_scalar(
            rows[0:1, 10:11], rows[0:1, 8:9], 1.0 / D, None, OP.mult),
             reads=("prow",), writes=("prow",))
        p.op("vector", lambda E: E.tensor_scalar(
            rows[0:1, 11:12], rows[0:1, 10:11], rows[0:1, 10:11], None, OP.mult),
             reads=("prow",), writes=("prow",))
        p.op("vector", lambda E: E.scalar_tensor_tensor(
            rows[0:1, 12:13], rows[0:1, 9:10], 1.0 / D, rows[0:1, 11:12], OP.mult, OP.subtract),
             reads=("prow",), writes=("prow",))
        p.op("vector", lambda E: E.tensor_scalar(
            rows[0:1, 12:13], rows[0:1, 12:13], EPS, None, OP.add),
             reads=("prow",), writes=("prow",))
        rec["prow_var"] = p.mile("vector", vS)
        p.wait("scalar", vS, rec["prow_var"])
        p.op("scalar", lambda E: E.activation(
            rows[0:1, 13:14], rows[0:1, 12:13], AF.Sqrt),
             reads=("prow",), writes=("prow",))
        rec["prow_sqv"] = p.mile("scalar", aS)
        p.wait("vector", aS, rec["prow_sqv"])
        p.op("vector", lambda E: E.reciprocal(rows[0:1, 14:15], rows[0:1, 13:14]),
             reads=("prow",), writes=("prow",))
        p.op("vector", lambda E: E.tensor_scalar(
            xnprev, xprev_sb, rows[0:1, 10:11], rows[0:1, 14:15],
            OP.subtract, OP.mult),
             reads=("prow",), writes=("xnprev",))
        rec["xnprev"] = p.mile("vector", vS)

        p.wait2("tensor", rec["setup_dma"])
        p.wait("tensor", vS, rec["xnprev"])
        for k in range(NF):
            last = (k == NF - 1)
            fn = lambda E, k=k, last=last: E.matmul(
                ps[0][:, k:k + 1], bigB[0:1, D + 128 * k:D + 128 * (k + 1)],
                ident[0:1, 0:1], start=(k == 0), stop=last)
            if last:
                rec["pcolT"] = p.op("tensor", fn, inc=tS)
            else:
                p.raw("tensor", fn)
        p.wait("vector", tS, rec["pcolT"])
        p.op("vector", lambda E: E.tensor_copy(cc[:, PC:PC + 16], ps[0][:, 0:16]),
             reads=("cc",), writes=("cc",))
        bank_free[id(ps[0])] = (vS, p.mile("vector", vS))

        wload_i = [0]
        wload_rec = {}
        slot_cons = {}
        slot_cur = [0]

        def load_col(kind, src_ap, extra_waits=(), width=2048):
            nhalf = 1 if width <= 2048 else 2
            if nhalf == 2 and slot_cur[0] % 2 == 1:
                slot_cur[0] += 1
            s0 = slot_cur[0] % 4
            slots = [(slot_cur[0] + k) % 4 for k in range(nhalf)]
            slot_cur[0] += nhalf
            g = wload_i[0]
            wload_i[0] += 1
            wbt = wb[s0 // 2]
            off = (s0 % 2) * 2048
            for s in slots:
                cv = slot_cons.get(s)
                if cv is not None:
                    p.wait("sync", tS, cv)
            for sv in extra_waits:
                p.wait2("sync", sv)
            if kind == "pair":
                dst = bass.AP(wbt, 0, [[4096, 128], [2048, 2], [1, 2048]])
            else:
                dst = wbt[:, off:off + width]
            wload_rec[g] = WGT.issue(
                p, "sync", lambda E, a=dst, b=src_ap: E.dma_start(a, b))
            region = wbt[:, off:off + (4096 if nhalf == 2 else width)]
            return (g, slots), region

        def col_wait(gs):
            p.wait2("tensor", wload_rec[gs[0]])

        def col_done(gs, tsval):
            for s in gs[1]:
                slot_cons[s] = tsval

        def mm_group(bank, width, lhsT_fn, rhs_fn, kk):
            out = bank[:, 0:width]
            for k in range(kk):
                last = (k == kk - 1)
                fn = lambda E, k=k, last=last, o=out: E.matmul(
                    o, lhsT_fn(k), rhs_fn(k), start=(k == 0), stop=last)
                if last:
                    return p.op("tensor", fn, inc=tS)
                p.raw("tensor", fn)

        def emit_ln(tc, out_tile, out_name, tag, wait_ts=None):
            if wait_ts is not None:
                p.wait("vector", tS, wait_ts)
                p.wait("scalar", tS, wait_ts)
            p.op("vector", lambda E, tc=tc: E.reduce_sum(
                rows[:, R_SUM:R_SUM + 1], xtm[:, tc * D:(tc + 1) * D], axis=AX),
                 reads=("xtm",), writes=("rows",))
            rec[("lnsum", tag)] = p.mile("vector", vS)
            p.op("scalar", lambda E, tc=tc: E.activation(
                xntma[:, :], xtm[:, tc * D:(tc + 1) * D], AF.Square,
                accum_out=rows[:, R_SSQ:R_SSQ + 1]),
                 reads=("xtm",), writes=("xntma", "rows"))
            rec[("lnssq", tag)] = p.mile("scalar", aS)
            p.wait("vector", aS, rec[("lnssq", tag)])
            p.op("vector", lambda E: E.tensor_scalar(
                rows[:, R_MEAN:R_MEAN + 1], rows[:, R_SUM:R_SUM + 1],
                1.0 / D, None, OP.mult),
                 reads=("rows",), writes=("rows",))
            p.op("vector", lambda E: E.tensor_scalar(
                rows[:, R_MSQ:R_MSQ + 1], rows[:, R_MEAN:R_MEAN + 1],
                rows[:, R_MEAN:R_MEAN + 1], None, OP.mult),
                 reads=("rows",), writes=("rows",))
            p.op("vector", lambda E: E.scalar_tensor_tensor(
                rows[:, R_VAR:R_VAR + 1], rows[:, R_SSQ:R_SSQ + 1], 1.0 / D,
                rows[:, R_MSQ:R_MSQ + 1], OP.mult, OP.subtract),
                 reads=("rows",), writes=("rows",))
            p.op("vector", lambda E: E.tensor_scalar(
                rows[:, R_VAR:R_VAR + 1], rows[:, R_VAR:R_VAR + 1], EPS, None, OP.add),
                 reads=("rows",), writes=("rows",))
            rec[("lnvar", tag)] = p.mile("vector", vS)
            p.wait("scalar", vS, rec[("lnvar", tag)])
            p.op("scalar", lambda E: E.activation(
                rows[:, R_SQV:R_SQV + 1], rows[:, R_VAR:R_VAR + 1], AF.Sqrt),
                 reads=("rows",), writes=("rows",))
            rec[("lnsqv", tag)] = p.mile("scalar", aS)
            p.wait("vector", aS, rec[("lnsqv", tag)])
            p.op("vector", lambda E: E.reciprocal(
                rows[:, R_RSTD:R_RSTD + 1], rows[:, R_SQV:R_SQV + 1]),
                 reads=("rows",), writes=("rows",))
            p.op("vector", lambda E, tc=tc, o=out_tile: E.tensor_scalar(
                o, xtm[:, tc * D:(tc + 1) * D], rows[:, R_MEAN:R_MEAN + 1],
                rows[:, R_RSTD:R_RSTD + 1], OP.subtract, OP.mult),
                 reads=("rows", "xtm"), writes=(out_name,))
            rec[("ln", tag)] = p.mile("vector", vS)

        def emit_transpose_tc(tc, src_tile, dst_tile, dst_name, idt, tag):
            for grp in range(4):
                bank = ps[0] if grp % 2 == 0 else ps[7]
                p.wait2("tensor", bank_free.get(id(bank)))
                for j in range(4):
                    f = grp * 4 + j
                    last = (j == 3)
                    fn = lambda E, f=f, j=j, b=bank, last=last: E.matmul(
                        b[:, 128 * j:128 * (j + 1)],
                        src_tile[:, 128 * f:128 * (f + 1)], idt[:, :],
                        start=(j == 0), stop=last)
                    if last:
                        v = p.op("tensor", fn, inc=tS)
                    else:
                        p.raw("tensor", fn)
                p.wait("vector", tS, v)
                f0 = grp * 4
                dst_ap = bass.AP(dst_tile, f0 * TT + tc * 128,
                                 [[NF * TT, 128], [TT, 4], [1, 128]])
                src_ap = bass.AP(bank, 0, [[512, 128], [128, 4], [1, 128]])
                p.op("vector", lambda E, d=dst_ap, s=src_ap: E.tensor_copy(d, s),
                     writes=(dst_name,))
                bank_free[id(bank)] = (vS, p.mile("vector", vS))
            rec[("T", tag)] = bank_free[id(ps[7])][1]
            rec[("Tts", tag)] = v

        def emit_gating(j):
            p.wait("vector", aS, rec[("gatelg", j)])
            scprev0 = rec.get(("scat", j - 2))
            if scprev0 is not None:
                p.wait2("vector", scprev0)
            for tc in range(NTC):
                p.op("vector", lambda E, tc=tc: E.reduce_max(
                    gat[:, G_SMAX:G_SMAX + 1], lgb[:, (j % 2) * 64 + tc * 16:(j % 2) * 64 + (tc + 1) * 16], axis=AX),
                     reads=("lgb",), writes=("gat_r",))
                p.op("vector", lambda E: E.tensor_scalar(
                    gat[:, G_NM:G_NM + 1], gat[:, G_SMAX:G_SMAX + 1], -1.0, None, OP.mult),
                     reads=("gat_r",), writes=("gat_r",))
                vnm = p.mile("vector", vS)
                p.wait("scalar", vS, vnm)
                p.op("scalar", lambda E, tc=tc: E.activation(
                    gat[:, G_ES:G_ES + 16], lgb[:, (j % 2) * 64 + tc * 16:(j % 2) * 64 + (tc + 1) * 16], AF.Exp,
                    bias=gat[:, G_NM:G_NM + 1], accum_out=gat[:, G_SSUM:G_SSUM + 1]),
                     reads=("lgb",), writes=("gat_es",))
                ves = p.mile("scalar", aS)
                p.wait("vector", aS, ves)
                p.op("vector", lambda E: E.reciprocal(
                    gat[:, G_RS:G_RS + 1], gat[:, G_SSUM:G_SSUM + 1]),
                     reads=("gat_es",), writes=("gat_r",))
                p.op("vector", lambda E: E.tensor_scalar(
                    gat[:, G_SCO:G_SCO + 16], gat[:, G_ES:G_ES + 16],
                    gat[:, G_RS:G_RS + 1], None, OP.mult),
                     reads=("gat_es", "gat_r"), writes=("gat_sco",))
                for g4 in range(4):
                    p.op("vector", lambda E, g4=g4: E.reduce_max(
                        gat[:, G_GMAX + g4:G_GMAX + g4 + 1],
                        gat[:, G_SCO + 4 * g4:G_SCO + 4 * (g4 + 1)], axis=AX),
                         reads=("gat_sco",), writes=("gat_gm",))
                p.op("vector", lambda E: E.reduce_max(
                    gat[:, G_M1:G_M1 + 1], gat[:, G_GMAX:G_GMAX + 4], axis=AX),
                     reads=("gat_gm",), writes=("gat_r",))
                p.op("vector", lambda E: E.tensor_scalar(
                    gat[:, G_MK1:G_MK1 + 4], gat[:, G_GMAX:G_GMAX + 4],
                    gat[:, G_M1:G_M1 + 1], None, OP.is_ge),
                     reads=("gat_gm", "gat_r"), writes=("gat_mk1",))
                p.op("vector", lambda E: E.scalar_tensor_tensor(
                    gat[:, G_G2A:G_G2A + 4], gat[:, G_MK1:G_MK1 + 4], NEGBIG,
                    gat[:, G_GMAX:G_GMAX + 4], OP.mult, OP.add),
                     reads=("gat_mk1", "gat_gm"), writes=("gat_g2",))
                p.op("vector", lambda E: E.reduce_max(
                    gat[:, G_M2:G_M2 + 1], gat[:, G_G2A:G_G2A + 4], axis=AX),
                     reads=("gat_g2",), writes=("gat_r",))
                p.op("vector", lambda E: E.tensor_scalar(
                    gat[:, G_GSEL:G_GSEL + 4], gat[:, G_GMAX:G_GMAX + 4],
                    gat[:, G_M2:G_M2 + 1], None, OP.is_ge),
                     reads=("gat_gm", "gat_r"), writes=("gat_gsel",))
                p.op("vector", lambda E: E.tensor_copy(
                    bass.AP(gat, G_MASK, [[128, 128], [4, 4], [1, 4]]),
                    bass.AP(gat, G_GSEL, [[128, 128], [1, 4], [0, 4]])),
                     reads=("gat_gsel",), writes=("gat_mask",))
                p.op("vector", lambda E: E.tensor_tensor(
                    gat[:, G_CUR:G_CUR + 16], gat[:, G_SCO:G_SCO + 16],
                    gat[:, G_MASK:G_MASK + 16], OP.mult),
                     reads=("gat_sco", "gat_mask"), writes=("gat_cur",))
                p.op("vector", lambda E: E.tensor_copy(
                    gat[:, G_MASK:G_MASK + 16], gat[:, G_CUR:G_CUR + 16]),
                     reads=("gat_cur",), writes=("gat_mask",))
                for tsel in range(4):
                    p.op("vector", lambda E, t2=tsel: E.reduce_max(
                        gat[:, G_M4 + t2:G_M4 + t2 + 1],
                        gat[:, G_CUR:G_CUR + 16], axis=AX),
                         reads=("gat_cur",), writes=("gat_m4",))
                    if tsel < 3:
                        p.op("vector", lambda E, t2=tsel: E.tensor_scalar(
                            gat[:, G_ES:G_ES + 16], gat[:, G_CUR:G_CUR + 16],
                            gat[:, G_M4 + t2:G_M4 + t2 + 1], None, OP.is_ge),
                             reads=("gat_cur", "gat_m4"), writes=("gat_es",))
                        p.op("vector", lambda E: E.scalar_tensor_tensor(
                            gat[:, G_CUR:G_CUR + 16], gat[:, G_ES:G_ES + 16], NEGBIG,
                            gat[:, G_CUR:G_CUR + 16], OP.mult, OP.add),
                             reads=("gat_es", "gat_cur"), writes=("gat_cur",))
                p.op("vector", lambda E: E.reduce_sum(
                    gat[:, G_SW:G_SW + 1], gat[:, G_M4:G_M4 + 4], axis=AX),
                     reads=("gat_m4",), writes=("gat_r",))
                p.op("vector", lambda E: E.tensor_scalar(
                    gat[:, G_SW + 1:G_SW + 2], gat[:, G_SW:G_SW + 1], 1e-20, None, OP.add),
                     reads=("gat_r",), writes=("gat_r",))
                p.op("vector", lambda E: E.reciprocal(
                    gat[:, G_RSW:G_RSW + 1], gat[:, G_SW + 1:G_SW + 2]),
                     reads=("gat_r",), writes=("gat_r",))
                p.op("vector", lambda E: E.tensor_scalar(
                    gat[:, G_ES:G_ES + 16], gat[:, G_MASK:G_MASK + 16],
                    gat[:, G_M4 + 3:G_M4 + 4], None, OP.is_ge),
                     reads=("gat_mask", "gat_m4"), writes=("gat_es",))
                p.op("vector", lambda E, tc=tc: E.scalar_tensor_tensor(
                    cwt[:, (j % 2) * 64 + tc * 16:(j % 2) * 64 + (tc + 1) * 16], gat[:, G_MASK:G_MASK + 16],
                    gat[:, G_RSW:G_RSW + 1], gat[:, G_ES:G_ES + 16],
                    OP.mult, OP.mult),
                     reads=("gat_mask", "gat_es", "gat_r"), writes=("cwt",))
            rec[("cwt", j)] = p.mile("vector", vS)

            MK, RK, OF = 0, TT, 2 * TT
            p.wait("tensor", vS, rec[("cwt", j)])
            p.wait2("tensor", bank_free.get(id(ps[0])))
            for tc in range(NTC):
                last = (tc == NTC - 1)
                fn = lambda E, tc=tc, last=last: E.matmul(
                    ps[0][0:16, tc * 128:(tc + 1) * 128],
                    cwt[:, (j % 2) * 64 + tc * 16:(j % 2) * 64 + (tc + 1) * 16], ident[:, :],
                    start=(tc == 0), stop=last)
                if last:
                    vcw = p.op("tensor", fn, inc=tS)
                else:
                    p.raw("tensor", fn)
            p.wait("vector", tS, vcw)
            p.op("vector", lambda E: E.tensor_copy(cw_fm[:, :], ps[0][0:16, 0:512]),
                 writes=("cw_fm",))
            bank_free[id(ps[0])] = (vS, p.mile("vector", vS))
            p.op("vector", lambda E: E.tensor_scalar(
                gfm[:, MK:MK + TT], cw_fm[:, :], 0.0, None, OP.is_gt),
                 reads=("cw_fm",), writes=("gfm",))
            p.op("vector", lambda E: E.tensor_tensor_scan(
                gfm[:, RK:RK + TT], bass.AP(onecol, 0, [[1, 16], [0, TT]]),
                gfm[:, MK:MK + TT], ebase[:, 0:1], OP.mult, OP.add),
                 reads=("gfm", "onecol", "ebase"), writes=("gfm",))
            p.op("vector", lambda E: E.tensor_copy(
                ebase[:, 0:1], gfm[:, RK + TT - 1:RK + TT]),
                 reads=("gfm",), writes=("ebase",))
            p.op("vector", lambda E: E.tensor_scalar(
                gfm[:, RK:RK + TT], gfm[:, RK:RK + TT], -1.0, None, OP.add),
                 reads=("gfm",), writes=("gfm",))
            p.op("vector", lambda E: E.tensor_scalar(
                gfm[:, OF:OF + TT], gfm[:, RK:RK + TT], 1.0 / 128.0, -0.496,
                OP.mult, OP.add),
                 reads=("gfm",), writes=("gfm",))
            p.op("vector", lambda E: E.tensor_scalar(
                gfm[:, OF:OF + TT], gfm[:, OF:OF + TT], 12582912.0, None, OP.add),
                 reads=("gfm",), writes=("gfm",))
            p.op("vector", lambda E: E.tensor_scalar(
                gfm[:, OF:OF + TT], gfm[:, OF:OF + TT], -12582912.0, None, OP.add),
                 reads=("gfm",), writes=("gfm",))
            p.op("vector", lambda E, ng=ngrp: E.tensor_scalar(
                cw_fm[:, :], gfm[:, RK:RK + TT], float(ng), None, OP.mult),
                 reads=("gfm",), writes=("cw_fm",))
            p.op("vector", lambda E, ng=ngrp: E.scalar_tensor_tensor(
                gfm[:, OF:OF + TT], gfm[:, OF:OF + TT], -float(128 * ng - 1),
                cw_fm[:, :], OP.mult, OP.add),
                 reads=("gfm", "cw_fm"), writes=("gfm",))
            p.op("vector", lambda E: E.tensor_tensor(
                gfm[:, OF:OF + TT], gfm[:, OF:OF + TT], gfm[:, MK:MK + TT], OP.mult),
                 reads=("gfm",), writes=("gfm",))
            p.op("vector", lambda E: E.tensor_tensor(
                gfm[:, RK:RK + TT], gfm[:, RK:RK + TT], gfm[:, MK:MK + TT], OP.mult),
                 reads=("gfm",), writes=("gfm",))
            p.op("vector", lambda E: E.tensor_scalar(
                gfm[:, OF:OF + TT], gfm[:, OF:OF + TT], ebase[:, 1:2], None, OP.add),
                 reads=("gfm", "ebase"), writes=("gfm",))
            p.op("vector", lambda E: E.tensor_scalar(
                gfm[:, RK:RK + TT], gfm[:, RK:RK + TT], ebase[:, 1:2], None, OP.add),
                 reads=("gfm", "ebase"), writes=("gfm",))
            p.op("vector", lambda E: E.tensor_scalar(
                gfm[:, MK:MK + TT], gfm[:, MK:MK + TT], -1.0, 1.0, OP.mult, OP.add),
                 reads=("gfm",), writes=("gfm",))
            p.op("vector", lambda E: E.scalar_tensor_tensor(
                gfm[:, OF:OF + TT], gfm[:, MK:MK + TT], 1.0e9, gfm[:, OF:OF + TT],
                OP.mult, OP.add),
                 reads=("gfm",), writes=("gfm",))
            p.op("vector", lambda E: E.scalar_tensor_tensor(
                gfm[:, RK:RK + TT], gfm[:, MK:MK + TT], 1.0e9, gfm[:, RK:RK + TT],
                OP.mult, OP.add),
                 reads=("gfm",), writes=("gfm",))
            rec[("offsfm", j)] = p.mile("vector", vS)
            p.wait("tensor", vS, rec[("offsfm", j)])
            p.wait2("tensor", bank_free.get(id(ps[7])))
            nT = 0
            for base, srcoff in ((0, OF), (64, RK)):
                for tc in range(NTC):
                    nT += 1
                    last = (nT == 8)
                    fn = lambda E, b2=base, so=srcoff, tc=tc, last=last, first=(nT == 1): \
                        E.matmul(ps[7][:, b2 + tc * 16:b2 + (tc + 1) * 16],
                                 gfm[0:16, so + tc * 128:so + (tc + 1) * 128],
                                 ident[0:16, 0:16], start=first, stop=last)
                    if last:
                        voff = p.op("tensor", fn, inc=tS)
                    else:
                        p.raw("tensor", fn)
            p.wait("vector", tS, voff)
            scprev = rec.get(("scat", j - 2))
            if scprev is not None:
                p.wait2("vector", scprev)
            p.op("vector", lambda E: E.tensor_copy(offsu[:, (j % 2) * 128:(j % 2) * 128 + 128], ps[7][:, 0:128]),
                 writes=("offsu",))
            bank_free[id(ps[7])] = (vS, p.mile("vector", vS))
            rec[("offsu", j)] = bank_free[id(ps[7])][1]
            p.wait("gpsimd", vS, rec[("offsu", j)])
            if j == 0:
                p.wait2("gpsimd", rec["idxinit"])
            ssem = SCT.sems[j % 2]
            for tc in range(NTC):
                g = j * NTC + tc
                for e in range(NE):
                    p.op("gpsimd", lambda E, e=e, tc=tc, g=g: E.indirect_dma_start(
                        idx_scr.rearrange("e s o -> (e s) o"),
                        bass.IndirectOffsetOnAxis(
                            ap=offsu[:, (j % 2) * 128 + tc * 16 + e:(j % 2) * 128 + tc * 16 + e + 1], axis=0),
                        tids_sb[:, g:g + 1], None,
                        bounds_check=bndreg, oob_is_err=False), inc=ssem, inc_n=16)
                    p.op("gpsimd", lambda E, e=e, tc=tc: E.indirect_dma_start(
                        cwc_scr.rearrange("e s o -> (e s) o"),
                        bass.IndirectOffsetOnAxis(
                            ap=offsu[:, (j % 2) * 128 + 64 + tc * 16 + e:(j % 2) * 128 + 64 + tc * 16 + e + 1], axis=0),
                        cwt[:, (j % 2) * 64 + tc * 16 + e:(j % 2) * 64 + tc * 16 + e + 1],
                        None,
                        bounds_check=bndreg, oob_is_err=False), inc=ssem, inc_n=16)
            rec[("scat", j)] = (ssem, ssem.cnt)

            rec[("lgused", j % 2)] = rec[("offsu", j)]

        ecl = []
        for e in range(NE):
            ecl.append(("eup%d" % e, eup_d[e], eup_b[e]))
            ecl.append(("edn%d" % e, edn_d[e], edn_b[e]))

        p.wait("vector", aS, rec["wcols"])
        for i in range(ntile):
            if i > 0:
                p.wait("sync", vS, rec[("ln", ("p1", i - 1, NTC - 1))])
                p.wait("sync", aS, rec[("lnssq", ("p1", i - 1, NTC - 1))])
            rec[("xload", i)] = XL.issue(
                p, "sync", lambda E, i=i: E.dma_start(
                    xtm[:, :].rearrange("p (c f) -> p c f", c=NTC),
                    x_d[i * TT:(i + 1) * TT, :].rearrange("(c p) f -> p c f", p=128)))
            p.wait2("vector", rec[("xload", i)])
            p.wait2("scalar", rec[("xload", i)])

            for tc in range(NTC):
                wts = rec.get(("Tts", ("p1", i, tc - 1))) if tc > 0 else \
                    (rec.get(("Tts", ("p1", i - 1, NTC - 1))) if i > 0 else None)
                emit_ln(tc, xntma[:, :], "xntma", ("p1", i, tc), wait_ts=wts)
                p.wait("tensor", vS, rec[("ln", ("p1", i, tc))])
                emit_transpose_tc(tc, xntma, bigB, "bigB", ident, ("p1", i, tc))

            p.wait("vector", vS, rec[("T", ("p1", i, NTC - 1))])
            if i > 0:
                p.wait("vector", tS, rec[("rmm", i - 1)])
            p.op("vector", lambda E: E.tensor_copy(
                cc[:, ET:ET + 16], cc[:, PC:PC + 16]),
                 reads=("cc",), writes=("cc",))
            def _dprime(f):
                p.op("vector", lambda E, f=f: E.tensor_tensor(
                    sgt[:, 1:TT], bigB[:, f * TT:f * TT + TT - 1],
                    bigB[:, f * TT + 1:f * TT + TT], OP.subtract),
                     reads=("bigB",), writes=("sgt",))
                p.op("vector", lambda E, f=f: E.tensor_tensor(
                    sgt[:, 0:1], cc[:, ET + f:ET + f + 1],
                    bigB[:, f * TT:f * TT + 1], OP.subtract),
                     reads=("bigB", "cc"), writes=("sgt",))
            for f in range(NF):
                _dprime(f)
                p.op("vector", lambda E, f=f: E.scalar_tensor_tensor(
                    mixA[:, f * TT:(f + 1) * TT], sgt[:, 0:TT],
                    cc[:, CK + f:CK + f + 1], bigB[:, f * TT:(f + 1) * TT],
                    OP.mult, OP.add),
                     reads=("sgt", "cc", "bigB"), writes=("mixA",))
            rec[("mixk", i)] = p.mile("vector", vS)
            for f in range(NF):
                _dprime(f)
                p.op("vector", lambda E, f=f: E.scalar_tensor_tensor(
                    mixB[:, f * TT:(f + 1) * TT], sgt[:, 0:TT],
                    cc[:, CV + f:CV + f + 1], bigB[:, f * TT:(f + 1) * TT],
                    OP.mult, OP.add),
                     reads=("sgt", "cc", "bigB"), writes=("mixB",))
                p.op("vector", lambda E, f=f: E.scalar_tensor_tensor(
                    bigCb[:, 8192 + f * TT:8192 + (f + 1) * TT], sgt[:, 0:TT],
                    cc[:, CR + f:CR + f + 1], bigB[:, f * TT:(f + 1) * TT],
                    OP.mult, OP.add),
                     reads=("sgt", "cc", "bigB"), writes=("bigC",))
                p.op("vector", lambda E, f=f: E.tensor_copy(
                    cc[:, PC + f:PC + f + 1], bigB[:, f * TT + TT - 1:f * TT + TT]),
                     reads=("bigB",), writes=("cc",))
            rec[("mixkv", i)] = p.mile("vector", vS)

            for m in range(NF):
                xw = [rec["cast_wk"]] if (i == 0 and m == 0) else []
                gk, bufk = load_col("plain", wk_b[m][:, :], xw, width=2048)
                xw = [rec["cast_wv"]] if (i == 0 and m == 0) else []
                gv, bufv = load_col("plain", wv_b[m][:, :], xw, width=2048)
                col_wait(gk)
                if m == 0:
                    p.wait("tensor", vS, rec[("mixk", i)])
                bkk = ps[1] if m % 2 == 0 else ps[3]
                bvv = ps[2] if m % 2 == 0 else ps[4]
                p.wait2("tensor", bank_free.get(id(bkk)))
                vk = mm_group(bkk, TT,
                              lambda k, b=bufk: b[:, 128 * k:128 * (k + 1)],
                              lambda k: mixA[:, k * TT:(k + 1) * TT], NF)
                col_done(gk, vk)
                col_wait(gv)
                if m == 0:
                    p.wait("tensor", vS, rec[("mixkv", i)])
                p.wait2("tensor", bank_free.get(id(bvv)))
                vv = mm_group(bvv, TT,
                              lambda k, b=bufv: b[:, 128 * k:128 * (k + 1)],
                              lambda k: mixB[:, k * TT:(k + 1) * TT], NF)
                col_done(gv, vv)
                rec[("kvmm_last", i)] = vv
                p.wait("scalar", tS, vk)
                kprev = rec.get(("kvtt", m % 2))
                if kprev is not None:
                    p.wait("scalar", vS, kprev)
                p.op("scalar", lambda E, m=m, bkk=bkk: E.copy(
                    kbuf[:, (m % 2) * 512:(m % 2) * 512 + 512], bkk[:, :]),
                     writes=("kbuf",))
                bank_free[id(bkk)] = (aS, p.mile("scalar", aS))
                gidx = m // 4
                gkey = i * 4 + gidx
                slot = (m % 4) * 512 + (gidx % 2) * 2048
                p.wait("vector", aS, bank_free[id(bkk)][1])
                p.wait("vector", tS, vv)
                dprev = rec.get(("kvdma_g", gkey - 2))
                if dprev is not None and m % 4 == 0:
                    p.wait2("vector", dprev)
                p.op("vector", lambda E, m=m, bvv=bvv, slot=slot: E.tensor_tensor(
                    bigCb[:, slot:slot + 512], bvv[:, :],
                    kbuf[:, (m % 2) * 512:(m % 2) * 512 + 512], OP.mult),
                     reads=("kbuf",), writes=("bigC",))
                kvtt = p.mile("vector", vS)
                rec[("kvtt", m % 2)] = kvtt
                bank_free[id(bvv)] = (vS, kvtt)
                p.op("vector", lambda E, m=m, slot=slot: E.tensor_tensor_scan(
                    st[:, 1:TT + 1],
                    bass.AP(cc, WC + m, [[160, 128], [0, TT]]),
                    bigCb[:, slot:slot + 512],
                    cc[:, SC + m:SC + m + 1], OP.mult, OP.add),
                     reads=("bigC", "cc"), writes=("st",))
                p.op("vector", lambda E, m=m: E.tensor_copy(
                    cc[:, SC + m:SC + m + 1], st[:, TT:TT + 1]),
                     reads=("st",), writes=("cc",))
                if m % 4 == 3:
                    vgrp = p.mile("vector", vS)
                    p.wait("scalar", vS, vgrp)
                    rec[("kvdma_g", gkey)] = KVST.issue(
                        p, "scalar", lambda E, i=i, gi=gidx: E.dma_start(
                            kv_scr.rearrange("(m p) T -> p m T", p=128)[
                                :, gi * 4:(gi + 1) * 4, i * TT:(i + 1) * TT],
                            bigCb[:, (gi % 2) * 2048:(gi % 2) * 2048 + 2048].rearrange(
                                "p (m t) -> p m t", m=4)))

            for m in range(NF):
                xw = [rec["cast_wr"]] if (i == 0 and m == 0) else []
                gr, bufr = load_col("plain", wr_b[m][:, :], xw, width=2048)
                col_wait(gr)
                brr = ps[5] if m % 2 == 0 else ps[6]
                p.wait2("tensor", bank_free.get(id(brr)))
                vr = mm_group(brr, TT,
                              lambda k, b=bufr: b[:, 128 * k:128 * (k + 1)],
                              lambda k: bigCb[:, 8192 + k * TT:8192 + (k + 1) * TT], NF)
                col_done(gr, vr)
                p.wait("scalar", tS, vr)
                gidx = m // 4
                gkey = i * 4 + gidx
                slot = 4096 + (m % 4) * 512 + (gidx % 2) * 2048
                dprev = rec.get(("rdma_g", gkey - 2))
                if dprev is not None and m % 4 == 0:
                    p.wait2("scalar", dprev)
                p.op("scalar", lambda E, slot=slot, brr=brr: E.activation(
                    bigCb[:, slot:slot + 512], brr[:, :], AF.Sigmoid),
                     writes=("bigC_r",))
                bank_free[id(brr)] = (aS, p.mile("scalar", aS))
                if m % 4 == 3:
                    rec[("rdma_g", gkey)] = RST.issue(
                        p, "scalar", lambda E, i=i, gi=gidx: E.dma_start(
                            r_scr.rearrange("(m p) T -> p m T", p=128)[
                                :, gi * 4:(gi + 1) * 4, i * TT:(i + 1) * TT],
                            bigCb[:, 4096 + (gi % 2) * 2048:
                                  4096 + (gi % 2) * 2048 + 2048].rearrange(
                                "p (m t) -> p m t", m=4)))
            p.wait2("tensor", bank_free.get(id(ps[7])))
            for tc in range(NTC):
                for k in range(NF):
                    last_mm = (tc == NTC - 1 and k == NF - 1)
                    fn = lambda E, tc=tc, k=k, last_mm=last_mm: E.matmul(
                        ps[7][:, tc * 16:(tc + 1) * 16],
                        bigB[:, k * TT + tc * 128:k * TT + (tc + 1) * 128],
                        gw_sb[:, k * 16:(k + 1) * 16],
                        start=(k == 0), stop=(k == NF - 1))
                    if last_mm:
                        vgt = p.op("tensor", fn, inc=tS)
                    else:
                        p.raw("tensor", fn)
            p.wait("scalar", tS, vgt)
            glprev = rec.get(("lgused", i % 2))
            if glprev is not None:
                p.wait("scalar", vS, glprev)
            p.op("scalar", lambda E, i=i: E.copy(
                lgb[:, (i % 2) * 64:(i % 2) * 64 + 64], ps[7][:, 0:64]),
                 writes=("lgb",))
            bank_free[id(ps[7])] = (aS, p.mile("scalar", aS))
            rec[("gatelg", i)] = bank_free[id(ps[7])][1]
            if i > 0:
                emit_gating(i - 1)
            emit_casts(ecl[(len(ecl) * i) // ntile:
                           (len(ecl) * (i + 1)) // ntile])

            rec[("rmm", i)] = vr

        emit_gating(ntile - 1)
        rec["p1_scan_done"] = p.mile("vector", vS)

        p.raw("gpsimd", lambda E: E.reg_mov(bndreg, t_loc - 1))

        p.wait("gpsimd", vS, rec["p1_scan_done"])
        rec["ccin"] = GPD.issue(p, "gpsimd",
                                lambda E: E.dma_start(cc_in[:, :], cc[:, SC:SC + 16]))
        p.wait2("gpsimd", rec["ccin"])
        rec["coll"] = p.op("gpsimd", lambda E: E.collective_compute(
            "AllGather", mybir.AluOpType.bypass, replica_groups=replica_groups,
            ins=[cc_in.ap().opt()], outs=[cc_out.ap().opt()]), inc=cS)
        p.wait("gpsimd", cS, rec["coll"])
        rec["sgin"] = GPD.issue(p, "gpsimd",
                                lambda E: E.dma_start(sg_in[:, :], cc_out[0:128, :]))
        p.wait2("vector", rec["sgin"])
        p.op("vector", lambda E: E.tensor_scalar(
            cc[:, SC:SC + 16], sg_in[:, :], cc[:, IO:IO + 1], None, OP.mult),
             reads=("cc",), writes=("cc",))
        rec["sinit"] = p.mile("vector", vS)

        p.wait("tensor", aS, aS.cnt)
        p.wait("tensor", vS, rec["sinit"])

        for i in range(ntile):
            wl = rec.get(("faccst", i - 1))
            if wl is not None:
                p.wait2("sync", wl)
            else:
                p.wait("sync", vS, rec[("ln", ("p1", ntile - 1, NTC - 1))])
                p.wait("sync", aS, rec[("lnssq", ("p1", ntile - 1, NTC - 1))])
            rec[("xload2", i)] = XL.issue(
                p, "sync", lambda E, i=i: E.dma_start(
                    xtm[:, :].rearrange("p (c f) -> p c f", c=NTC),
                    x_d[i * TT:(i + 1) * TT, :].rearrange("(c p) f -> p c f", p=128)))
            if i == 0:
                for gk in (ntile * 4 - 1, ntile * 4 - 2):
                    p.wait2("sync", rec[("kvdma_g", gk)])
                    p.wait2("sync", rec[("rdma_g", gk)])
            else:
                p.wait("sync", vS, rec[("wkvdone", i - 1)])
            rec[("kvload", i)] = KVL.issue(
                p, "sync", lambda E, i=i: E.dma_start(
                    bigCb[:, 0:8192].rearrange("p (m t) -> p m t", m=16),
                    kv_scr.rearrange("(m p) T -> p m T", p=128)[
                        :, 0:16, i * TT:(i + 1) * TT]))
            rec[("rload", i)] = RL.issue(
                p, "sync", lambda E, i=i: E.dma_start(
                    bigCb[:, 8192:16384].rearrange("p (m t) -> p m t", m=16),
                    r_scr.rearrange("(m p) T -> p m T", p=128)[
                        :, 0:16, i * TT:(i + 1) * TT]))

            p.wait2("vector", rec[("xload2", i)])
            p.wait2("scalar", rec[("xload2", i)])
            for tc in range(NTC):
                wts = rec.get(("Tts", ("p2", i, tc - 1))) if tc > 0 else \
                    (rec.get(("Tts", ("p2", i - 1, NTC - 1))) if i > 0 else
                     rec.get(("Tts", ("p1", ntile - 1, NTC - 1))))
                xprev_st = rec.get(("xnst", i, tc - 1)) if tc > 0 else \
                    rec.get(("xnst", i - 1, NTC - 1))
                if xprev_st is not None:
                    p.wait2("vector", xprev_st)
                    p.wait2("scalar", xprev_st)
                emit_ln(tc, xntmb, "xntma", ("p2", i, tc), wait_ts=wts)
                p.wait("scalar", vS, rec[("ln", ("p2", i, tc))])
                rec[("xnst", i, tc)] = XNST.issue(
                    p, "scalar", lambda E, i=i, tc=tc: E.dma_start(
                        xn_scr[i * TT + tc * 128:i * TT + (tc + 1) * 128, :],
                        xntmb))
                p.wait("tensor", vS, rec[("ln", ("p2", i, tc))])
                emit_transpose_tc(tc, xntmb, mixA, "mixA", identb, ("p2", i, tc))

            if i > 0:
                p.wait("vector", tS, rec[("smm", i - 1)])
            p.wait2("vector", rec[("kvload", i)])
            p.wait2("vector", rec[("rload", i)])
            for f in range(NF):
                p.op("vector", lambda E, f=f: E.tensor_copy(
                    st[:, 0:1], cc[:, SC + f:SC + f + 1]),
                     reads=("cc",), writes=("st",))
                p.op("vector", lambda E, f=f: E.tensor_tensor_scan(
                    st[:, 1:TT + 1],
                    bass.AP(cc, WC + f, [[160, 128], [0, TT]]),
                    bigCb[:, f * 512:(f + 1) * 512],
                    cc[:, SC + f:SC + f + 1], OP.mult, OP.add),
                     reads=("bigC", "cc"), writes=("st",))
                p.op("vector", lambda E, f=f: E.tensor_copy(
                    cc[:, SC + f:SC + f + 1], st[:, TT:TT + 1]),
                     reads=("st",), writes=("cc",))
                p.op("vector", lambda E, f=f: E.scalar_tensor_tensor(
                    usc[:, :], bigCb[:, f * 512:(f + 1) * 512],
                    cc[:, UC + f:UC + f + 1], st[:, 0:TT], OP.mult, OP.add),
                     reads=("bigC", "cc", "st"), writes=("usc",))
                p.op("vector", lambda E, f=f: E.tensor_tensor(
                    mixB[:, f * TT:(f + 1) * TT], usc[:, :],
                    bigCb[:, 8192 + f * 512:8192 + (f + 1) * 512], OP.mult),
                     reads=("usc", "bigC"), writes=("mixB",))
            rec[("wkvdone", i)] = p.mile("vector", vS)
            rec[("attn", i)] = rec[("wkvdone", i)]

            for j in range(NSI):
                xw = []
                if i == 0 and j == 0:
                    xw.append(rec["cast_sup"])
                gcol, buf = load_col("pair", sup_b[j].rearrange("g k x -> k g x")[:, :, :],
                                     xw, width=4096)
                col_wait(gcol)
                if j == 0:
                    p.wait("tensor", vS, rec[("T", ("p2", i, NTC - 1))])
                pg = ps[1] if j % 2 == 0 else ps[3]
                pu = ps[2] if j % 2 == 0 else ps[4]
                p.wait2("tensor", bank_free.get(id(pg)))
                p.wait2("tensor", bank_free.get(id(pu)))
                vg = mm_group(pg, TT,
                              lambda k, b=buf: b[:, 128 * k:128 * (k + 1)],
                              lambda k: mixA[:, k * TT:(k + 1) * TT], NF)
                vu = mm_group(pu, TT,
                              lambda k, b=buf: b[:, 2048 + 128 * k:2048 + 128 * (k + 1)],
                              lambda k: mixA[:, k * TT:(k + 1) * TT], NF)
                col_done(gcol, vu)
                p.wait("scalar", tS, vg)
                ju = rec.get(("jdone", j % 2))
                if ju is not None:
                    p.wait("scalar", vS, ju)
                p.op("scalar", lambda E, j=j, pg=pg: E.activation(
                    sgt[:, (j % 2) * 512:(j % 2) * 512 + 512], pg[:, :], AF.Silu),
                     writes=("sgt",))
                bank_free[id(pg)] = (aS, p.mile("scalar", aS))
                p.wait("vector", tS, vu)
                p.wait("vector", aS, bank_free[id(pg)][1])
                if j == 0:
                    sprev = rec.get(("sstore", i - 1))
                    if sprev is not None:
                        p.wait2("vector", sprev)
                p.op("vector", lambda E, pu=pu: E.tensor_copy(usc[:, :], pu[:, :]),
                     writes=("usc",))
                p.op("vector", lambda E, j=j: E.tensor_tensor(
                    abf[:, j * TT:(j + 1) * TT],
                    sgt[:, (j % 2) * 512:(j % 2) * 512 + 512], usc[:, :], OP.mult),
                     reads=("sgt", "usc"), writes=("abf",))
                jv = p.mile("vector", vS)
                rec[("jdone", j % 2)] = jv
                bank_free[id(pu)] = (vS, jv)
            rec[("adone", i)] = jv

            p.wait("tensor", vS, rec[("adone", i)])
            for m in range(16):
                xw = []
                if i == 0 and m == 0:
                    xw.append(rec["cast_sdn"])
                gcol, buf = load_col("plain", sdn_b[m][:, :], xw, width=2816)
                col_wait(gcol)
                dbank = ps[5] if m % 2 == 0 else ps[6]
                p.wait2("tensor", bank_free.get(id(dbank)))
                vd = mm_group(dbank, TT,
                              lambda k, b=buf: b[:, 128 * k:128 * (k + 1)],
                              lambda k: abf[:, k * TT:(k + 1) * TT], NSI)
                col_done(gcol, vd)
                p.wait("vector", tS, vd)
                p.op("vector", lambda E, m=m, db=dbank: E.tensor_copy(
                    bigB[:, m * TT:(m + 1) * TT], db[:, :]),
                     writes=("acc%d" % m,))
                bank_free[id(dbank)] = (vS, p.mile("vector", vS))
                rec[("sdacc", m)] = bank_free[id(dbank)][1]

            for m in range(NF):
                xw = [rec["cast_wo"]] if (i == 0 and m == 0) else []
                gcol, buf = load_col("plain", wo_b[m][:, :], xw, width=2048)
                col_wait(gcol)
                if m == 0:
                    p.wait("tensor", vS, rec[("attn", i)])
                sbank = ps[7] if m % 2 == 0 else ps[0]
                p.wait2("tensor", bank_free.get(id(sbank)))
                vsm = mm_group(sbank, TT,
                               lambda k, b=buf: b[:, 128 * k:128 * (k + 1)],
                               lambda k: mixB[:, k * TT:(k + 1) * TT], NF)
                col_done(gcol, vsm)
                mtprev = rec.get(("mbfT", m % 2))
                if mtprev is not None:
                    p.wait("vector", tS, mtprev)
                p.wait("vector", tS, vsm)
                p.op("vector", lambda E, m=m, sbank=sbank: E.tensor_copy(
                    mbf[:, (m % 2) * 512:(m % 2) * 512 + 512], sbank[:, :]),
                     writes=("mbf",))
                mv = p.mile("vector", vS)
                bank_free[id(sbank)] = (vS, mv)
                tbank = ps[5] if m % 2 == 0 else ps[6]
                p.wait("tensor", vS, mv)
                p.wait2("tensor", bank_free.get(id(tbank)))
                for tc in range(NTC):
                    last = (tc == NTC - 1)
                    fn = lambda E, m=m, tc=tc, tb=tbank, last=last: E.matmul(
                        tb[:, tc * 128:(tc + 1) * 128],
                        mbf[:, (m % 2) * 512 + tc * 128:(m % 2) * 512 + (tc + 1) * 128],
                        identb[:, :], start=(tc == 0), stop=last)
                    if last:
                        vtm = p.op("tensor", fn, inc=tS)
                    else:
                        p.raw("tensor", fn)
                rec[("mbfT", m % 2)] = vtm
                p.wait("vector", tS, vtm)
                sdst = bass.AP(abf, m * 128, [[2 * NIC * TT, 128], [2048, NTC], [1, 128]])
                psrc = bass.AP(tbank, 0, [[512, 128], [128, NTC], [1, 128]])
                p.op("vector", lambda E, d=sdst, s2=psrc: E.tensor_copy(d, s2),
                     writes=("abf",))
                bank_free[id(tbank)] = (vS, p.mile("vector", vS))
                rec[("scomb", m)] = bank_free[id(tbank)][1]
            rec[("smm", i)] = vsm
            p.wait("scalar", vS, rec[("scomb", NF - 1)])
            rec[("sstore", i)] = SST.issue(
                p, "scalar", lambda E, i=i: E.dma_start(
                    s_scr[i * TT:(i + 1) * TT, :].rearrange("(c p) f -> p c f", p=128),
                    abf[:, 0:8192].rearrange("p (c f) -> p c f", c=NTC)))

            for m in range(NF):
                tbank = ps[5] if m % 2 == 0 else ps[6]
                p.wait("tensor", vS, rec[("sdacc", m)])
                p.wait2("tensor", bank_free.get(id(tbank)))
                for tc in range(NTC):
                    last = (tc == NTC - 1)
                    fn = lambda E, m=m, tc=tc, tb=tbank, last=last: E.matmul(
                        tb[:, tc * 128:(tc + 1) * 128],
                        bigB[:, m * TT + tc * 128:m * TT + (tc + 1) * 128],
                        ident[:, :], start=(tc == 0), stop=last)
                    if last:
                        vft = p.op("tensor", fn, inc=tS)
                    else:
                        p.raw("tensor", fn)
                rec[("faccT", i)] = vft
                p.wait("vector", tS, vft)
                fdst = bass.AP(xtm, m * 128, [[NTC * D, 128], [D, NTC], [1, 128]])
                psrc = bass.AP(tbank, 0, [[512, 128], [128, NTC], [1, 128]])
                p.op("vector", lambda E, d=fdst, s2=psrc: E.tensor_copy(d, s2),
                     writes=("xtm",))
                bank_free[id(tbank)] = (vS, p.mile("vector", vS))
                rec[("fcomb", m)] = bank_free[id(tbank)][1]
            p.wait("scalar", vS, rec[("fcomb", NF - 1)])
            rec[("faccst", i)] = FST.issue(
                p, "scalar", lambda E, i=i: E.dma_start(
                    facc_scr[i * TT:(i + 1) * TT, :].rearrange("(c p) f -> p c f", p=128),
                    xtm[:, :].rearrange("p (c f) -> p c f", c=NTC)))

        gbufB = bigB.bitcast(BF16)
        nt0s = []
        _o = 0
        for _n in nts_list:
            nt0s.append(_o)
            _o += _n
        gg = [0]
        rec[("upmm_last2", -2)] = rec[("smm", ntile - 1)]
        rec[("upmm_last2", -1)] = rec[("smm", ntile - 1)]
        p.wait("gpsimd", tS, rec[("faccT", ntile - 1)])
        p.wait2("vector", rec[("sstore", ntile - 1)])
        p.wait2("scalar", rec[("sstore", ntile - 1)])
        p.wait2("vector", rec[("faccst", ntile - 1)])
        p.wait2("gpsimd", rec[("xnst", ntile - 1, NTC - 1)])
        for e in range(NE):
            if e == 0:
                p.wait2("sync", rec[("scat", ntile - 1)])
                if ntile > 1:
                    p.wait2("sync", rec[("scat", ntile - 2)])
                p.wait("sync", vS, rec[("wkvdone", ntile - 1)])
            iw = rec.get(("idxused", e - 2))
            if iw is not None:
                p.wait("sync", tS, iw[0])
                p.wait2("sync", iw[1])
            rec[("idxl", e)] = CWL.issue(
                p, "sync", lambda E, e=e: E.dma_start(
                    idxb[:, (e % 2) * ngrp:(e % 2) * ngrp + ngrp],
                    idx_scr[e].rearrange("(p k) o -> p (k o)", p=128)))
            rec[("cwcl", e)] = CWL.issue(
                p, "sync", lambda E, e=e: E.dma_start(
                    cwc_sb[e % 2][:, :],
                    cwc_scr[e].rearrange("s o -> o s")))
            p.wait2("gpsimd", rec[("idxl", e)])
            for k in range(ngrp):
                g2 = gg[0]
                gg[0] += 1
                pv = rec.get(("cxst", g2 - 2))
                if pv is not None:
                    p.wait2("gpsimd", pv)
                gth = GTH.issue(
                    p, "gpsimd", lambda E, e=e, k=k, g2=g2: E.indirect_dma_start(
                        gbufB[:, (g2 % 2) * 2048:(g2 % 2) * 2048 + 2048], None,
                        xn_scr[:, :],
                        bass.IndirectOffsetOnAxis(
                            ap=idxb[:, (e % 2) * ngrp + k:(e % 2) * ngrp + k + 1], axis=0),
                        bounds_check=bndreg, oob_is_err=False))
                p.wait2("sync", gth)
                rec[("cxst", g2)] = CXS.issue(
                    p, "sync", lambda E, e=e, k=k, g2=g2: E.dma_start(
                        cx_scr[e, k * 128:(k + 1) * 128, :],
                        gbufB[:, (g2 % 2) * 2048:(g2 % 2) * 2048 + 2048]))
            rec[("cx_all", e)] = rec[("cxst", gg[0] - 1)]
            def emit_trl(nti2):
                nts2 = nts_list[nti2]
                nt02 = nt0s[nti2]
                gnt2 = e * len(nts_list) + nti2
                tw = rec.get(("upmm_last2", gnt2 - 2))
                lastg = (nt02 + nts2) // 128 - 1
                for f in range(NF):
                    if f == 0:
                        p.wait2("sync", rec[("cxst", gg[0] - ngrp + lastg)])
                        if tw is not None:
                            p.wait("sync", tS, tw)
                    if gnt2 % 2 == 0:
                        d = mixA[:, f * TT:f * TT + nts2]
                    else:
                        d = bigCb[:, 8192 + f * 512:8192 + f * 512 + nts2]
                    rec[("trl", e, nti2, f)] = TRLp[gnt2 % 2].issue(
                        p, "sync", lambda E, e=e, nt02=nt02, nts2=nts2, f=f, d=d:
                        E.dma_start(d, cx_scr[e, nt02:nt02 + nts2,
                                              128 * f:128 * (f + 1)],
                                    transpose=True))

            for nti, nts in enumerate(nts_list):
                nt0 = nt0s[nti]
                gnt = e * len(nts_list) + nti
                if nti == 0:
                    emit_trl(0)
                    if len(nts_list) > 1:
                        emit_trl(1)
                elif nti + 1 < len(nts_list):
                    emit_trl(nti + 1)
                p.wait2("tensor", bank_free.get(id(ps[7])))
                p.wait2("tensor", rec[("cwcl", e)])
                vbc = p.op("tensor", lambda E, e=e, nt0=nt0, nts=nts: E.matmul(
                    ps[7][:, 0:nts], onesT[(e % 2) * 32:(e % 2) * 32 + 1, :],
                    cwc_sb[e % 2][:, nt0:nt0 + nts],
                    start=True, stop=True), inc=tS)
                p.wait("scalar", tS, vbc)
                bprev = rec.get(("bcused", nti % 2))
                if bprev is not None:
                    p.wait("scalar", vS, bprev)
                p.op("scalar", lambda E, nti=nti, nts=nts: E.copy(
                    bcs[:, (nti % 2) * 512:(nti % 2) * 512 + nts], ps[7][:, 0:nts]),
                     writes=("bcs",))
                bank_free[id(ps[7])] = (aS, p.mile("scalar", aS))
                rec[("bcevac", e, nti)] = bank_free[id(ps[7])][1]
                for j in range(NIC):
                    xw = []
                    if nti == 0 and j == 0:
                        xw.append(rec["cast_eup%d" % e])
                    gcol, buf = load_col(
                        "pair", eup_b[e, j].rearrange("g k x -> k g x")[:, :, :],
                        xw, width=4096)
                    col_wait(gcol)
                    if j == 0:
                        for f in range(NF - 4, NF):
                            p.wait2("tensor", rec[("trl", e, nti, f)])
                    pg = ps[1] if j % 2 == 0 else ps[3]
                    pu = ps[2] if j % 2 == 0 else ps[4]
                    p.wait2("tensor", bank_free.get(id(pg)))
                    p.wait2("tensor", bank_free.get(id(pu)))
                    if gnt % 2 == 0:
                        rhsf = lambda k, nts=nts: mixA[:, k * TT:k * TT + nts]
                    else:
                        rhsf = lambda k, nts=nts: bigCb[:, 8192 + k * 512:
                                                        8192 + k * 512 + nts]
                    vg = mm_group(pg, nts,
                                  lambda k, b=buf: b[:, 128 * k:128 * (k + 1)],
                                  rhsf, NF)
                    vu = mm_group(pu, nts,
                                  lambda k, b=buf: b[:, 2048 + 128 * k:2048 + 128 * (k + 1)],
                                  rhsf, NF)
                    col_done(gcol, vu)
                    rec[("upmm_last2", gnt)] = vu
                    p.wait("scalar", tS, vg)
                    ju = rec.get(("jdone", j % 2))
                    if ju is not None:
                        p.wait("scalar", vS, ju)
                    p.op("scalar", lambda E, j=j, pg=pg, nts=nts: E.activation(
                        sgt[:, (j % 2) * 512:(j % 2) * 512 + nts], pg[:, 0:nts], AF.Silu),
                         writes=("sgt",))
                    bank_free[id(pg)] = (aS, p.mile("scalar", aS))
                    p.wait("vector", tS, vu)
                    p.wait("vector", aS, bank_free[id(pg)][1])
                    p.wait("vector", aS, rec[("bcevac", e, nti)])
                    p.op("vector", lambda E, pu=pu, nti=nti, nts=nts: E.tensor_tensor(
                        usc[:, 0:nts], pu[:, 0:nts],
                        bcs[:, (nti % 2) * 512:(nti % 2) * 512 + nts], OP.mult),
                         reads=("bcs",), writes=("usc",))
                    p.op("vector", lambda E, j=j, nti=nti, nts=nts: E.tensor_tensor(
                        abf[:, (nti % 2) * NIC * TT + j * TT:
                            (nti % 2) * NIC * TT + j * TT + nts],
                        sgt[:, (j % 2) * 512:(j % 2) * 512 + nts],
                        usc[:, 0:nts], OP.mult),
                         reads=("sgt", "usc"), writes=("abf",))
                    jv = p.mile("vector", vS)
                    rec[("jdone", j % 2)] = jv
                    bank_free[id(pu)] = (vS, jv)
                rec[("bcused", nti % 2)] = jv
                p.wait("tensor", vS, jv)
                for m in range(16):
                    xw = []
                    if nti == 0 and m == 0:
                        xw.append(rec["cast_edn%d" % e])
                    gcol, buf = load_col("plain", edn_b[e, m][:, :], xw, width=1408)
                    col_wait(gcol)
                    dbank = ps[5] if m % 2 == 0 else ps[6]
                    p.wait2("tensor", bank_free.get(id(dbank)))
                    vd = mm_group(dbank, nts,
                                  lambda k, b=buf: b[:, 128 * k:128 * (k + 1)],
                                  lambda k, nti=nti, nts=nts: abf[
                                      :, (nti % 2) * NIC * TT + k * TT:
                                      (nti % 2) * NIC * TT + k * TT + nts],
                                  NIC)
                    col_done(gcol, vd)
                    rec["dn_last"] = vd
                    p.wait("scalar", tS, vd)
                    dprev = rec.get(("dnTm", m))
                    if dprev is not None:
                        p.wait("scalar", tS, dprev)
                    p.op("scalar", lambda E, m=m, db=dbank, nts=nts: E.copy(
                        mixB[:, m * TT:m * TT + nts], db[:, 0:nts]),
                         writes=("mixB",))
                    bank_free[id(dbank)] = (aS, p.mile("scalar", aS))
                    rec[("dnstg", m)] = bank_free[id(dbank)][1]
                for sg in range(nts // 128):
                    saprev = rec.get(("sca", sg))
                    for mg in range(4):
                        tbank = ps[0] if mg % 2 == 0 else ps[7]
                        p.wait2("tensor", bank_free.get(id(tbank)))
                        if mg == 0 and sg == 0:
                            for m2 in range(16):
                                p.wait("tensor", aS, rec[("dnstg", m2)])
                        for mm in range(4):
                            m = mg * 4 + mm
                            last = (mm == 3)
                            fn = lambda E, m=m, mm=mm, sg=sg, tb=tbank, last=last: \
                                E.matmul(tb[:, mm * 128:(mm + 1) * 128],
                                         mixB[:, m * TT + sg * 128:m * TT + (sg + 1) * 128],
                                         identb[:, :], start=(mm == 0), stop=last)
                            if last:
                                vt2 = p.op("tensor", fn, inc=tS)
                            else:
                                p.raw("tensor", fn)
                        if sg == nts // 128 - 1:
                            for mm2 in range(4):
                                rec[("dnTm", mg * 4 + mm2)] = vt2
                        p.wait("vector", tS, vt2)
                        if saprev is not None and mg == 0:
                            p.wait2("vector", saprev)
                        p.op("vector", lambda E, sg=sg, mg=mg, tb=tbank: E.tensor_copy(
                            xtm[:, sg * 2048 + mg * 512:sg * 2048 + (mg + 1) * 512],
                            tb[:, :]),
                             writes=("xtm",))
                        bank_free[id(tbank)] = (vS, p.mile("vector", vS))
                    rec[("xstg", sg)] = bank_free[id(tbank)][1]
                    p.wait("gpsimd", vS, rec[("xstg", sg)])
                    if e == 0 and nti == 0 and sg == 0:
                        p.wait2("gpsimd", rec[("faccst", ntile - 1)])
                    kk2 = nt0 // 128 + sg
                    rec[("sca", sg)] = SCA.issue(
                        p, "gpsimd", lambda E, e=e, sg=sg, kk2=kk2: E.indirect_dma_start(
                            facc_scr[:, :],
                            bass.IndirectOffsetOnAxis(
                                ap=idxb[:, (e % 2) * ngrp + kk2:(e % 2) * ngrp + kk2 + 1],
                                axis=0),
                            xtm[:, sg * 2048:(sg + 1) * 2048], None,
                            bounds_check=bndreg, oob_is_err=False,
                            compute_op=OP.add), serialize=True)
            rec[("idxused", e)] = (rec[("upmm_last2", e * len(nts_list) + len(nts_list) - 1)],
                                   rec[("sca", nts_list[-1] // 128 - 1)])
        rec["sca_final"] = (SCA.sems[0], SCA.sems[0].cnt)

        for i in range(ntile):
            if i == 0:
                p.wait2("sync", rec["sca_final"])
                p.wait2("sync", rec[("cx_all", NE - 1)])
                p.wait("sync", tS, rec["dn_last"])
                p.wait2("sync", rec[("sstore", ntile - 1)])
            else:
                p.wait2("sync", rec[("ystore", i - 1)])
                p.wait("sync", vS, rec[("cdone", i - 1)])
            rec[("fload", i)] = KVL.issue(
                p, "sync", lambda E, i=i: E.dma_start(
                    bigB[:, :].rearrange("p (c f) -> p c f", c=NTC),
                    facc_scr[i * TT:(i + 1) * TT, :].rearrange("(c p) f -> p c f", p=128)))
            rec[("sload", i)] = RL.issue(
                p, "sync", lambda E, i=i: E.dma_start(
                    abf[:, 0:8192].rearrange("p (c f) -> p c f", c=NTC),
                    s_scr[i * TT:(i + 1) * TT, :].rearrange("(c p) f -> p c f", p=128)))
            rec[("xload3", i)] = XL.issue(
                p, "sync", lambda E, i=i: E.dma_start(
                    xtm[:, :].rearrange("p (c f) -> p c f", c=NTC),
                    x_d[i * TT:(i + 1) * TT, :].rearrange("(c p) f -> p c f", p=128)))
            p.wait2("scalar", rec[("fload", i)])
            p.wait2("vector", rec[("fload", i)])
            p.wait2("vector", rec[("sload", i)])
            p.wait2("vector", rec[("xload3", i)])
            for tc in range(NTC):
                if i == 0 and tc == 0:
                    p.wait("scalar", tS, rec[("idxused", NE - 1)][0])
                p.op("scalar", lambda E, tc=tc: E.activation(
                    mixA[:, tc * D:(tc + 1) * D],
                    bigB[:, tc * D:(tc + 1) * D], AF.Silu),
                     writes=("mixA",))
                gsr = p.mile("scalar", aS)
                p.wait("vector", aS, gsr)
                p.op("vector", lambda E, tc=tc: E.tensor_tensor(
                    bigB[:, tc * D:(tc + 1) * D], mixA[:, tc * D:(tc + 1) * D],
                    abf[:, tc * D:(tc + 1) * D], OP.mult),
                     reads=("mixA", "abf"), writes=("acc%d" % tc,))
                p.op("vector", lambda E, tc=tc: E.tensor_tensor(
                    xtm[:, tc * D:(tc + 1) * D], bigB[:, tc * D:(tc + 1) * D],
                    xtm[:, tc * D:(tc + 1) * D], OP.add),
                     reads=("acc%d" % tc, "xtm"), writes=("xtm",))
            rec[("cdone", i)] = p.mile("vector", vS)
            p.wait("scalar", vS, rec[("cdone", i)])
            rec[("ystore", i)] = YST.issue(
                p, "scalar", lambda E, i=i: E.dma_start(
                    y_d[i * TT:(i + 1) * TT, :].rearrange("(c p) f -> p c f", p=128),
                    xtm[:, :].rearrange("p (c f) -> p c f", c=NTC)))

        for stream, eng in ((YST, "scalar"), (CWST, "scalar"), (KVST, "scalar"),
                            (RST, "scalar"), (FST, "scalar"), (SST, "scalar"),
                            (XNST, "scalar"), (SETUP, "sync"), (WGT, "sync"),
                            (XL, "sync"), (KVL, "sync"), (RL, "sync"),
                            (CWL, "sync"), (CXS, "sync"),
                            (TRLp[0], "sync"), (TRLp[1], "sync"),
                            (CAST, "gpsimd"), (GPD, "gpsimd"), (SCT, "gpsimd"),
                            (GTH, "gpsimd"), (SCA, "gpsimd")):
            for s in stream.sems:
                if s.cnt > 0:
                    p.wait(eng, s, s.cnt)

        block = ctx.enter_context(nc.Block())
        for eng in Prog.ENGS:
            if p.st[eng]:
                getattr(block, eng)(_mk_runner(p.st[eng]))
    return nc


def _emit_down(p, rec, bank_free, load_col, col_wait, col_done, mm_group,
               ps, abf, bigB, edn_b, sdn_b, tS, vS, aS, i, e):
    shared = (e == NE)
    kk = NSI if shared else NIC
    w = 2816 if shared else 1408
    first_wait_done = False
    for m in range(16):
        src = (sdn_b[m] if shared else edn_b[e, m])
        xw = []
        if i == 0 and m == 0:
            xw.append(rec["cast_sdn" if shared else "cast_edn%d" % e])
        gcol, buf = load_col("plain", src[:, :], xw, width=w)
        col_wait(gcol)
        if not first_wait_done:
            p.wait("tensor", vS, rec[("adone", i, e)])
            first_wait_done = True
        dbank = ps[5] if m % 2 == 0 else ps[6]
        p.wait2("tensor", bank_free.get(id(dbank)))
        if shared:
            rhs = lambda k: abf[:, k * TT:(k + 1) * TT]
        else:
            rhs = lambda k, e=e: abf[:, (e % 2) * NIC * TT + k * TT:
                                     (e % 2) * NIC * TT + (k + 1) * TT]
        vd = mm_group(dbank, TT,
                      lambda k, b=buf: b[:, 128 * k:128 * (k + 1)], rhs, kk)
        col_done(gcol, vd)
        p.wait("vector", tS, vd)
        if e == 0:
            ga = rec.get(("gsil_done", m))
            if ga is not None:
                p.wait("vector", aS, ga)
            p.op("vector", lambda E, m=m, db=dbank: E.tensor_copy(
                bigB[:, m * TT:(m + 1) * TT], db[:, :]),
                 writes=("acc%d" % m,))
        else:
            p.op("vector", lambda E, m=m, db=dbank: E.tensor_tensor(
                bigB[:, m * TT:(m + 1) * TT], db[:, :],
                bigB[:, m * TT:(m + 1) * TT], OP.add),
                 reads=("acc%d" % m,), writes=("acc%d" % m,))
        bank_free[id(dbank)] = (vS, p.mile("vector", vS))
        rec[("dacc", e, m)] = bank_free[id(dbank)][1]



def kernel(**inputs):
    x = np.asarray(inputs["L0_attn_hidden_states"], np.float32)
    B, T, _ = x.shape
    t_loc = T // 2
    cap = 1280
    shared = _stage_shared_inputs(inputs)
    shared["tids"] = np.ascontiguousarray(
        np.arange(t_loc, dtype=np.uint32).reshape(t_loc // 128, 128).T)
    shared["idxinit"] = np.full((NE, cap, 1), 0x3FFFFFFF, np.uint32)
    shared["ecap"] = (np.arange(16, dtype=np.float32) * cap).reshape(16, 1)
    nc = build_program(t_loc, [[0, 1], [2, 3], [4, 5], [6, 7]], cap)
    in_maps = []
    for c in range(8):
        b, h = c // 2, c % 2
        m = dict(shared)
        m["x"] = np.ascontiguousarray(x[b, h * t_loc:(h + 1) * t_loc, :])
        m["xprev"] = (np.ascontiguousarray(x[b, t_loc - 1:t_loc, :]) if h == 1
                      else np.zeros((1, D), np.float32))
        m["isodd"] = np.full((128, 1), float(h), np.float32)
        in_maps.append(m)
    res = run_bass_kernel_spmd(nc, in_maps, core_ids=list(range(8)))
    out = np.empty((B, T, D), np.float32)
    for c in range(8):
        b, h = c // 2, c % 2
        out[b, h * t_loc:(h + 1) * t_loc, :] = np.asarray(res.results[c]["y"])
    return out


# version 13
# speedup vs baseline: 1.2037x; 1.0530x over previous
import numpy as np
from contextlib import ExitStack

import concourse.bass as bass
import concourse.mybir as mybir
from concourse.bass_utils import run_bass_kernel_spmd
from concourse.alu_op_type import AluOpType as OP

F32 = mybir.dt.float32
BF16 = mybir.dt.bfloat16
AF = mybir.ActivationFunctionType
AX = mybir.AxisListType.X

D = 2048
NF = 16
TT = 512
NTC = 4
NE = 16
NIC = 11
NSI = 22
EPS = 1e-5
NEGBIG = -1.0e30



def _cm(W):
    O, I = W.shape
    M, F = O // 128, I // 128
    W4 = W.reshape(M, 128, F, 128)
    return np.ascontiguousarray(W4.transpose(0, 3, 2, 1).reshape(M, 128, F * 128))


def _pairs(cm_grid):
    G = cm_grid.shape[0] // 2
    order = []
    for j in range(G):
        order += [j, G + j]
    return np.ascontiguousarray(cm_grid[order].reshape(G, 2, 128, cm_grid.shape[2]))


def _colT(v):
    return np.ascontiguousarray(v.reshape(16, 128).T.astype(np.float32))


def _stage_shared_inputs(inputs):
    f32 = np.float32
    g = {}
    g["wk_cm"] = _cm(np.asarray(inputs["L0_attn_Wk"], f32))
    g["wv_cm"] = _cm(np.asarray(inputs["L0_attn_Wv"], f32))
    g["wr_cm"] = _cm(np.asarray(inputs["L0_attn_Wr"], f32))
    g["wo_cm"] = _cm(np.asarray(inputs["L0_attn_Wo"], f32))
    eup = np.asarray(inputs["L0_ffn_expert_up_w"], f32)
    edn = np.asarray(inputs["L0_ffn_expert_down_w"], f32)
    g["eup_cm"] = np.ascontiguousarray(
        np.stack([_pairs(_cm(eup[e])) for e in range(NE)]))
    g["edn_cm"] = np.ascontiguousarray(
        np.stack([_cm(edn[e]) for e in range(NE)]))
    g["sup_cm"] = _pairs(_cm(np.asarray(inputs["L0_ffn_shared_up_w"], f32)))
    g["sdn_cm"] = _cm(np.asarray(inputs["L0_ffn_shared_down_w"], f32))
    g["gwt"] = np.ascontiguousarray(np.asarray(inputs["L0_ffn_gate_w"], f32).T)
    g["tdT"] = _colT(np.asarray(inputs["L0_attn_time_decay"], f32).reshape(-1))
    g["tfT"] = _colT(np.asarray(inputs["L0_attn_time_first"], f32).reshape(-1))
    g["tmk"] = _colT(np.asarray(inputs["L0_attn_time_mix_k"], f32).reshape(-1))
    g["tmv"] = _colT(np.asarray(inputs["L0_attn_time_mix_v"], f32).reshape(-1))
    g["tmr"] = _colT(np.asarray(inputs["L0_attn_time_mix_r"], f32).reshape(-1))
    g["ident"] = np.eye(128, dtype=f32)
    return g



class Sem:
    __slots__ = ("h", "cnt", "name")

    def __init__(self, h, name):
        self.h = h
        self.cnt = 0
        self.name = name


class Prog:
    ENGS = ("sync", "scalar", "vector", "tensor", "gpsimd")

    def __init__(self):
        self.st = {e: [] for e in self.ENGS}
        self.wr = {e: set() for e in self.ENGS}
        self.rd = {e: set() for e in self.ENGS}

    def wait(self, eng, sem, val):
        if val is None or val <= 0:
            return
        self.st[eng].append(lambda E, h=sem.h, v=val: E.wait_ge(h, v))

    def wait2(self, eng, sv):
        if sv is None:
            return
        self.wait(eng, sv[0], sv[1])

    def raw(self, eng, fn):
        self.st[eng].append(fn)

    def op(self, eng, fn, reads=(), writes=(), inc=None, inc_n=1):
        w = self.wr[eng]
        r = self.rd[eng]
        if eng in ("vector", "scalar", "gpsimd"):
            if (any(x in w for x in reads) or any(x in w for x in writes)
                    or any(x in r for x in writes)):
                self.st[eng].append(lambda E: E.drain())
                w.clear()
                r.clear()
        if inc is not None:
            inc.cnt += inc_n
            self.st[eng].append(
                lambda E, f=fn, h=inc.h, n=inc_n: f(E).then_inc(h, n))
            tgt = inc.cnt
        else:
            self.st[eng].append(fn)
            tgt = None
        w.update(writes)
        r.update(reads)
        return tgt

    def mile(self, eng, sem, n=1):
        assert eng != "tensor"
        self.wr[eng].clear()
        self.rd[eng].clear()
        sem.cnt += n
        self.st[eng].append(lambda E, h=sem.h, n2=n: E.drain().then_inc(h, n2))
        return sem.cnt


class DmaStream:

    def __init__(self, sems):
        self.sems = sems
        self.i = 0

    def issue(self, p, eng, fn, serialize=False):
        s = self.sems[self.i % len(self.sems)]
        self.i += 1
        if serialize and s.cnt > 0:
            p.wait(eng, s, s.cnt)
        tgt = p.op(eng, fn, inc=s, inc_n=16)
        return (s, tgt)


def _mk_runner(ops):
    def _run(E):
        for f in ops:
            f(E)
    return _run



def build_program(t_loc, replica_groups, cap):
    ntile = t_loc // TT
    ngrp = cap // 128
    nts_list = []
    _r = cap
    while _r > 0:
        nts_list.append(min(512, _r))
        _r -= min(512, _r)
    nc = bass.Bass("TRN2", target_bir_lowering=False)

    x_d = nc.dram_tensor("x", [t_loc, D], F32, kind="ExternalInput")
    xprev_d = nc.dram_tensor("xprev", [1, D], F32, kind="ExternalInput")
    isodd_d = nc.dram_tensor("isodd", [128, 1], F32, kind="ExternalInput")
    tdT_d = nc.dram_tensor("tdT", [128, 16], F32, kind="ExternalInput")
    tfT_d = nc.dram_tensor("tfT", [128, 16], F32, kind="ExternalInput")
    tmk_d = nc.dram_tensor("tmk", [128, 16], F32, kind="ExternalInput")
    tmv_d = nc.dram_tensor("tmv", [128, 16], F32, kind="ExternalInput")
    tmr_d = nc.dram_tensor("tmr", [128, 16], F32, kind="ExternalInput")
    ident_d = nc.dram_tensor("ident", [128, 128], F32, kind="ExternalInput")
    gwt_d = nc.dram_tensor("gwt", [D, 16], F32, kind="ExternalInput")
    tids_d = nc.dram_tensor("tids", [128, t_loc // 128], mybir.dt.uint32, kind="ExternalInput")
    idxinit_d = nc.dram_tensor("idxinit", [NE, cap, 1], mybir.dt.uint32, kind="ExternalInput")
    ecap_d = nc.dram_tensor("ecap", [16, 1], F32, kind="ExternalInput")
    wk_d = nc.dram_tensor("wk_cm", [16, 128, 2048], F32, kind="ExternalInput")
    wv_d = nc.dram_tensor("wv_cm", [16, 128, 2048], F32, kind="ExternalInput")
    wr_d = nc.dram_tensor("wr_cm", [16, 128, 2048], F32, kind="ExternalInput")
    wo_d = nc.dram_tensor("wo_cm", [16, 128, 2048], F32, kind="ExternalInput")
    eup_d = nc.dram_tensor("eup_cm", [NE, NIC, 2, 128, 2048], F32, kind="ExternalInput")
    edn_d = nc.dram_tensor("edn_cm", [NE, 16, 128, 1408], F32, kind="ExternalInput")
    sup_d = nc.dram_tensor("sup_cm", [NSI, 2, 128, 2048], F32, kind="ExternalInput")
    sdn_d = nc.dram_tensor("sdn_cm", [16, 128, 2816], F32, kind="ExternalInput")
    y_d = nc.dram_tensor("y", [t_loc, D], F32, kind="ExternalOutput")

    wk_b = nc.dram_tensor("wk_b", [16, 128, 2048], BF16)
    wv_b = nc.dram_tensor("wv_b", [16, 128, 2048], BF16)
    wr_b = nc.dram_tensor("wr_b", [16, 128, 2048], BF16)
    wo_b = nc.dram_tensor("wo_b", [16, 128, 2048], BF16)
    eup_b = nc.dram_tensor("eup_b", [NE, NIC, 2, 128, 2048], BF16)
    edn_b = nc.dram_tensor("edn_b", [NE, 16, 128, 1408], BF16)
    sup_b = nc.dram_tensor("sup_b", [NSI, 2, 128, 2048], BF16)
    sdn_b = nc.dram_tensor("sdn_b", [16, 128, 2816], BF16)
    kv_scr = nc.dram_tensor("kv_scr", [D, t_loc], BF16)
    r_scr = nc.dram_tensor("r_scr", [D, t_loc], BF16)
    xn_scr = nc.dram_tensor("xn_scr", [t_loc, D], BF16)
    s_scr = nc.dram_tensor("s_scr", [t_loc, D], BF16)
    facc_scr = nc.dram_tensor("facc_scr", [t_loc, D], F32)
    idx_scr = nc.dram_tensor("idx_scr", [NE, cap, 1], mybir.dt.uint32)
    cwc_scr = nc.dram_tensor("cwc_scr", [NE, cap, 1], F32)
    cx_scr = nc.dram_tensor("cx_scr", [NE, cap, D], BF16)
    cc_in = nc.dram_tensor("cc_in", [128, 16], F32)
    cc_out = nc.dram_tensor("cc_out", [128 * 2, 16], F32)

    with ExitStack() as ctx:
        sb = lambda nm, sh, dt: ctx.enter_context(nc.sbuf_tensor(nm, sh, dt))
        mkpsum = lambda nm: ctx.enter_context(nc.psum_tensor(nm, [128, 512], F32))

        xtm = sb("xtm", [128, NTC * D], F32)
        bigB = sb("bigB", [128, NF * TT], F32)
        bigC = sb("bigC", [128, NF * TT], F32)
        bigCb = bigC.bitcast(BF16)
        mixA = sb("mixA", [128, NF * TT], BF16)
        mixB = sb("mixB", [128, NF * TT], BF16)
        abf = sb("abf", [128, 2 * NIC * TT], BF16)
        wb = [sb("wb0", [128, 4096], BF16), sb("wb1", [128, 4096], BF16)]
        xntma = sb("xntma", [128, D], F32)
        xntmb = xntma.bitcast(BF16)[:, 0:D]
        st = sb("st", [128, TT + 1], F32)
        sgt = sb("sgt", [128, 2 * 512], F32)
        usc = sb("usc", [128, 512], F32)
        bcs = sb("bcs", [128, 2 * 512], F32)
        gsil = sb("gsil", [128, 2 * 512], F32)
        kbuf = gsil
        mbf = sgt.bitcast(BF16)[:, 0:1024]
        rows = sb("rows", [128, 16], F32)
        gat = sb("gat", [128, 128], F32)
        cwt = sb("cwt", [128, 2 * NTC * 16], F32)
        lgb = sb("lgb", [128, 2 * NTC * 16], F32)
        cw_fm = sb("cw_fm", [16, TT], F32)
        gfm = sb("gfm", [16, 3 * TT], F32)
        ebase = sb("ebase", [16, 2], F32)
        onecol = sb("onecol", [128, 1], F32)
        onesT = sb("onesT", [33, 128], F32)
        offsu = sb("offsu", [128, 256], mybir.dt.uint32)
        idxb = sb("idxb", [128, 2 * ngrp], mybir.dt.uint32)
        cc = sb("cc", [128, 160], F32)
        ident = sb("identt", [128, 128], F32)
        identb = sb("identtb", [128, 128], BF16)
        gw_sb = sb("gw_sb", [128, NF * 16], F32)
        tids_sb = sb("tids_sb", [128, t_loc // 128], mybir.dt.uint32)
        sg_in = sb("sg_in", [128, 16], F32)

        xprev_sb = bigB[0:1, 0:D]
        xnprev = bigB[0:1, D:2 * D]
        cwc_sb = [bigC[0:1, 0:cap], bigC[32:33, 0:cap]]
        ps = [mkpsum("ps%d" % i) for i in range(8)]

        mksem = lambda nm: Sem(ctx.enter_context(nc.semaphore(nm)), nm)
        cS = mksem("cS")
        tS = mksem("tS")
        vS = mksem("vS")
        aS = mksem("aS")
        SETUP = DmaStream([mksem("dSet")])
        WGT = DmaStream([mksem("dW%d" % k) for k in range(4)])
        XL = DmaStream([mksem("dXL")])
        KVL = DmaStream([mksem("dKVL")])
        RL = DmaStream([mksem("dRL")])
        CWL = DmaStream([mksem("dCWL%d" % k) for k in range(2)])
        CWST = DmaStream([mksem("dCWS")])
        KVST = DmaStream([mksem("dKS%d" % k) for k in range(2)])
        RST = DmaStream([mksem("dRS%d" % k) for k in range(2)])
        YST = DmaStream([mksem("dYS")])
        CAST = DmaStream([mksem("dCast")])
        GPD = DmaStream([mksem("dGP")])
        SCT = DmaStream([mksem("dSCT%d" % k) for k in range(2)])
        GTH = DmaStream([mksem("dGT%d" % k) for k in range(2)])
        CXS = DmaStream([mksem("dCXS%d" % k) for k in range(2)])
        SCA = DmaStream([mksem("dSCA")])
        TRLp = [DmaStream([mksem("dTRL%d_%d" % (pp, k)) for k in range(4)])
                for pp in range(2)]
        FST = DmaStream([mksem("dFST")])
        SST = DmaStream([mksem("dSST")])
        XNST = DmaStream([mksem("dXNS")])

        WC, UC, CK, CV, CR, SC, PC = 0, 16, 32, 48, 64, 80, 96
        IO = 112
        ET = 128
        R_SUM, R_SSQ, R_MEAN, R_MSQ, R_VAR, R_SQV, R_RSTD = 0, 1, 2, 3, 4, 5, 6
        G_LG, G_ES, G_SCO, G_MASK, G_CUR = 0, 16, 32, 48, 64
        G_GMAX, G_M4, G_GSEL, G_MK1 = 80, 84, 88, 92
        G_SMAX, G_NM, G_SSUM, G_RS, G_M1, G_M2, G_G2A, G_SW, G_RSW = (
            96, 97, 98, 99, 100, 101, 102, 104, 108)

        p = Prog()
        rec = {}
        bank_free = {}

        bndreg = nc.gpsimd.alloc_register("bndreg")

        def emit_casts(cast_list):
            for nm, csrc, cdst in cast_list:
                if len(csrc.shape) == 3:
                    s2 = csrc.rearrange("a b c -> (a b) c")
                    d2 = cdst.rearrange("a b c -> (a b) c")
                else:
                    s2 = csrc.rearrange("a b c d -> (a b c) d")
                    d2 = cdst.rearrange("a b c d -> (a b c) d")
                rec["cast_" + nm] = CAST.issue(
                    p, "gpsimd", lambda E, a=d2, b=s2: E.dma_start(a[:, :], b[:, :]),
                    serialize=True)

        rec["idxinit"] = CAST.issue(
            p, "gpsimd", lambda E: E.dma_start(
                idx_scr.rearrange("e s o -> (e s) o")[:, :],
                idxinit_d.rearrange("e s o -> (e s) o")[:, :]))
        p.raw("gpsimd", lambda E: E.reg_mov(bndreg, NE * cap - 1))
        emit_casts([("wk", wk_d, wk_b), ("wv", wv_d, wv_b), ("wr", wr_d, wr_b),
                    ("sup", sup_d, sup_b), ("sdn", sdn_d, sdn_b), ("wo", wo_d, wo_b)])

        def sdma(fn):
            return SETUP.issue(p, "sync", fn)

        sdma(lambda E: E.dma_start(cc[:, WC:WC + 16], tdT_d[:, :]))
        sdma(lambda E: E.dma_start(cc[:, UC:UC + 16], tfT_d[:, :]))
        sdma(lambda E: E.dma_start(cc[:, CK:CK + 16], tmk_d[:, :]))
        sdma(lambda E: E.dma_start(cc[:, CV:CV + 16], tmv_d[:, :]))
        sdma(lambda E: E.dma_start(cc[:, CR:CR + 16], tmr_d[:, :]))
        sdma(lambda E: E.dma_start(cc[:, IO:IO + 1], isodd_d[:, :]))
        sdma(lambda E: E.dma_start(ident[:, :], ident_d[:, :]))
        sdma(lambda E: E.dma_start(tids_sb[:, :], tids_d[:, :]))
        sdma(lambda E: E.dma_start(ebase[0:16, 1:2], ecap_d[:, :]))
        sdma(lambda E: E.dma_start(
            gw_sb[:, :].rearrange("p (f e) -> p f e", f=NF),
            gwt_d.rearrange("(f p) e -> p f e", p=128)[:, :, :]))
        sdma(lambda E: E.dma_start(xprev_sb, xprev_d[:, :]))
        rec["setup_dma"] = (SETUP.sems[0], SETUP.sems[0].cnt)

        p.wait2("scalar", rec["setup_dma"])
        p.op("scalar", lambda E: E.activation(cc[:, ET:ET + 16], cc[:, WC:WC + 16], AF.Exp),
             reads=("cc",), writes=("cc",))
        rec["wc1"] = p.mile("scalar", aS)
        p.wait("vector", aS, rec["wc1"])
        p.op("vector", lambda E: E.tensor_scalar(
            cc[:, ET:ET + 16], cc[:, ET:ET + 16], -1.0, None, OP.mult),
             reads=("cc",), writes=("cc",))
        rec["wc2"] = p.mile("vector", vS)
        p.wait("scalar", vS, rec["wc2"])
        p.op("scalar", lambda E: E.activation(cc[:, WC:WC + 16], cc[:, ET:ET + 16], AF.Exp),
             reads=("cc",), writes=("cc",))
        rec["wcols"] = p.mile("scalar", aS)

        p.wait2("vector", rec["setup_dma"])
        for off in (CK, CV, CR):
            p.op("vector", lambda E, o=off: E.tensor_scalar(
                cc[:, o:o + 16], cc[:, o:o + 16], -1.0, 1.0, OP.mult, OP.add),
                 reads=("cc",), writes=("cc",))
        p.op("vector", lambda E: E.tensor_copy(identb[:, :], ident[:, :]),
             writes=("identb",))
        p.op("vector", lambda E: E.memset(cc[:, SC:SC + 16], 0.0),
             reads=("cc",), writes=("cc",))
        p.op("vector", lambda E: E.memset(onecol[:, :], 1.0), writes=("onecol",))
        p.op("vector", lambda E: E.memset(onesT[:, :], 1.0), writes=("onesT",))
        p.op("vector", lambda E: E.memset(ebase[:, 0:1], 0.0), writes=("ebase",))
        p.op("vector", lambda E: E.reduce_sum(rows[0:1, 8:9], xprev_sb, axis=AX),
             writes=("prow",))
        rec["prow_sum"] = p.mile("vector", vS)
        p.wait("scalar", vS, rec["prow_sum"])
        p.op("scalar", lambda E: E.activation(xnprev, xprev_sb, AF.Square,
                                              accum_out=rows[0:1, 9:10]),
             writes=("xnprev", "prow"))
        rec["prow_ssq"] = p.mile("scalar", aS)
        p.wait("vector", aS, rec["prow_ssq"])
        p.op("vector", lambda E: E.tensor_scalar(
            rows[0:1, 10:11], rows[0:1, 8:9], 1.0 / D, None, OP.mult),
             reads=("prow",), writes=("prow",))
        p.op("vector", lambda E: E.tensor_scalar(
            rows[0:1, 11:12], rows[0:1, 10:11], rows[0:1, 10:11], None, OP.mult),
             reads=("prow",), writes=("prow",))
        p.op("vector", lambda E: E.scalar_tensor_tensor(
            rows[0:1, 12:13], rows[0:1, 9:10], 1.0 / D, rows[0:1, 11:12], OP.mult, OP.subtract),
             reads=("prow",), writes=("prow",))
        p.op("vector", lambda E: E.tensor_scalar(
            rows[0:1, 12:13], rows[0:1, 12:13], EPS, None, OP.add),
             reads=("prow",), writes=("prow",))
        rec["prow_var"] = p.mile("vector", vS)
        p.wait("scalar", vS, rec["prow_var"])
        p.op("scalar", lambda E: E.activation(
            rows[0:1, 13:14], rows[0:1, 12:13], AF.Sqrt),
             reads=("prow",), writes=("prow",))
        rec["prow_sqv"] = p.mile("scalar", aS)
        p.wait("vector", aS, rec["prow_sqv"])
        p.op("vector", lambda E: E.reciprocal(rows[0:1, 14:15], rows[0:1, 13:14]),
             reads=("prow",), writes=("prow",))
        p.op("vector", lambda E: E.tensor_scalar(
            xnprev, xprev_sb, rows[0:1, 10:11], rows[0:1, 14:15],
            OP.subtract, OP.mult),
             reads=("prow",), writes=("xnprev",))
        rec["xnprev"] = p.mile("vector", vS)

        p.wait2("tensor", rec["setup_dma"])
        p.wait("tensor", vS, rec["xnprev"])
        for k in range(NF):
            last = (k == NF - 1)
            fn = lambda E, k=k, last=last: E.matmul(
                ps[0][:, k:k + 1], bigB[0:1, D + 128 * k:D + 128 * (k + 1)],
                ident[0:1, 0:1], start=(k == 0), stop=last)
            if last:
                rec["pcolT"] = p.op("tensor", fn, inc=tS)
            else:
                p.raw("tensor", fn)
        p.wait("vector", tS, rec["pcolT"])
        p.op("vector", lambda E: E.tensor_copy(cc[:, PC:PC + 16], ps[0][:, 0:16]),
             reads=("cc",), writes=("cc",))
        bank_free[id(ps[0])] = (vS, p.mile("vector", vS))

        wload_i = [0]
        wload_rec = {}
        slot_cons = {}
        slot_cur = [0]

        def load_col(kind, src_ap, extra_waits=(), width=2048):
            nhalf = 1 if width <= 2048 else 2
            if nhalf == 2 and slot_cur[0] % 2 == 1:
                slot_cur[0] += 1
            s0 = slot_cur[0] % 4
            slots = [(slot_cur[0] + k) % 4 for k in range(nhalf)]
            slot_cur[0] += nhalf
            g = wload_i[0]
            wload_i[0] += 1
            wbt = wb[s0 // 2]
            off = (s0 % 2) * 2048
            for s in slots:
                cv = slot_cons.get(s)
                if cv is not None:
                    p.wait("sync", tS, cv)
            for sv in extra_waits:
                p.wait2("sync", sv)
            if kind == "pair":
                dst = bass.AP(wbt, 0, [[4096, 128], [2048, 2], [1, 2048]])
            else:
                dst = wbt[:, off:off + width]
            wload_rec[g] = WGT.issue(
                p, "sync", lambda E, a=dst, b=src_ap: E.dma_start(a, b))
            region = wbt[:, off:off + (4096 if nhalf == 2 else width)]
            return (g, slots), region

        def col_wait(gs):
            p.wait2("tensor", wload_rec[gs[0]])

        def col_done(gs, tsval):
            for s in gs[1]:
                slot_cons[s] = tsval

        def mm_group(bank, width, lhsT_fn, rhs_fn, kk):
            out = bank[:, 0:width]
            for k in range(kk):
                last = (k == kk - 1)
                fn = lambda E, k=k, last=last, o=out: E.matmul(
                    o, lhsT_fn(k), rhs_fn(k), start=(k == 0), stop=last)
                if last:
                    return p.op("tensor", fn, inc=tS)
                p.raw("tensor", fn)

        def emit_ln(tc, out_tile, out_name, tag, wait_ts=None):
            if wait_ts is not None:
                p.wait("vector", tS, wait_ts)
                p.wait("scalar", tS, wait_ts)
            p.op("vector", lambda E, tc=tc: E.reduce_sum(
                rows[:, R_SUM:R_SUM + 1], xtm[:, tc * D:(tc + 1) * D], axis=AX),
                 reads=("xtm",), writes=("rows",))
            rec[("lnsum", tag)] = p.mile("vector", vS)
            p.op("scalar", lambda E, tc=tc: E.activation(
                xntma[:, :], xtm[:, tc * D:(tc + 1) * D], AF.Square,
                accum_out=rows[:, R_SSQ:R_SSQ + 1]),
                 reads=("xtm",), writes=("xntma", "rows"))
            rec[("lnssq", tag)] = p.mile("scalar", aS)
            p.wait("vector", aS, rec[("lnssq", tag)])
            p.op("vector", lambda E: E.tensor_scalar(
                rows[:, R_MEAN:R_MEAN + 1], rows[:, R_SUM:R_SUM + 1],
                1.0 / D, None, OP.mult),
                 reads=("rows",), writes=("rows",))
            p.op("vector", lambda E: E.tensor_scalar(
                rows[:, R_MSQ:R_MSQ + 1], rows[:, R_MEAN:R_MEAN + 1],
                rows[:, R_MEAN:R_MEAN + 1], None, OP.mult),
                 reads=("rows",), writes=("rows",))
            p.op("vector", lambda E: E.scalar_tensor_tensor(
                rows[:, R_VAR:R_VAR + 1], rows[:, R_SSQ:R_SSQ + 1], 1.0 / D,
                rows[:, R_MSQ:R_MSQ + 1], OP.mult, OP.subtract),
                 reads=("rows",), writes=("rows",))
            p.op("vector", lambda E: E.tensor_scalar(
                rows[:, R_VAR:R_VAR + 1], rows[:, R_VAR:R_VAR + 1], EPS, None, OP.add),
                 reads=("rows",), writes=("rows",))
            rec[("lnvar", tag)] = p.mile("vector", vS)
            p.wait("scalar", vS, rec[("lnvar", tag)])
            p.op("scalar", lambda E: E.activation(
                rows[:, R_SQV:R_SQV + 1], rows[:, R_VAR:R_VAR + 1], AF.Sqrt),
                 reads=("rows",), writes=("rows",))
            rec[("lnsqv", tag)] = p.mile("scalar", aS)
            p.wait("vector", aS, rec[("lnsqv", tag)])
            p.op("vector", lambda E: E.reciprocal(
                rows[:, R_RSTD:R_RSTD + 1], rows[:, R_SQV:R_SQV + 1]),
                 reads=("rows",), writes=("rows",))
            p.op("vector", lambda E, tc=tc, o=out_tile: E.tensor_scalar(
                o, xtm[:, tc * D:(tc + 1) * D], rows[:, R_MEAN:R_MEAN + 1],
                rows[:, R_RSTD:R_RSTD + 1], OP.subtract, OP.mult),
                 reads=("rows", "xtm"), writes=(out_name,))
            rec[("ln", tag)] = p.mile("vector", vS)

        def emit_transpose_tc(tc, src_tile, dst_tile, dst_name, idt, tag):
            for grp in range(4):
                bank = ps[0] if grp % 2 == 0 else ps[7]
                p.wait2("tensor", bank_free.get(id(bank)))
                for j in range(4):
                    f = grp * 4 + j
                    last = (j == 3)
                    fn = lambda E, f=f, j=j, b=bank, last=last: E.matmul(
                        b[:, 128 * j:128 * (j + 1)],
                        src_tile[:, 128 * f:128 * (f + 1)], idt[:, :],
                        start=(j == 0), stop=last)
                    if last:
                        v = p.op("tensor", fn, inc=tS)
                    else:
                        p.raw("tensor", fn)
                p.wait("vector", tS, v)
                f0 = grp * 4
                dst_ap = bass.AP(dst_tile, f0 * TT + tc * 128,
                                 [[NF * TT, 128], [TT, 4], [1, 128]])
                src_ap = bass.AP(bank, 0, [[512, 128], [128, 4], [1, 128]])
                p.op("vector", lambda E, d=dst_ap, s=src_ap: E.tensor_copy(d, s),
                     writes=(dst_name,))
                bank_free[id(bank)] = (vS, p.mile("vector", vS))
            rec[("T", tag)] = bank_free[id(ps[7])][1]
            rec[("Tts", tag)] = v

        def emit_gating(j):
            p.wait("vector", aS, rec[("gatelg", j)])
            scprev0 = rec.get(("scat", j - 2))
            if scprev0 is not None:
                p.wait2("vector", scprev0)
            for tc in range(NTC):
                p.op("vector", lambda E, tc=tc: E.reduce_max(
                    gat[:, G_SMAX:G_SMAX + 1], lgb[:, (j % 2) * 64 + tc * 16:(j % 2) * 64 + (tc + 1) * 16], axis=AX),
                     reads=("lgb",), writes=("gat_r",))
                p.op("vector", lambda E: E.tensor_scalar(
                    gat[:, G_NM:G_NM + 1], gat[:, G_SMAX:G_SMAX + 1], -1.0, None, OP.mult),
                     reads=("gat_r",), writes=("gat_r",))
                vnm = p.mile("vector", vS)
                p.wait("scalar", vS, vnm)
                p.op("scalar", lambda E, tc=tc: E.activation(
                    gat[:, G_ES:G_ES + 16], lgb[:, (j % 2) * 64 + tc * 16:(j % 2) * 64 + (tc + 1) * 16], AF.Exp,
                    bias=gat[:, G_NM:G_NM + 1], accum_out=gat[:, G_SSUM:G_SSUM + 1]),
                     reads=("lgb",), writes=("gat_es",))
                ves = p.mile("scalar", aS)
                p.wait("vector", aS, ves)
                p.op("vector", lambda E: E.reciprocal(
                    gat[:, G_RS:G_RS + 1], gat[:, G_SSUM:G_SSUM + 1]),
                     reads=("gat_es",), writes=("gat_r",))
                p.op("vector", lambda E: E.tensor_scalar(
                    gat[:, G_SCO:G_SCO + 16], gat[:, G_ES:G_ES + 16],
                    gat[:, G_RS:G_RS + 1], None, OP.mult),
                     reads=("gat_es", "gat_r"), writes=("gat_sco",))
                for g4 in range(4):
                    p.op("vector", lambda E, g4=g4: E.reduce_max(
                        gat[:, G_GMAX + g4:G_GMAX + g4 + 1],
                        gat[:, G_SCO + 4 * g4:G_SCO + 4 * (g4 + 1)], axis=AX),
                         reads=("gat_sco",), writes=("gat_gm",))
                p.op("vector", lambda E: E.reduce_max(
                    gat[:, G_M1:G_M1 + 1], gat[:, G_GMAX:G_GMAX + 4], axis=AX),
                     reads=("gat_gm",), writes=("gat_r",))
                p.op("vector", lambda E: E.tensor_scalar(
                    gat[:, G_MK1:G_MK1 + 4], gat[:, G_GMAX:G_GMAX + 4],
                    gat[:, G_M1:G_M1 + 1], None, OP.is_ge),
                     reads=("gat_gm", "gat_r"), writes=("gat_mk1",))
                p.op("vector", lambda E: E.scalar_tensor_tensor(
                    gat[:, G_G2A:G_G2A + 4], gat[:, G_MK1:G_MK1 + 4], NEGBIG,
                    gat[:, G_GMAX:G_GMAX + 4], OP.mult, OP.add),
                     reads=("gat_mk1", "gat_gm"), writes=("gat_g2",))
                p.op("vector", lambda E: E.reduce_max(
                    gat[:, G_M2:G_M2 + 1], gat[:, G_G2A:G_G2A + 4], axis=AX),
                     reads=("gat_g2",), writes=("gat_r",))
                p.op("vector", lambda E: E.tensor_scalar(
                    gat[:, G_GSEL:G_GSEL + 4], gat[:, G_GMAX:G_GMAX + 4],
                    gat[:, G_M2:G_M2 + 1], None, OP.is_ge),
                     reads=("gat_gm", "gat_r"), writes=("gat_gsel",))
                p.op("vector", lambda E: E.tensor_copy(
                    bass.AP(gat, G_MASK, [[128, 128], [4, 4], [1, 4]]),
                    bass.AP(gat, G_GSEL, [[128, 128], [1, 4], [0, 4]])),
                     reads=("gat_gsel",), writes=("gat_mask",))
                p.op("vector", lambda E: E.tensor_tensor(
                    gat[:, G_CUR:G_CUR + 16], gat[:, G_SCO:G_SCO + 16],
                    gat[:, G_MASK:G_MASK + 16], OP.mult),
                     reads=("gat_sco", "gat_mask"), writes=("gat_cur",))
                p.op("vector", lambda E: E.tensor_copy(
                    gat[:, G_MASK:G_MASK + 16], gat[:, G_CUR:G_CUR + 16]),
                     reads=("gat_cur",), writes=("gat_mask",))
                for tsel in range(4):
                    p.op("vector", lambda E, t2=tsel: E.reduce_max(
                        gat[:, G_M4 + t2:G_M4 + t2 + 1],
                        gat[:, G_CUR:G_CUR + 16], axis=AX),
                         reads=("gat_cur",), writes=("gat_m4",))
                    if tsel < 3:
                        p.op("vector", lambda E, t2=tsel: E.tensor_scalar(
                            gat[:, G_ES:G_ES + 16], gat[:, G_CUR:G_CUR + 16],
                            gat[:, G_M4 + t2:G_M4 + t2 + 1], None, OP.is_ge),
                             reads=("gat_cur", "gat_m4"), writes=("gat_es",))
                        p.op("vector", lambda E: E.scalar_tensor_tensor(
                            gat[:, G_CUR:G_CUR + 16], gat[:, G_ES:G_ES + 16], NEGBIG,
                            gat[:, G_CUR:G_CUR + 16], OP.mult, OP.add),
                             reads=("gat_es", "gat_cur"), writes=("gat_cur",))
                p.op("vector", lambda E: E.reduce_sum(
                    gat[:, G_SW:G_SW + 1], gat[:, G_M4:G_M4 + 4], axis=AX),
                     reads=("gat_m4",), writes=("gat_r",))
                p.op("vector", lambda E: E.tensor_scalar(
                    gat[:, G_SW + 1:G_SW + 2], gat[:, G_SW:G_SW + 1], 1e-20, None, OP.add),
                     reads=("gat_r",), writes=("gat_r",))
                p.op("vector", lambda E: E.reciprocal(
                    gat[:, G_RSW:G_RSW + 1], gat[:, G_SW + 1:G_SW + 2]),
                     reads=("gat_r",), writes=("gat_r",))
                p.op("vector", lambda E: E.tensor_scalar(
                    gat[:, G_ES:G_ES + 16], gat[:, G_MASK:G_MASK + 16],
                    gat[:, G_M4 + 3:G_M4 + 4], None, OP.is_ge),
                     reads=("gat_mask", "gat_m4"), writes=("gat_es",))
                p.op("vector", lambda E, tc=tc: E.scalar_tensor_tensor(
                    cwt[:, (j % 2) * 64 + tc * 16:(j % 2) * 64 + (tc + 1) * 16], gat[:, G_MASK:G_MASK + 16],
                    gat[:, G_RSW:G_RSW + 1], gat[:, G_ES:G_ES + 16],
                    OP.mult, OP.mult),
                     reads=("gat_mask", "gat_es", "gat_r"), writes=("cwt",))
            rec[("cwt", j)] = p.mile("vector", vS)

            MK, RK, OF = 0, TT, 2 * TT
            p.wait("tensor", vS, rec[("cwt", j)])
            p.wait2("tensor", bank_free.get(id(ps[0])))
            for tc in range(NTC):
                last = (tc == NTC - 1)
                fn = lambda E, tc=tc, last=last: E.matmul(
                    ps[0][0:16, tc * 128:(tc + 1) * 128],
                    cwt[:, (j % 2) * 64 + tc * 16:(j % 2) * 64 + (tc + 1) * 16], ident[:, :],
                    start=(tc == 0), stop=last)
                if last:
                    vcw = p.op("tensor", fn, inc=tS)
                else:
                    p.raw("tensor", fn)
            p.wait("vector", tS, vcw)
            p.op("vector", lambda E: E.tensor_copy(cw_fm[:, :], ps[0][0:16, 0:512]),
                 writes=("cw_fm",))
            bank_free[id(ps[0])] = (vS, p.mile("vector", vS))
            p.op("vector", lambda E: E.tensor_scalar(
                gfm[:, MK:MK + TT], cw_fm[:, :], 0.0, None, OP.is_gt),
                 reads=("cw_fm",), writes=("gfm",))
            p.op("vector", lambda E: E.tensor_tensor_scan(
                gfm[:, RK:RK + TT], bass.AP(onecol, 0, [[1, 16], [0, TT]]),
                gfm[:, MK:MK + TT], ebase[:, 0:1], OP.mult, OP.add),
                 reads=("gfm", "onecol", "ebase"), writes=("gfm",))
            p.op("vector", lambda E: E.tensor_copy(
                ebase[:, 0:1], gfm[:, RK + TT - 1:RK + TT]),
                 reads=("gfm",), writes=("ebase",))
            p.op("vector", lambda E: E.tensor_scalar(
                gfm[:, RK:RK + TT], gfm[:, RK:RK + TT], -1.0, None, OP.add),
                 reads=("gfm",), writes=("gfm",))
            p.op("vector", lambda E: E.tensor_scalar(
                gfm[:, OF:OF + TT], gfm[:, RK:RK + TT], 1.0 / 128.0, -0.496,
                OP.mult, OP.add),
                 reads=("gfm",), writes=("gfm",))
            p.op("vector", lambda E: E.tensor_scalar(
                gfm[:, OF:OF + TT], gfm[:, OF:OF + TT], 12582912.0, None, OP.add),
                 reads=("gfm",), writes=("gfm",))
            p.op("vector", lambda E: E.tensor_scalar(
                gfm[:, OF:OF + TT], gfm[:, OF:OF + TT], -12582912.0, None, OP.add),
                 reads=("gfm",), writes=("gfm",))
            p.op("vector", lambda E, ng=ngrp: E.tensor_scalar(
                cw_fm[:, :], gfm[:, RK:RK + TT], float(ng), None, OP.mult),
                 reads=("gfm",), writes=("cw_fm",))
            p.op("vector", lambda E, ng=ngrp: E.scalar_tensor_tensor(
                gfm[:, OF:OF + TT], gfm[:, OF:OF + TT], -float(128 * ng - 1),
                cw_fm[:, :], OP.mult, OP.add),
                 reads=("gfm", "cw_fm"), writes=("gfm",))
            p.op("vector", lambda E: E.tensor_tensor(
                gfm[:, OF:OF + TT], gfm[:, OF:OF + TT], gfm[:, MK:MK + TT], OP.mult),
                 reads=("gfm",), writes=("gfm",))
            p.op("vector", lambda E: E.tensor_tensor(
                gfm[:, RK:RK + TT], gfm[:, RK:RK + TT], gfm[:, MK:MK + TT], OP.mult),
                 reads=("gfm",), writes=("gfm",))
            p.op("vector", lambda E: E.tensor_scalar(
                gfm[:, OF:OF + TT], gfm[:, OF:OF + TT], ebase[:, 1:2], None, OP.add),
                 reads=("gfm", "ebase"), writes=("gfm",))
            p.op("vector", lambda E: E.tensor_scalar(
                gfm[:, RK:RK + TT], gfm[:, RK:RK + TT], ebase[:, 1:2], None, OP.add),
                 reads=("gfm", "ebase"), writes=("gfm",))
            p.op("vector", lambda E: E.tensor_scalar(
                gfm[:, MK:MK + TT], gfm[:, MK:MK + TT], -1.0, 1.0, OP.mult, OP.add),
                 reads=("gfm",), writes=("gfm",))
            p.op("vector", lambda E: E.scalar_tensor_tensor(
                gfm[:, OF:OF + TT], gfm[:, MK:MK + TT], 1.0e9, gfm[:, OF:OF + TT],
                OP.mult, OP.add),
                 reads=("gfm",), writes=("gfm",))
            p.op("vector", lambda E: E.scalar_tensor_tensor(
                gfm[:, RK:RK + TT], gfm[:, MK:MK + TT], 1.0e9, gfm[:, RK:RK + TT],
                OP.mult, OP.add),
                 reads=("gfm",), writes=("gfm",))
            rec[("offsfm", j)] = p.mile("vector", vS)
            p.wait("tensor", vS, rec[("offsfm", j)])
            p.wait2("tensor", bank_free.get(id(ps[7])))
            nT = 0
            for base, srcoff in ((0, OF), (64, RK)):
                for tc in range(NTC):
                    nT += 1
                    last = (nT == 8)
                    fn = lambda E, b2=base, so=srcoff, tc=tc, last=last, first=(nT == 1): \
                        E.matmul(ps[7][:, b2 + tc * 16:b2 + (tc + 1) * 16],
                                 gfm[0:16, so + tc * 128:so + (tc + 1) * 128],
                                 ident[0:16, 0:16], start=first, stop=last)
                    if last:
                        voff = p.op("tensor", fn, inc=tS)
                    else:
                        p.raw("tensor", fn)
            p.wait("vector", tS, voff)
            scprev = rec.get(("scat", j - 2))
            if scprev is not None:
                p.wait2("vector", scprev)
            p.op("vector", lambda E: E.tensor_copy(offsu[:, (j % 2) * 128:(j % 2) * 128 + 128], ps[7][:, 0:128]),
                 writes=("offsu",))
            bank_free[id(ps[7])] = (vS, p.mile("vector", vS))
            rec[("offsu", j)] = bank_free[id(ps[7])][1]
            p.wait("gpsimd", vS, rec[("offsu", j)])
            if j == 0:
                p.wait2("gpsimd", rec["idxinit"])
            ssem = SCT.sems[j % 2]
            for tc in range(NTC):
                g = j * NTC + tc
                for e in range(NE):
                    p.op("gpsimd", lambda E, e=e, tc=tc, g=g: E.indirect_dma_start(
                        idx_scr.rearrange("e s o -> (e s) o"),
                        bass.IndirectOffsetOnAxis(
                            ap=offsu[:, (j % 2) * 128 + tc * 16 + e:(j % 2) * 128 + tc * 16 + e + 1], axis=0),
                        tids_sb[:, g:g + 1], None,
                        bounds_check=bndreg, oob_is_err=False), inc=ssem, inc_n=16)
                    p.op("gpsimd", lambda E, e=e, tc=tc: E.indirect_dma_start(
                        cwc_scr.rearrange("e s o -> (e s) o"),
                        bass.IndirectOffsetOnAxis(
                            ap=offsu[:, (j % 2) * 128 + 64 + tc * 16 + e:(j % 2) * 128 + 64 + tc * 16 + e + 1], axis=0),
                        cwt[:, (j % 2) * 64 + tc * 16 + e:(j % 2) * 64 + tc * 16 + e + 1],
                        None,
                        bounds_check=bndreg, oob_is_err=False), inc=ssem, inc_n=16)
            rec[("scat", j)] = (ssem, ssem.cnt)

            rec[("lgused", j % 2)] = rec[("offsu", j)]

        ecl = []
        for e in range(NE):
            ecl.append(("eup%d" % e, eup_d[e], eup_b[e]))
            ecl.append(("edn%d" % e, edn_d[e], edn_b[e]))

        p.wait("vector", aS, rec["wcols"])
        for i in range(ntile):
            if i > 0:
                p.wait("sync", vS, rec[("ln", ("p1", i - 1, NTC - 1))])
                p.wait("sync", aS, rec[("lnssq", ("p1", i - 1, NTC - 1))])
            rec[("xload", i)] = XL.issue(
                p, "sync", lambda E, i=i: E.dma_start(
                    xtm[:, :].rearrange("p (c f) -> p c f", c=NTC),
                    x_d[i * TT:(i + 1) * TT, :].rearrange("(c p) f -> p c f", p=128)))
            p.wait2("vector", rec[("xload", i)])
            p.wait2("scalar", rec[("xload", i)])

            for tc in range(NTC):
                wts = rec.get(("Tts", ("p1", i, tc - 1))) if tc > 0 else \
                    (rec.get(("Tts", ("p1", i - 1, NTC - 1))) if i > 0 else None)
                emit_ln(tc, xntma[:, :], "xntma", ("p1", i, tc), wait_ts=wts)
                p.wait("tensor", vS, rec[("ln", ("p1", i, tc))])
                emit_transpose_tc(tc, xntma, bigB, "bigB", ident, ("p1", i, tc))

            p.wait("vector", vS, rec[("T", ("p1", i, NTC - 1))])
            if i > 0:
                p.wait("vector", tS, rec[("rmm", i - 1)])
            p.op("vector", lambda E: E.tensor_copy(
                cc[:, ET:ET + 16], cc[:, PC:PC + 16]),
                 reads=("cc",), writes=("cc",))
            def _dprime(f):
                p.op("vector", lambda E, f=f: E.tensor_tensor(
                    sgt[:, 1:TT], bigB[:, f * TT:f * TT + TT - 1],
                    bigB[:, f * TT + 1:f * TT + TT], OP.subtract),
                     reads=("bigB",), writes=("sgt",))
                p.op("vector", lambda E, f=f: E.tensor_tensor(
                    sgt[:, 0:1], cc[:, ET + f:ET + f + 1],
                    bigB[:, f * TT:f * TT + 1], OP.subtract),
                     reads=("bigB", "cc"), writes=("sgt",))
            for f in range(NF):
                _dprime(f)
                p.op("vector", lambda E, f=f: E.scalar_tensor_tensor(
                    mixA[:, f * TT:(f + 1) * TT], sgt[:, 0:TT],
                    cc[:, CK + f:CK + f + 1], bigB[:, f * TT:(f + 1) * TT],
                    OP.mult, OP.add),
                     reads=("sgt", "cc", "bigB"), writes=("mixA",))
            rec[("mixk", i)] = p.mile("vector", vS)
            for f in range(NF):
                _dprime(f)
                p.op("vector", lambda E, f=f: E.scalar_tensor_tensor(
                    mixB[:, f * TT:(f + 1) * TT], sgt[:, 0:TT],
                    cc[:, CV + f:CV + f + 1], bigB[:, f * TT:(f + 1) * TT],
                    OP.mult, OP.add),
                     reads=("sgt", "cc", "bigB"), writes=("mixB",))
                p.op("vector", lambda E, f=f: E.scalar_tensor_tensor(
                    bigCb[:, 8192 + f * TT:8192 + (f + 1) * TT], sgt[:, 0:TT],
                    cc[:, CR + f:CR + f + 1], bigB[:, f * TT:(f + 1) * TT],
                    OP.mult, OP.add),
                     reads=("sgt", "cc", "bigB"), writes=("bigC",))
                p.op("vector", lambda E, f=f: E.tensor_copy(
                    cc[:, PC + f:PC + f + 1], bigB[:, f * TT + TT - 1:f * TT + TT]),
                     reads=("bigB",), writes=("cc",))
            rec[("mixkv", i)] = p.mile("vector", vS)

            for m in range(NF):
                xw = [rec["cast_wk"]] if (i == 0 and m == 0) else []
                gk, bufk = load_col("plain", wk_b[m][:, :], xw, width=2048)
                xw = [rec["cast_wv"]] if (i == 0 and m == 0) else []
                gv, bufv = load_col("plain", wv_b[m][:, :], xw, width=2048)
                col_wait(gk)
                if m == 0:
                    p.wait("tensor", vS, rec[("mixk", i)])
                bkk = ps[1] if m % 2 == 0 else ps[3]
                bvv = ps[2] if m % 2 == 0 else ps[4]
                p.wait2("tensor", bank_free.get(id(bkk)))
                vk = mm_group(bkk, TT,
                              lambda k, b=bufk: b[:, 128 * k:128 * (k + 1)],
                              lambda k: mixA[:, k * TT:(k + 1) * TT], NF)
                col_done(gk, vk)
                col_wait(gv)
                if m == 0:
                    p.wait("tensor", vS, rec[("mixkv", i)])
                p.wait2("tensor", bank_free.get(id(bvv)))
                vv = mm_group(bvv, TT,
                              lambda k, b=bufv: b[:, 128 * k:128 * (k + 1)],
                              lambda k: mixB[:, k * TT:(k + 1) * TT], NF)
                col_done(gv, vv)
                rec[("kvmm_last", i)] = vv
                p.wait("scalar", tS, vk)
                kprev = rec.get(("kvtt", m % 2))
                if kprev is not None:
                    p.wait("scalar", vS, kprev)
                p.op("scalar", lambda E, m=m, bkk=bkk: E.copy(
                    kbuf[:, (m % 2) * 512:(m % 2) * 512 + 512], bkk[:, :]),
                     writes=("kbuf",))
                bank_free[id(bkk)] = (aS, p.mile("scalar", aS))
                gidx = m // 4
                gkey = i * 4 + gidx
                slot = (m % 4) * 512 + (gidx % 2) * 2048
                p.wait("vector", aS, bank_free[id(bkk)][1])
                p.wait("vector", tS, vv)
                dprev = rec.get(("kvdma_g", gkey - 2))
                if dprev is not None and m % 4 == 0:
                    p.wait2("vector", dprev)
                p.op("vector", lambda E, m=m, bvv=bvv, slot=slot: E.tensor_tensor(
                    bigCb[:, slot:slot + 512], bvv[:, :],
                    kbuf[:, (m % 2) * 512:(m % 2) * 512 + 512], OP.mult),
                     reads=("kbuf",), writes=("bigC",))
                kvtt = p.mile("vector", vS)
                rec[("kvtt", m % 2)] = kvtt
                bank_free[id(bvv)] = (vS, kvtt)
                p.op("vector", lambda E, m=m, slot=slot: E.tensor_tensor_scan(
                    st[:, 1:TT + 1],
                    bass.AP(cc, WC + m, [[160, 128], [0, TT]]),
                    bigCb[:, slot:slot + 512],
                    cc[:, SC + m:SC + m + 1], OP.mult, OP.add),
                     reads=("bigC", "cc"), writes=("st",))
                p.op("vector", lambda E, m=m: E.tensor_copy(
                    cc[:, SC + m:SC + m + 1], st[:, TT:TT + 1]),
                     reads=("st",), writes=("cc",))
                if m % 4 == 3:
                    vgrp = p.mile("vector", vS)
                    p.wait("scalar", vS, vgrp)
                    rec[("kvdma_g", gkey)] = KVST.issue(
                        p, "scalar", lambda E, i=i, gi=gidx: E.dma_start(
                            kv_scr.rearrange("(m p) T -> p m T", p=128)[
                                :, gi * 4:(gi + 1) * 4, i * TT:(i + 1) * TT],
                            bigCb[:, (gi % 2) * 2048:(gi % 2) * 2048 + 2048].rearrange(
                                "p (m t) -> p m t", m=4)))

            for m in range(NF):
                xw = [rec["cast_wr"]] if (i == 0 and m == 0) else []
                gr, bufr = load_col("plain", wr_b[m][:, :], xw, width=2048)
                col_wait(gr)
                brr = ps[5] if m % 2 == 0 else ps[6]
                p.wait2("tensor", bank_free.get(id(brr)))
                vr = mm_group(brr, TT,
                              lambda k, b=bufr: b[:, 128 * k:128 * (k + 1)],
                              lambda k: bigCb[:, 8192 + k * TT:8192 + (k + 1) * TT], NF)
                col_done(gr, vr)
                p.wait("scalar", tS, vr)
                gidx = m // 4
                gkey = i * 4 + gidx
                slot = 4096 + (m % 4) * 512 + (gidx % 2) * 2048
                dprev = rec.get(("rdma_g", gkey - 2))
                if dprev is not None and m % 4 == 0:
                    p.wait2("scalar", dprev)
                p.op("scalar", lambda E, slot=slot, brr=brr: E.activation(
                    bigCb[:, slot:slot + 512], brr[:, :], AF.Sigmoid),
                     writes=("bigC_r",))
                bank_free[id(brr)] = (aS, p.mile("scalar", aS))
                if m % 4 == 3:
                    rec[("rdma_g", gkey)] = RST.issue(
                        p, "scalar", lambda E, i=i, gi=gidx: E.dma_start(
                            r_scr.rearrange("(m p) T -> p m T", p=128)[
                                :, gi * 4:(gi + 1) * 4, i * TT:(i + 1) * TT],
                            bigCb[:, 4096 + (gi % 2) * 2048:
                                  4096 + (gi % 2) * 2048 + 2048].rearrange(
                                "p (m t) -> p m t", m=4)))
            p.wait2("tensor", bank_free.get(id(ps[7])))
            for tc in range(NTC):
                for k in range(NF):
                    last_mm = (tc == NTC - 1 and k == NF - 1)
                    fn = lambda E, tc=tc, k=k, last_mm=last_mm: E.matmul(
                        ps[7][:, tc * 16:(tc + 1) * 16],
                        bigB[:, k * TT + tc * 128:k * TT + (tc + 1) * 128],
                        gw_sb[:, k * 16:(k + 1) * 16],
                        start=(k == 0), stop=(k == NF - 1))
                    if last_mm:
                        vgt = p.op("tensor", fn, inc=tS)
                    else:
                        p.raw("tensor", fn)
            p.wait("scalar", tS, vgt)
            glprev = rec.get(("lgused", i % 2))
            if glprev is not None:
                p.wait("scalar", vS, glprev)
            p.op("scalar", lambda E, i=i: E.copy(
                lgb[:, (i % 2) * 64:(i % 2) * 64 + 64], ps[7][:, 0:64]),
                 writes=("lgb",))
            bank_free[id(ps[7])] = (aS, p.mile("scalar", aS))
            rec[("gatelg", i)] = bank_free[id(ps[7])][1]
            if i > 0:
                emit_gating(i - 1)
            emit_casts(ecl[(8 * i) // ntile:(8 * (i + 1)) // ntile])

            rec[("rmm", i)] = vr

        emit_gating(ntile - 1)
        rec["p1_scan_done"] = p.mile("vector", vS)

        p.raw("gpsimd", lambda E: E.reg_mov(bndreg, t_loc - 1))

        p.wait("gpsimd", vS, rec["p1_scan_done"])
        rec["ccin"] = GPD.issue(p, "gpsimd",
                                lambda E: E.dma_start(cc_in[:, :], cc[:, SC:SC + 16]))
        p.wait2("gpsimd", rec["ccin"])
        rec["coll"] = p.op("gpsimd", lambda E: E.collective_compute(
            "AllGather", mybir.AluOpType.bypass, replica_groups=replica_groups,
            ins=[cc_in.ap().opt()], outs=[cc_out.ap().opt()]), inc=cS)
        p.wait("gpsimd", cS, rec["coll"])
        rec["sgin"] = GPD.issue(p, "gpsimd",
                                lambda E: E.dma_start(sg_in[:, :], cc_out[0:128, :]))
        p.wait2("vector", rec["sgin"])
        p.op("vector", lambda E: E.tensor_scalar(
            cc[:, SC:SC + 16], sg_in[:, :], cc[:, IO:IO + 1], None, OP.mult),
             reads=("cc",), writes=("cc",))
        rec["sinit"] = p.mile("vector", vS)

        p.wait("tensor", aS, aS.cnt)
        p.wait("tensor", vS, rec["sinit"])

        for i in range(ntile):
            wl = rec.get(("faccst", i - 1))
            if wl is not None:
                p.wait2("sync", wl)
            else:
                p.wait("sync", vS, rec[("ln", ("p1", ntile - 1, NTC - 1))])
                p.wait("sync", aS, rec[("lnssq", ("p1", ntile - 1, NTC - 1))])
            rec[("xload2", i)] = XL.issue(
                p, "sync", lambda E, i=i: E.dma_start(
                    xtm[:, :].rearrange("p (c f) -> p c f", c=NTC),
                    x_d[i * TT:(i + 1) * TT, :].rearrange("(c p) f -> p c f", p=128)))
            if i == 0:
                for gk in (ntile * 4 - 1, ntile * 4 - 2):
                    p.wait2("sync", rec[("kvdma_g", gk)])
                    p.wait2("sync", rec[("rdma_g", gk)])
            else:
                p.wait("sync", vS, rec[("wkvdone", i - 1)])
            rec[("kvload", i)] = KVL.issue(
                p, "sync", lambda E, i=i: E.dma_start(
                    bigCb[:, 0:8192].rearrange("p (m t) -> p m t", m=16),
                    kv_scr.rearrange("(m p) T -> p m T", p=128)[
                        :, 0:16, i * TT:(i + 1) * TT]))
            rec[("rload", i)] = RL.issue(
                p, "sync", lambda E, i=i: E.dma_start(
                    bigCb[:, 8192:16384].rearrange("p (m t) -> p m t", m=16),
                    r_scr.rearrange("(m p) T -> p m T", p=128)[
                        :, 0:16, i * TT:(i + 1) * TT]))

            p.wait2("vector", rec[("xload2", i)])
            p.wait2("scalar", rec[("xload2", i)])
            for tc in range(NTC):
                wts = rec.get(("Tts", ("p2", i, tc - 1))) if tc > 0 else \
                    (rec.get(("Tts", ("p2", i - 1, NTC - 1))) if i > 0 else
                     rec.get(("Tts", ("p1", ntile - 1, NTC - 1))))
                xprev_st = rec.get(("xnst", i, tc - 1)) if tc > 0 else \
                    rec.get(("xnst", i - 1, NTC - 1))
                if xprev_st is not None:
                    p.wait2("vector", xprev_st)
                    p.wait2("scalar", xprev_st)
                emit_ln(tc, xntmb, "xntma", ("p2", i, tc), wait_ts=wts)
                p.wait("scalar", vS, rec[("ln", ("p2", i, tc))])
                rec[("xnst", i, tc)] = XNST.issue(
                    p, "scalar", lambda E, i=i, tc=tc: E.dma_start(
                        xn_scr[i * TT + tc * 128:i * TT + (tc + 1) * 128, :],
                        xntmb))
                p.wait("tensor", vS, rec[("ln", ("p2", i, tc))])
                emit_transpose_tc(tc, xntmb, mixA, "mixA", identb, ("p2", i, tc))

            if i > 0:
                p.wait("vector", tS, rec[("smm", i - 1)])
            p.wait2("vector", rec[("kvload", i)])
            p.wait2("vector", rec[("rload", i)])
            for f in range(NF):
                p.op("vector", lambda E, f=f: E.tensor_copy(
                    st[:, 0:1], cc[:, SC + f:SC + f + 1]),
                     reads=("cc",), writes=("st",))
                p.op("vector", lambda E, f=f: E.tensor_tensor_scan(
                    st[:, 1:TT + 1],
                    bass.AP(cc, WC + f, [[160, 128], [0, TT]]),
                    bigCb[:, f * 512:(f + 1) * 512],
                    cc[:, SC + f:SC + f + 1], OP.mult, OP.add),
                     reads=("bigC", "cc"), writes=("st",))
                p.op("vector", lambda E, f=f: E.tensor_copy(
                    cc[:, SC + f:SC + f + 1], st[:, TT:TT + 1]),
                     reads=("st",), writes=("cc",))
                p.op("vector", lambda E, f=f: E.scalar_tensor_tensor(
                    usc[:, :], bigCb[:, f * 512:(f + 1) * 512],
                    cc[:, UC + f:UC + f + 1], st[:, 0:TT], OP.mult, OP.add),
                     reads=("bigC", "cc", "st"), writes=("usc",))
                p.op("vector", lambda E, f=f: E.tensor_tensor(
                    mixB[:, f * TT:(f + 1) * TT], usc[:, :],
                    bigCb[:, 8192 + f * 512:8192 + (f + 1) * 512], OP.mult),
                     reads=("usc", "bigC"), writes=("mixB",))
            rec[("wkvdone", i)] = p.mile("vector", vS)
            rec[("attn", i)] = rec[("wkvdone", i)]

            for j in range(NSI):
                xw = []
                if i == 0 and j == 0:
                    xw.append(rec["cast_sup"])
                gcol, buf = load_col("pair", sup_b[j].rearrange("g k x -> k g x")[:, :, :],
                                     xw, width=4096)
                col_wait(gcol)
                if j == 0:
                    p.wait("tensor", vS, rec[("T", ("p2", i, NTC - 1))])
                pg = ps[1] if j % 2 == 0 else ps[3]
                pu = ps[2] if j % 2 == 0 else ps[4]
                p.wait2("tensor", bank_free.get(id(pg)))
                p.wait2("tensor", bank_free.get(id(pu)))
                vg = mm_group(pg, TT,
                              lambda k, b=buf: b[:, 128 * k:128 * (k + 1)],
                              lambda k: mixA[:, k * TT:(k + 1) * TT], NF)
                vu = mm_group(pu, TT,
                              lambda k, b=buf: b[:, 2048 + 128 * k:2048 + 128 * (k + 1)],
                              lambda k: mixA[:, k * TT:(k + 1) * TT], NF)
                col_done(gcol, vu)
                p.wait("scalar", tS, vg)
                ju = rec.get(("jdone", j % 2))
                if ju is not None:
                    p.wait("scalar", vS, ju)
                p.op("scalar", lambda E, j=j, pg=pg: E.activation(
                    sgt[:, (j % 2) * 512:(j % 2) * 512 + 512], pg[:, :], AF.Silu),
                     writes=("sgt",))
                bank_free[id(pg)] = (aS, p.mile("scalar", aS))
                p.wait("vector", tS, vu)
                p.wait("vector", aS, bank_free[id(pg)][1])
                if j == 0:
                    sprev = rec.get(("sstore", i - 1))
                    if sprev is not None:
                        p.wait2("vector", sprev)
                p.op("vector", lambda E, pu=pu: E.tensor_copy(usc[:, :], pu[:, :]),
                     writes=("usc",))
                p.op("vector", lambda E, j=j: E.tensor_tensor(
                    abf[:, j * TT:(j + 1) * TT],
                    sgt[:, (j % 2) * 512:(j % 2) * 512 + 512], usc[:, :], OP.mult),
                     reads=("sgt", "usc"), writes=("abf",))
                jv = p.mile("vector", vS)
                rec[("jdone", j % 2)] = jv
                bank_free[id(pu)] = (vS, jv)
            rec[("adone", i)] = jv

            p.wait("tensor", vS, rec[("adone", i)])
            for m in range(16):
                xw = []
                if i == 0 and m == 0:
                    xw.append(rec["cast_sdn"])
                gcol, buf = load_col("plain", sdn_b[m][:, :], xw, width=2816)
                col_wait(gcol)
                dbank = ps[5] if m % 2 == 0 else ps[6]
                p.wait2("tensor", bank_free.get(id(dbank)))
                vd = mm_group(dbank, TT,
                              lambda k, b=buf: b[:, 128 * k:128 * (k + 1)],
                              lambda k: abf[:, k * TT:(k + 1) * TT], NSI)
                col_done(gcol, vd)
                p.wait("vector", tS, vd)
                p.op("vector", lambda E, m=m, db=dbank: E.tensor_copy(
                    bigB[:, m * TT:(m + 1) * TT], db[:, :]),
                     writes=("acc%d" % m,))
                bank_free[id(dbank)] = (vS, p.mile("vector", vS))
                rec[("sdacc", m)] = bank_free[id(dbank)][1]

            for m in range(NF):
                xw = [rec["cast_wo"]] if (i == 0 and m == 0) else []
                gcol, buf = load_col("plain", wo_b[m][:, :], xw, width=2048)
                col_wait(gcol)
                if m == 0:
                    p.wait("tensor", vS, rec[("attn", i)])
                sbank = ps[7] if m % 2 == 0 else ps[0]
                p.wait2("tensor", bank_free.get(id(sbank)))
                vsm = mm_group(sbank, TT,
                               lambda k, b=buf: b[:, 128 * k:128 * (k + 1)],
                               lambda k: mixB[:, k * TT:(k + 1) * TT], NF)
                col_done(gcol, vsm)
                mtprev = rec.get(("mbfT", m % 2))
                if mtprev is not None:
                    p.wait("vector", tS, mtprev)
                p.wait("vector", tS, vsm)
                p.op("vector", lambda E, m=m, sbank=sbank: E.tensor_copy(
                    mbf[:, (m % 2) * 512:(m % 2) * 512 + 512], sbank[:, :]),
                     writes=("mbf",))
                mv = p.mile("vector", vS)
                bank_free[id(sbank)] = (vS, mv)
                tbank = ps[5] if m % 2 == 0 else ps[6]
                p.wait("tensor", vS, mv)
                p.wait2("tensor", bank_free.get(id(tbank)))
                for tc in range(NTC):
                    last = (tc == NTC - 1)
                    fn = lambda E, m=m, tc=tc, tb=tbank, last=last: E.matmul(
                        tb[:, tc * 128:(tc + 1) * 128],
                        mbf[:, (m % 2) * 512 + tc * 128:(m % 2) * 512 + (tc + 1) * 128],
                        identb[:, :], start=(tc == 0), stop=last)
                    if last:
                        vtm = p.op("tensor", fn, inc=tS)
                    else:
                        p.raw("tensor", fn)
                rec[("mbfT", m % 2)] = vtm
                p.wait("vector", tS, vtm)
                sdst = bass.AP(abf, m * 128, [[2 * NIC * TT, 128], [2048, NTC], [1, 128]])
                psrc = bass.AP(tbank, 0, [[512, 128], [128, NTC], [1, 128]])
                p.op("vector", lambda E, d=sdst, s2=psrc: E.tensor_copy(d, s2),
                     writes=("abf",))
                bank_free[id(tbank)] = (vS, p.mile("vector", vS))
                rec[("scomb", m)] = bank_free[id(tbank)][1]
            rec[("smm", i)] = vsm
            p.wait("scalar", vS, rec[("scomb", NF - 1)])
            rec[("sstore", i)] = SST.issue(
                p, "scalar", lambda E, i=i: E.dma_start(
                    s_scr[i * TT:(i + 1) * TT, :].rearrange("(c p) f -> p c f", p=128),
                    abf[:, 0:8192].rearrange("p (c f) -> p c f", c=NTC)))

            for m in range(NF):
                tbank = ps[5] if m % 2 == 0 else ps[6]
                p.wait("tensor", vS, rec[("sdacc", m)])
                p.wait2("tensor", bank_free.get(id(tbank)))
                for tc in range(NTC):
                    last = (tc == NTC - 1)
                    fn = lambda E, m=m, tc=tc, tb=tbank, last=last: E.matmul(
                        tb[:, tc * 128:(tc + 1) * 128],
                        bigB[:, m * TT + tc * 128:m * TT + (tc + 1) * 128],
                        ident[:, :], start=(tc == 0), stop=last)
                    if last:
                        vft = p.op("tensor", fn, inc=tS)
                    else:
                        p.raw("tensor", fn)
                rec[("faccT", i)] = vft
                p.wait("vector", tS, vft)
                fdst = bass.AP(xtm, m * 128, [[NTC * D, 128], [D, NTC], [1, 128]])
                psrc = bass.AP(tbank, 0, [[512, 128], [128, NTC], [1, 128]])
                p.op("vector", lambda E, d=fdst, s2=psrc: E.tensor_copy(d, s2),
                     writes=("xtm",))
                bank_free[id(tbank)] = (vS, p.mile("vector", vS))
                rec[("fcomb", m)] = bank_free[id(tbank)][1]
            p.wait("scalar", vS, rec[("fcomb", NF - 1)])
            rec[("faccst", i)] = FST.issue(
                p, "scalar", lambda E, i=i: E.dma_start(
                    facc_scr[i * TT:(i + 1) * TT, :].rearrange("(c p) f -> p c f", p=128),
                    xtm[:, :].rearrange("p (c f) -> p c f", c=NTC)))

        gbufB = bigB.bitcast(BF16)
        nt0s = []
        _o = 0
        for _n in nts_list:
            nt0s.append(_o)
            _o += _n
        gg = [0]
        rec[("upmm_last2", -2)] = rec[("smm", ntile - 1)]
        rec[("upmm_last2", -1)] = rec[("smm", ntile - 1)]
        p.wait("gpsimd", tS, rec[("faccT", ntile - 1)])
        p.wait2("vector", rec[("sstore", ntile - 1)])
        p.wait2("scalar", rec[("sstore", ntile - 1)])
        p.wait2("vector", rec[("faccst", ntile - 1)])
        p.wait2("gpsimd", rec[("xnst", ntile - 1, NTC - 1)])
        for e in range(NE):
            if e == 0:
                p.wait2("sync", rec[("scat", ntile - 1)])
                if ntile > 1:
                    p.wait2("sync", rec[("scat", ntile - 2)])
                p.wait("sync", vS, rec[("wkvdone", ntile - 1)])
            iw = rec.get(("idxused", e - 2))
            if iw is not None:
                p.wait("sync", tS, iw[0])
                p.wait2("sync", iw[1])
            rec[("idxl", e)] = CWL.issue(
                p, "sync", lambda E, e=e: E.dma_start(
                    idxb[:, (e % 2) * ngrp:(e % 2) * ngrp + ngrp],
                    idx_scr[e].rearrange("(p k) o -> p (k o)", p=128)))
            rec[("cwcl", e)] = CWL.issue(
                p, "sync", lambda E, e=e: E.dma_start(
                    cwc_sb[e % 2][:, :],
                    cwc_scr[e].rearrange("s o -> o s")))
            p.wait2("gpsimd", rec[("idxl", e)])
            for k in range(ngrp):
                g2 = gg[0]
                gg[0] += 1
                pv = rec.get(("cxst", g2 - 2))
                if pv is not None:
                    p.wait2("gpsimd", pv)
                gth = GTH.issue(
                    p, "gpsimd", lambda E, e=e, k=k, g2=g2: E.indirect_dma_start(
                        gbufB[:, (g2 % 2) * 2048:(g2 % 2) * 2048 + 2048], None,
                        xn_scr[:, :],
                        bass.IndirectOffsetOnAxis(
                            ap=idxb[:, (e % 2) * ngrp + k:(e % 2) * ngrp + k + 1], axis=0),
                        bounds_check=bndreg, oob_is_err=False))
                p.wait2("sync", gth)
                rec[("cxst", g2)] = CXS.issue(
                    p, "sync", lambda E, e=e, k=k, g2=g2: E.dma_start(
                        cx_scr[e, k * 128:(k + 1) * 128, :],
                        gbufB[:, (g2 % 2) * 2048:(g2 % 2) * 2048 + 2048]))
            rec[("cx_all", e)] = rec[("cxst", gg[0] - 1)]
            if e + 3 < NE and e >= 1:
                emit_casts(ecl[2 * (e + 3):2 * (e + 3) + 2])
            def emit_trl(nti2):
                nts2 = nts_list[nti2]
                nt02 = nt0s[nti2]
                gnt2 = e * len(nts_list) + nti2
                tw = rec.get(("upmm_last2", gnt2 - 2))
                lastg = (nt02 + nts2) // 128 - 1
                for f in range(NF):
                    if f == 0:
                        p.wait2("sync", rec[("cxst", gg[0] - ngrp + lastg)])
                        if tw is not None:
                            p.wait("sync", tS, tw)
                    if gnt2 % 2 == 0:
                        d = mixA[:, f * TT:f * TT + nts2]
                    else:
                        d = bigCb[:, 8192 + f * 512:8192 + f * 512 + nts2]
                    rec[("trl", e, nti2, f)] = TRLp[gnt2 % 2].issue(
                        p, "sync", lambda E, e=e, nt02=nt02, nts2=nts2, f=f, d=d:
                        E.dma_start(d, cx_scr[e, nt02:nt02 + nts2,
                                              128 * f:128 * (f + 1)],
                                    transpose=True))

            for nti, nts in enumerate(nts_list):
                nt0 = nt0s[nti]
                gnt = e * len(nts_list) + nti
                if nti == 0:
                    emit_trl(0)
                    if len(nts_list) > 1:
                        emit_trl(1)
                elif nti + 1 < len(nts_list):
                    emit_trl(nti + 1)
                p.wait2("tensor", bank_free.get(id(ps[7])))
                p.wait2("tensor", rec[("cwcl", e)])
                vbc = p.op("tensor", lambda E, e=e, nt0=nt0, nts=nts: E.matmul(
                    ps[7][:, 0:nts], onesT[(e % 2) * 32:(e % 2) * 32 + 1, :],
                    cwc_sb[e % 2][:, nt0:nt0 + nts],
                    start=True, stop=True), inc=tS)
                p.wait("scalar", tS, vbc)
                bprev = rec.get(("bcused", nti % 2))
                if bprev is not None:
                    p.wait("scalar", vS, bprev)
                p.op("scalar", lambda E, nti=nti, nts=nts: E.copy(
                    bcs[:, (nti % 2) * 512:(nti % 2) * 512 + nts], ps[7][:, 0:nts]),
                     writes=("bcs",))
                bank_free[id(ps[7])] = (aS, p.mile("scalar", aS))
                rec[("bcevac", e, nti)] = bank_free[id(ps[7])][1]
                for j in range(NIC):
                    xw = []
                    if nti == 0 and j == 0:
                        xw.append(rec["cast_eup%d" % e])
                    gcol, buf = load_col(
                        "pair", eup_b[e, j].rearrange("g k x -> k g x")[:, :, :],
                        xw, width=4096)
                    col_wait(gcol)
                    if j == 0:
                        for f in range(NF - 4, NF):
                            p.wait2("tensor", rec[("trl", e, nti, f)])
                    pg = ps[1] if j % 2 == 0 else ps[3]
                    pu = ps[2] if j % 2 == 0 else ps[4]
                    p.wait2("tensor", bank_free.get(id(pg)))
                    p.wait2("tensor", bank_free.get(id(pu)))
                    if gnt % 2 == 0:
                        rhsf = lambda k, nts=nts: mixA[:, k * TT:k * TT + nts]
                    else:
                        rhsf = lambda k, nts=nts: bigCb[:, 8192 + k * 512:
                                                        8192 + k * 512 + nts]
                    vg = mm_group(pg, nts,
                                  lambda k, b=buf: b[:, 128 * k:128 * (k + 1)],
                                  rhsf, NF)
                    vu = mm_group(pu, nts,
                                  lambda k, b=buf: b[:, 2048 + 128 * k:2048 + 128 * (k + 1)],
                                  rhsf, NF)
                    col_done(gcol, vu)
                    rec[("upmm_last2", gnt)] = vu
                    p.wait("scalar", tS, vg)
                    ju = rec.get(("jdone", j % 2))
                    if ju is not None:
                        p.wait("scalar", vS, ju)
                    p.op("scalar", lambda E, j=j, pg=pg, nts=nts: E.activation(
                        sgt[:, (j % 2) * 512:(j % 2) * 512 + nts], pg[:, 0:nts], AF.Silu),
                         writes=("sgt",))
                    bank_free[id(pg)] = (aS, p.mile("scalar", aS))
                    p.wait("vector", tS, vu)
                    p.wait("vector", aS, bank_free[id(pg)][1])
                    p.wait("vector", aS, rec[("bcevac", e, nti)])
                    p.op("vector", lambda E, pu=pu, nti=nti, nts=nts: E.tensor_tensor(
                        usc[:, 0:nts], pu[:, 0:nts],
                        bcs[:, (nti % 2) * 512:(nti % 2) * 512 + nts], OP.mult),
                         reads=("bcs",), writes=("usc",))
                    p.op("vector", lambda E, j=j, nti=nti, nts=nts: E.tensor_tensor(
                        abf[:, (nti % 2) * NIC * TT + j * TT:
                            (nti % 2) * NIC * TT + j * TT + nts],
                        sgt[:, (j % 2) * 512:(j % 2) * 512 + nts],
                        usc[:, 0:nts], OP.mult),
                         reads=("sgt", "usc"), writes=("abf",))
                    jv = p.mile("vector", vS)
                    rec[("jdone", j % 2)] = jv
                    bank_free[id(pu)] = (vS, jv)
                rec[("bcused", nti % 2)] = jv
                p.wait("tensor", vS, jv)
                for m in range(16):
                    xw = []
                    if nti == 0 and m == 0:
                        xw.append(rec["cast_edn%d" % e])
                    gcol, buf = load_col("plain", edn_b[e, m][:, :], xw, width=1408)
                    col_wait(gcol)
                    dbank = ps[5] if m % 2 == 0 else ps[6]
                    p.wait2("tensor", bank_free.get(id(dbank)))
                    vd = mm_group(dbank, nts,
                                  lambda k, b=buf: b[:, 128 * k:128 * (k + 1)],
                                  lambda k, nti=nti, nts=nts: abf[
                                      :, (nti % 2) * NIC * TT + k * TT:
                                      (nti % 2) * NIC * TT + k * TT + nts],
                                  NIC)
                    col_done(gcol, vd)
                    rec["dn_last"] = vd
                    p.wait("scalar", tS, vd)
                    dprev = rec.get(("dnTm", m))
                    if dprev is not None:
                        p.wait("scalar", tS, dprev)
                    p.op("scalar", lambda E, m=m, db=dbank, nts=nts: E.copy(
                        mixB[:, m * TT:m * TT + nts], db[:, 0:nts]),
                         writes=("mixB",))
                    bank_free[id(dbank)] = (aS, p.mile("scalar", aS))
                    rec[("dnstg", m)] = bank_free[id(dbank)][1]
                for sg in range(nts // 128):
                    saprev = rec.get(("sca", sg))
                    for mg in range(4):
                        tbank = ps[0] if mg % 2 == 0 else ps[7]
                        p.wait2("tensor", bank_free.get(id(tbank)))
                        if mg == 0 and sg == 0:
                            for m2 in range(16):
                                p.wait("tensor", aS, rec[("dnstg", m2)])
                        for mm in range(4):
                            m = mg * 4 + mm
                            last = (mm == 3)
                            fn = lambda E, m=m, mm=mm, sg=sg, tb=tbank, last=last: \
                                E.matmul(tb[:, mm * 128:(mm + 1) * 128],
                                         mixB[:, m * TT + sg * 128:m * TT + (sg + 1) * 128],
                                         identb[:, :], start=(mm == 0), stop=last)
                            if last:
                                vt2 = p.op("tensor", fn, inc=tS)
                            else:
                                p.raw("tensor", fn)
                        if sg == nts // 128 - 1:
                            for mm2 in range(4):
                                rec[("dnTm", mg * 4 + mm2)] = vt2
                        p.wait("vector", tS, vt2)
                        if saprev is not None and mg == 0:
                            p.wait2("vector", saprev)
                        p.op("vector", lambda E, sg=sg, mg=mg, tb=tbank: E.tensor_copy(
                            xtm[:, sg * 2048 + mg * 512:sg * 2048 + (mg + 1) * 512],
                            tb[:, :]),
                             writes=("xtm",))
                        bank_free[id(tbank)] = (vS, p.mile("vector", vS))
                    rec[("xstg", sg)] = bank_free[id(tbank)][1]
                    p.wait("gpsimd", vS, rec[("xstg", sg)])
                    if e == 0 and nti == 0 and sg == 0:
                        p.wait2("gpsimd", rec[("faccst", ntile - 1)])
                    kk2 = nt0 // 128 + sg
                    rec[("sca", sg)] = SCA.issue(
                        p, "gpsimd", lambda E, e=e, sg=sg, kk2=kk2: E.indirect_dma_start(
                            facc_scr[:, :],
                            bass.IndirectOffsetOnAxis(
                                ap=idxb[:, (e % 2) * ngrp + kk2:(e % 2) * ngrp + kk2 + 1],
                                axis=0),
                            xtm[:, sg * 2048:(sg + 1) * 2048], None,
                            bounds_check=bndreg, oob_is_err=False,
                            compute_op=OP.add), serialize=True)
            rec[("idxused", e)] = (rec[("upmm_last2", e * len(nts_list) + len(nts_list) - 1)],
                                   rec[("sca", nts_list[-1] // 128 - 1)])
        rec["sca_final"] = (SCA.sems[0], SCA.sems[0].cnt)

        for i in range(ntile):
            if i == 0:
                p.wait2("sync", rec["sca_final"])
                p.wait2("sync", rec[("cx_all", NE - 1)])
                p.wait("sync", tS, rec["dn_last"])
                p.wait2("sync", rec[("sstore", ntile - 1)])
            else:
                p.wait2("sync", rec[("ystore", i - 1)])
                p.wait("sync", vS, rec[("cdone", i - 1)])
            rec[("fload", i)] = KVL.issue(
                p, "sync", lambda E, i=i: E.dma_start(
                    bigB[:, :].rearrange("p (c f) -> p c f", c=NTC),
                    facc_scr[i * TT:(i + 1) * TT, :].rearrange("(c p) f -> p c f", p=128)))
            rec[("sload", i)] = RL.issue(
                p, "sync", lambda E, i=i: E.dma_start(
                    abf[:, 0:8192].rearrange("p (c f) -> p c f", c=NTC),
                    s_scr[i * TT:(i + 1) * TT, :].rearrange("(c p) f -> p c f", p=128)))
            rec[("xload3", i)] = XL.issue(
                p, "sync", lambda E, i=i: E.dma_start(
                    xtm[:, :].rearrange("p (c f) -> p c f", c=NTC),
                    x_d[i * TT:(i + 1) * TT, :].rearrange("(c p) f -> p c f", p=128)))
            p.wait2("scalar", rec[("fload", i)])
            p.wait2("vector", rec[("fload", i)])
            p.wait2("vector", rec[("sload", i)])
            p.wait2("vector", rec[("xload3", i)])
            for tc in range(NTC):
                if i == 0 and tc == 0:
                    p.wait("scalar", tS, rec[("idxused", NE - 1)][0])
                p.op("scalar", lambda E, tc=tc: E.activation(
                    mixA[:, tc * D:(tc + 1) * D],
                    bigB[:, tc * D:(tc + 1) * D], AF.Silu),
                     writes=("mixA",))
                gsr = p.mile("scalar", aS)
                p.wait("vector", aS, gsr)
                p.op("vector", lambda E, tc=tc: E.tensor_tensor(
                    bigB[:, tc * D:(tc + 1) * D], mixA[:, tc * D:(tc + 1) * D],
                    abf[:, tc * D:(tc + 1) * D], OP.mult),
                     reads=("mixA", "abf"), writes=("acc%d" % tc,))
                p.op("vector", lambda E, tc=tc: E.tensor_tensor(
                    xtm[:, tc * D:(tc + 1) * D], bigB[:, tc * D:(tc + 1) * D],
                    xtm[:, tc * D:(tc + 1) * D], OP.add),
                     reads=("acc%d" % tc, "xtm"), writes=("xtm",))
            rec[("cdone", i)] = p.mile("vector", vS)
            p.wait("scalar", vS, rec[("cdone", i)])
            rec[("ystore", i)] = YST.issue(
                p, "scalar", lambda E, i=i: E.dma_start(
                    y_d[i * TT:(i + 1) * TT, :].rearrange("(c p) f -> p c f", p=128),
                    xtm[:, :].rearrange("p (c f) -> p c f", c=NTC)))

        for stream, eng in ((YST, "scalar"), (CWST, "scalar"), (KVST, "scalar"),
                            (RST, "scalar"), (FST, "scalar"), (SST, "scalar"),
                            (XNST, "scalar"), (SETUP, "sync"), (WGT, "sync"),
                            (XL, "sync"), (KVL, "sync"), (RL, "sync"),
                            (CWL, "sync"), (CXS, "sync"),
                            (TRLp[0], "sync"), (TRLp[1], "sync"),
                            (CAST, "gpsimd"), (GPD, "gpsimd"), (SCT, "gpsimd"),
                            (GTH, "gpsimd"), (SCA, "gpsimd")):
            for s in stream.sems:
                if s.cnt > 0:
                    p.wait(eng, s, s.cnt)

        block = ctx.enter_context(nc.Block())
        for eng in Prog.ENGS:
            if p.st[eng]:
                getattr(block, eng)(_mk_runner(p.st[eng]))
    return nc


def _emit_down(p, rec, bank_free, load_col, col_wait, col_done, mm_group,
               ps, abf, bigB, edn_b, sdn_b, tS, vS, aS, i, e):
    shared = (e == NE)
    kk = NSI if shared else NIC
    w = 2816 if shared else 1408
    first_wait_done = False
    for m in range(16):
        src = (sdn_b[m] if shared else edn_b[e, m])
        xw = []
        if i == 0 and m == 0:
            xw.append(rec["cast_sdn" if shared else "cast_edn%d" % e])
        gcol, buf = load_col("plain", src[:, :], xw, width=w)
        col_wait(gcol)
        if not first_wait_done:
            p.wait("tensor", vS, rec[("adone", i, e)])
            first_wait_done = True
        dbank = ps[5] if m % 2 == 0 else ps[6]
        p.wait2("tensor", bank_free.get(id(dbank)))
        if shared:
            rhs = lambda k: abf[:, k * TT:(k + 1) * TT]
        else:
            rhs = lambda k, e=e: abf[:, (e % 2) * NIC * TT + k * TT:
                                     (e % 2) * NIC * TT + (k + 1) * TT]
        vd = mm_group(dbank, TT,
                      lambda k, b=buf: b[:, 128 * k:128 * (k + 1)], rhs, kk)
        col_done(gcol, vd)
        p.wait("vector", tS, vd)
        if e == 0:
            ga = rec.get(("gsil_done", m))
            if ga is not None:
                p.wait("vector", aS, ga)
            p.op("vector", lambda E, m=m, db=dbank: E.tensor_copy(
                bigB[:, m * TT:(m + 1) * TT], db[:, :]),
                 writes=("acc%d" % m,))
        else:
            p.op("vector", lambda E, m=m, db=dbank: E.tensor_tensor(
                bigB[:, m * TT:(m + 1) * TT], db[:, :],
                bigB[:, m * TT:(m + 1) * TT], OP.add),
                 reads=("acc%d" % m,), writes=("acc%d" % m,))
        bank_free[id(dbank)] = (vS, p.mile("vector", vS))
        rec[("dacc", e, m)] = bank_free[id(dbank)][1]



def kernel(**inputs):
    x = np.asarray(inputs["L0_attn_hidden_states"], np.float32)
    B, T, _ = x.shape
    t_loc = T // 2
    cap = 1280
    shared = _stage_shared_inputs(inputs)
    shared["tids"] = np.ascontiguousarray(
        np.arange(t_loc, dtype=np.uint32).reshape(t_loc // 128, 128).T)
    shared["idxinit"] = np.full((NE, cap, 1), 0x3FFFFFFF, np.uint32)
    shared["ecap"] = (np.arange(16, dtype=np.float32) * cap).reshape(16, 1)
    nc = build_program(t_loc, [[0, 1], [2, 3], [4, 5], [6, 7]], cap)
    in_maps = []
    for c in range(8):
        b, h = c // 2, c % 2
        m = dict(shared)
        m["x"] = np.ascontiguousarray(x[b, h * t_loc:(h + 1) * t_loc, :])
        m["xprev"] = (np.ascontiguousarray(x[b, t_loc - 1:t_loc, :]) if h == 1
                      else np.zeros((1, D), np.float32))
        m["isodd"] = np.full((128, 1), float(h), np.float32)
        in_maps.append(m)
    res = run_bass_kernel_spmd(nc, in_maps, core_ids=list(range(8)))
    out = np.empty((B, T, D), np.float32)
    for c in range(8):
        b, h = c // 2, c % 2
        out[b, h * t_loc:(h + 1) * t_loc, :] = np.asarray(res.results[c]["y"])
    return out
